# version 05
# baseline (speedup 1.0000x reference)
import os
import sys
import math
import contextlib

sys.path.insert(0, "/opt/trn_rl_repo")

import numpy as np
import ml_dtypes

import concourse.bass as bass
import concourse.mybir as mybir
from concourse.bass_utils import run_bass_kernel_spmd

F32 = mybir.dt.float32
BF16 = mybir.dt.bfloat16
I32 = mybir.dt.int32
AF = mybir.ActivationFunctionType
ALU = mybir.AluOpType
BF = ml_dtypes.bfloat16

B, T, D = 2, 8192, 1024
H = 4096
C = 128
V1 = 2048
NCORE = 8
TPC = 2048
NTILE = 16
EPS5 = np.float32(1e-5)
MAGIC = np.float32(1.5 * 2.0 ** 23)
NEG30 = np.float32(-1.0e30)
INV_TAU = np.float32(1.0 / math.sqrt(C))

XL_PAD = 2560
XLCOLS = T + XL_PAD
STRIP_PITCH = [(2 * rp + 2 + 3) * 512 for rp in range(8)]
STRIP_MAX = max(STRIP_PITCH)
KVL = 2 * 128 * 1024 + 2 * 1024 * V1
KV_VOFF = 2 * 128 * 1024

PHASES = os.environ.get(
    "MK_PHASES", "ffn0a,att0,ffn0b,ffn1a,att1,ffn1b").split(",")

_NC_CACHE = {}

ENGS = ("sync", "tensor", "vector", "scalar", "gpsimd")


class Buf:
    __slots__ = ("w", "rs", "name")

    def __init__(self, name=""):
        self.w = None
        self.rs = []
        self.name = name


class Rec:
    __slots__ = ("fn", "waits", "inc", "dma", "tok")

    def __init__(self, fn, waits, dma, tok):
        self.fn = fn
        self.waits = waits
        self.inc = False
        self.dma = dma
        self.tok = tok


class Sched:
    def __init__(self):
        self.recs = {e: [] for e in ENGS}
        self.ndma = {e: 0 for e in ENGS}
        self.nx = {}
        self.all = {}
        self.fencep = {e: None for e in ENGS}
        self.lastc = {e: None for e in ENGS}
        self.lastd = {e: None for e in ENGS}
        self.barp = {e: [] for e in ENGS}

    def op(self, eng, fn, reads=(), writes=(), dma=False, fence=False,
           always_inc=False, dsem=None):
        if os.environ.get("MK_FENCE_ALL", "0") == "1" and not dma:
            fence = True
        if dma and dsem is not None:
            tok = ("x", dsem, self.nx.get(dsem, 0))
            self.nx[dsem] = self.nx.get(dsem, 0) + 1
        elif dma:
            tok = ("d", eng, self.ndma[eng])
            self.ndma[eng] += 1
        else:
            tok = ("c", eng, len(self.recs[eng]))
        deps = set()
        for b in reads:
            if b.w is not None:
                deps.add(b.w)
        for b in writes:
            if b.w is not None:
                deps.add(b.w)
            for t in b.rs:
                deps.add(t)
        forced = []
        if self.barp[eng]:
            forced.extend(self.barp[eng])
            self.barp[eng] = []
        if self.fencep[eng] is not None:
            forced.append(self.fencep[eng])
            self.fencep[eng] = None
        if fence and self.lastc[eng] is not None:
            forced.append(self.lastc[eng])
        waits = []
        for t in deps:
            kind, e, i = t
            if kind == "c" and e == eng:
                continue
            waits.append(t)
            self.all[t].inc = True
        for t in forced:
            if t in waits:
                continue
            waits.append(t)
            self.all[t].inc = True
        rec = Rec(fn, waits, dma, tok)
        if always_inc:
            rec.inc = True
        self.recs[eng].append(rec)
        self.all[tok] = rec
        if dma:
            self.lastd[eng] = tok
        else:
            self.lastc[eng] = tok
        if fence:
            self.fencep[eng] = tok
        for b in reads:
            b.rs.append(tok)
        for b in writes:
            b.w = tok
            b.rs = []
        return tok

    def barrier(self):
        toks = []
        for e in ENGS:
            if self.lastc[e] is not None:
                toks.append(self.lastc[e])
            if self.lastd[e] is not None:
                toks.append(self.lastd[e])
        for e in ENGS:
            self.barp[e] = list(set(self.barp[e]) | set(toks))

    def emit(self, nc, ctx):
        sems = {}
        for e in ENGS:
            sems[("c", e)] = ctx.enter_context(nc.semaphore(f"sem_c_{e}"))
            sems[("d", e)] = ctx.enter_context(nc.semaphore(f"sem_d_{e}"))
        for nm in self.nx:
            sems[("x", nm)] = ctx.enter_context(nc.semaphore(f"sem_x_{nm}"))
        incidx = {}
        for e in ENGS:
            n = 0
            for r in self.recs[e]:
                if not r.dma:
                    if r.inc:
                        n += 1
                    incidx[r.tok] = n

        def target(t):
            kind, e, i = t
            if kind == "d":
                return sems[("d", e)], (i + 1) * 16
            if kind == "x":
                return sems[("x", e)], (i + 1) * 16
            return sems[("c", e)], incidx[t]

        with nc.Block() as block:
            for e in ENGS:
                def make(e):
                    def body(eng):
                        hw = {}
                        for r in self.recs[e]:
                            for t in sorted(r.waits):
                                sem, val = target(t)
                                kk = id(sem)
                                if hw.get(kk, -1) >= val:
                                    continue
                                hw[kk] = val
                                eng.wait_ge(sem, val)
                            if r.fn is None:
                                if r.inc:
                                    eng.nop().then_inc(sems[("c", e)], 1)
                                continue
                            h = r.fn(eng)
                            if r.dma:
                                if r.tok[0] == "x":
                                    h.then_inc(sems[("x", r.tok[1])], 16)
                                else:
                                    h.then_inc(sems[("d", e)], 16)
                            elif r.inc:
                                h.then_inc(sems[("c", e)], 1)
                    return body
                getattr(block, e)(make(e))


class TH:
    __slots__ = ("ap", "buf")

    def __init__(self, ap, buf):
        self.ap = ap
        self.buf = buf

    def __getitem__(self, sl):
        return TH(self.ap[sl], self.buf)


class Arena:
    def __init__(self, tensor, cols, name):
        self.t = tensor
        self.cols = cols
        self.pos = 0
        self.marks = []
        self.name = name
        self.recs = []

    def _mkbuf(self, n, name):
        start, end = self.pos, self.pos + n
        b = Buf(name)
        keep = []
        for (s0, e0, ob) in self.recs:
            if s0 < end and start < e0:
                if ob.w is not None:
                    b.rs.append(ob.w)
                b.rs.extend(ob.rs)
            else:
                keep.append((s0, e0, ob))
        keep.append((start, end, b))
        self.recs = keep
        self.pos = end
        return b

    def alloc(self, n, name=""):
        assert self.pos + n <= self.cols, \
            f"{self.name} overflow at {name}: {self.pos}+{n}>{self.cols}"
        start = self.pos
        b = self._mkbuf(n, name)
        return TH(self.t[:, start:start + n], b)

    def row(self, n, name=""):
        assert self.pos + n <= self.cols
        start = self.pos
        b = self._mkbuf(n, name)
        return TH(self.t[0:1, start:start + n], b)

    def push(self):
        self.marks.append(self.pos)

    def pop(self):
        self.pos = self.marks.pop()



def _host_tables():
    t = {}
    t["ident_bf"] = np.eye(128, dtype=np.float32).astype(BF)
    r = np.arange(128)
    t["utri"] = (r[:, None] <= r[None, :]).astype(np.float32)
    t["neg_utri"] = -t["utri"]
    t["stril"] = (r[:, None] > r[None, :]).astype(np.float32)
    t["ones_row"] = np.ones((1, 512), np.float32)
    t["neg30"] = np.full((128, 1152), NEG30, np.float32).astype(BF)
    t["iota512"] = np.broadcast_to(np.arange(512, dtype=np.float32),
                                   (128, 512)).copy()
    pos = np.arange(T, dtype=np.float32)
    inv_lams = 1.0 / 10000.0 ** (np.arange(0, D, 2, dtype=np.float32) / D)
    pre = pos[:, None] * inv_lams[None, :]
    emb = np.concatenate([np.sin(pre), np.cos(pre)], axis=-1)
    emb = np.flip(emb, axis=0)
    embT = np.zeros((D, XLCOLS), np.float32)
    embT[:, :T] = emb.T
    t["embT"] = embT.astype(BF)
    return t


def _host_percore(c):
    d = {}
    onehot = np.zeros((128, 8), np.float32)
    onehot[:, c] = 1.0
    d["sel"] = onehot
    offq, offwv, offxlr, offpad = [], [], [], []
    CONTRIBQ, TILEQ = 16 * 128 * 128, 128 * 128
    CONTRIBW, TILEW = 16 * 128 * V1, 128 * V1
    for r in range(16):
        rp = r % 8
        if r < 8:
            offq.append(r * CONTRIBQ + c * TILEQ)
        else:
            u = r - 8
            offq.append((7 - u) * CONTRIBQ + (8 + c) * TILEQ)
        I0 = (8 * rp + c) * 128
        offxlr.append((T - 1) - (I0 + 127))
        offpad.append(0)
    for j in range(8):
        offwv.append(j * CONTRIBW + c * TILEW)
    for j in range(8):
        offwv.append(j * CONTRIBW + (8 + (7 - c)) * TILEW)
    d["offq"] = np.array(offq, np.int32).reshape(1, 16)
    d["offwv"] = np.array(offwv, np.int32).reshape(1, 16)
    d["offxlr"] = np.array(offxlr, np.int32).reshape(1, 16)
    d["offpad"] = np.array(offpad, np.int32).reshape(1, 16)
    gi = np.zeros((128, 16), np.float32)
    for r in range(16):
        I0 = (8 * (r % 8) + c) * 128
        gi[:, r] = I0 + np.arange(128)
    d["gi"] = gi
    return d


def _shard_inputs(inp):
    tabs = _host_tables()
    x = np.ascontiguousarray(inp["L0_attn_x"].astype(np.float32))
    w = {}
    w["l0f_w1T"] = inp["L0_ffn_w1"].T
    w["l0f_w2T"] = inp["L0_ffn_w2"].T
    w["l0f_b1"] = inp["L0_ffn_b1"].reshape(1, -1)
    w["l0f_b2"] = inp["L0_ffn_b2"].reshape(1, -1)
    w["l0f_ln1w"] = inp["L0_ffn_ln1_w"].reshape(1, -1)
    w["l0f_ln1b"] = inp["L0_ffn_ln1_b"].reshape(1, -1)
    w["l0f_ln2w"] = inp["L0_ffn_ln2_w"].reshape(1, -1)
    w["l0f_ln2b"] = inp["L0_ffn_ln2_b"].reshape(1, -1)
    w["l0a_WqT"] = inp["L0_attn_Wq"].T
    w["l0a_WkT"] = inp["L0_attn_Wk"].T
    w["l0a_WvT"] = inp["L0_attn_Wv"].T
    w["l0a_WgT"] = inp["L0_attn_Wg"].T
    w["l0a_WabT"] = np.concatenate([inp["L0_attn_Wa"], inp["L0_attn_Wb"]], 0).T
    w["l0a_WoT"] = inp["L0_attn_Wo"].T
    w["l0a_bq"] = inp["L0_attn_bq"].reshape(1, -1)
    w["l0a_bk"] = inp["L0_attn_bk"].reshape(1, -1)
    w["l0a_bv"] = inp["L0_attn_bv"].reshape(1, -1)
    w["l0a_bg"] = inp["L0_attn_bg"].reshape(1, -1)
    w["l0a_bab"] = np.concatenate(
        [inp["L0_attn_ba"], inp["L0_attn_bb"]]).reshape(1, 2)
    w["l0a_bo"] = inp["L0_attn_bo"].reshape(1, -1)
    w["l1f_wgT"] = inp["L1_ffn_w_gate"].T
    w["l1f_wuT"] = inp["L1_ffn_w_up"].T
    w["l1f_wdT"] = inp["L1_ffn_w_down"].T
    w["l1a_gln"] = inp["L1_attn_g_ln"].reshape(1, -1)
    w["l1a_Wq"] = inp["L1_attn_W_q"]
    w["l1a_Wk"] = inp["L1_attn_W_k"]
    w["l1a_Wv"] = inp["L1_attn_W_v"]
    w["l1a_Wg"] = inp["L1_attn_W_g"]
    w["l1a_Wr"] = inp["L1_attn_W_r"]
    w["l1a_xlu"] = inp["L1_attn_xl_u"].reshape(-1, 1)
    w["l1a_xlv"] = inp["L1_attn_xl_v"].reshape(-1, 1)
    w["l1a_Wo"] = inp["L1_attn_W_o"]
    in_maps = []
    for c in range(NCORE):
        m = {}
        b0 = x[0, 1024 * c:1024 * (c + 1)]
        b1 = x[1, 1024 * (7 - c):1024 * (8 - c)]
        m["x_shard"] = np.ascontiguousarray(np.concatenate([b0, b1], 0))
        for k2, v in w.items():
            m[k2] = np.ascontiguousarray(v)
        for k2, v in tabs.items():
            m[k2] = np.ascontiguousarray(v)
        m.update(_host_percore(c))
        in_maps.append(m)
    return in_maps


def _unshard_output(results):
    y = np.zeros((B, T, D), np.float32)
    for c in range(NCORE):
        ys = results[c]["y_shard"]
        y[0, 1024 * c:1024 * (c + 1)] = ys[:1024]
        y[1, 1024 * (7 - c):1024 * (8 - c)] = ys[1024:]
    return y



class St:
    pass


def declare_io(nc):
    st = St()
    st.nc = nc
    di = {}

    def inp(name, shape, dt=F32):
        di[name] = nc.dram_tensor(name, list(shape), dt, kind="ExternalInput")

    inp("x_shard", (TPC, D))
    inp("l0f_w1T", (D, H)); inp("l0f_w2T", (H, D))
    inp("l0f_b1", (1, H)); inp("l0f_b2", (1, D))
    inp("l0f_ln1w", (1, D)); inp("l0f_ln1b", (1, D))
    inp("l0f_ln2w", (1, H)); inp("l0f_ln2b", (1, H))
    inp("l0a_WqT", (D, C)); inp("l0a_WkT", (D, C)); inp("l0a_WvT", (D, C))
    inp("l0a_WgT", (D, C)); inp("l0a_WabT", (D, 2)); inp("l0a_WoT", (C, D))
    inp("l0a_bq", (1, C)); inp("l0a_bk", (1, C)); inp("l0a_bv", (1, C))
    inp("l0a_bg", (1, C)); inp("l0a_bab", (1, 2)); inp("l0a_bo", (1, D))
    inp("l1f_wgT", (D, H)); inp("l1f_wuT", (D, H)); inp("l1f_wdT", (H, D))
    inp("l1a_gln", (1, D))
    inp("l1a_Wq", (D, C)); inp("l1a_Wk", (D, C))
    inp("l1a_Wv", (D, V1)); inp("l1a_Wg", (D, V1)); inp("l1a_Wr", (D, C))
    inp("l1a_xlu", (C, 1)); inp("l1a_xlv", (C, 1)); inp("l1a_Wo", (V1, D))
    inp("ident_bf", (128, 128), BF16)
    inp("utri", (128, 128)); inp("neg_utri", (128, 128))
    inp("stril", (128, 128))
    inp("ones_row", (1, 512)); inp("neg30", (128, 1152), BF16)
    inp("embT", (D, XLCOLS), BF16)
    inp("sel", (128, 8))
    inp("offq", (1, 16), I32); inp("offwv", (1, 16), I32)
    inp("offxlr", (1, 16), I32); inp("offpad", (1, 16), I32)
    inp("gi", (128, 16)); inp("iota512", (128, 512))
    st.di = di
    st.y = nc.dram_tensor("y_shard", [TPC, D], F32, kind="ExternalOutput")
    if os.environ.get("MK_DBG", "0") == "1":
        st.dbg1 = nc.dram_tensor("dbg1", [128, 4096], F32,
                                 kind="ExternalOutput")
        st.dbg2 = nc.dram_tensor("dbg2", [128, 4096], F32,
                                 kind="ExternalOutput")
        st.dbg3 = nc.dram_tensor("dbg3", [128, 64], F32,
                                 kind="ExternalOutput")

    st.wq1 = nc.dram_tensor("wq1", [D, H], BF16)
    st.wq2 = nc.dram_tensor("wq2", [H, D], BF16)
    st.wqg = nc.dram_tensor("wqg", [D, H], BF16)
    st.wqu = nc.dram_tensor("wqu", [D, H], BF16)
    st.wqd = nc.dram_tensor("wqd", [H, D], BF16)
    st.xlrT = nc.dram_tensor("xlrT", [C * XLCOLS], BF16)
    st.stripd = nc.dram_tensor("stripd", [2 * 128 * STRIP_MAX], BF16)
    st.xfm_sp = nc.dram_tensor("xfm_sp", [128, 8 * TPC], BF16)
    st.st_in = nc.dram_tensor("st_in", [4, 128, 128], F32)
    st.st_out = nc.dram_tensor("st_out", [NCORE, 4, 128, 128], F32)
    st.q_in = nc.dram_tensor("q_in", [16, 128, 128], BF16)
    st.q_out = nc.dram_tensor("q_out", [NCORE * 16 * 128 * 128], BF16)
    st.kv_in = nc.dram_tensor("kv_in", [KVL], BF16)
    st.kv_out = nc.dram_tensor("kv_out", [NCORE * KVL], BF16)
    st.wv_in = nc.dram_tensor("wv_in", [16, 128, V1], BF16)
    st.wv_out = nc.dram_tensor("wv_out", [NCORE * 16 * 128 * V1], BF16)
    st.db = {nm: Buf(nm) for nm in
             ["wq1", "wq2", "wqg", "wqu", "wqd", "xlrT", "strip0", "strip1",
              "st_in", "st_out", "q_in", "q_out", "kv_in", "kv_out", "wv_in",
              "wv_out", "y", "xfm_sp"]}
    return st



def _fd(ap):
    try:
        dims = list(ap.ap)
    except Exception:
        return 10 ** 9
    n = 1
    for st_, cnt in dims[1:]:
        n *= cnt
    if len(dims) == 1:
        n = dims[0][1]
    return n


def _small(out_ap):
    return _fd(out_ap) <= 32


def ld_dram(st, q, dst_th, src_ap, extra_reads=()):
    st.S.op(q, lambda e, d=dst_th.ap, s=src_ap: e.dma_start(out=d, in_=s),
            reads=list(extra_reads), writes=[dst_th.buf], dma=True)


def st_dram(st, q, dst_ap, src_th, dbuf, extra_reads=(), dsem=None):
    st.S.op(q, lambda e, d=dst_ap, s=src_th.ap: e.dma_start(out=d, in_=s),
            reads=[src_th.buf] + list(extra_reads), writes=[dbuf], dma=True,
            dsem=dsem)


def mm(st, out_th, lhsT_th, rhs_th, start, stop):
    st.S.op("tensor",
            lambda e, o=out_th.ap, l=lhsT_th.ap, r=rhs_th.ap:
            e.matmul(o, l, r, start=start, stop=stop),
            reads=[lhsT_th.buf, rhs_th.buf], writes=[out_th.buf])


def tpose(st, out_th, in_th):
    idb = st.sb["ident_bf"]
    st.S.op("tensor",
            lambda e, o=out_th.ap, i=in_th.ap, d=idb.ap:
            e.matmul(o, i, d, start=True, stop=True, is_transpose=True),
            reads=[in_th.buf, idb.buf], writes=[out_th.buf])


def ts(st, out_th, in_th, s1, op0, s2=None, op1=ALU.bypass, accum=None):
    r = [in_th.buf]
    a1 = s1.ap if isinstance(s1, TH) else (None if s1 is None else float(s1))
    if isinstance(s1, TH):
        r.append(s1.buf)
    a2 = s2.ap if isinstance(s2, TH) else (None if s2 is None else float(s2))
    if isinstance(s2, TH):
        r.append(s2.buf)
    wr = [out_th.buf]
    aac = None
    if accum is not None:
        aac = accum.ap
        wr.append(accum.buf)
    st.S.op("vector",
            lambda e, o=out_th.ap, i=in_th.ap, x1=a1, x2=a2, ac=aac:
            e.tensor_scalar(o, i, x1, x2, op0, op1, accum_out=ac),
            reads=r, writes=wr, fence=_small(out_th.ap) or aac is not None)


def tt(st, out_th, a_th, b_th, op):
    st.S.op("vector",
            lambda e, o=out_th.ap, a=a_th.ap, b=b_th.ap:
            e.tensor_tensor(o, a, b, op),
            reads=[a_th.buf, b_th.buf], writes=[out_th.buf],
            fence=_small(out_th.ap))


def stt(st, out_th, in0_th, s, op0, op1, in1_th):
    r = [in0_th.buf, in1_th.buf]
    a = s.ap if isinstance(s, TH) else float(s)
    if isinstance(s, TH):
        r.append(s.buf)
    st.S.op("vector",
            lambda e, o=out_th.ap, i0=in0_th.ap, x=a, i1=in1_th.ap:
            e.scalar_tensor_tensor(o, i0, x, i1, op0, op1),
            reads=r, writes=[out_th.buf])


def act(st, out_th, in_th, func, bias=0.0, scale=1.0, accum=None):
    r = [in_th.buf]
    ab = bias.ap if isinstance(bias, TH) else float(bias)
    if isinstance(bias, TH):
        r.append(bias.buf)
    asc = scale.ap if isinstance(scale, TH) else float(scale)
    if isinstance(scale, TH):
        r.append(scale.buf)
    wr = [out_th.buf]
    aac = None
    if accum is not None:
        aac = accum.ap
        wr.append(accum.buf)
    st.S.op("scalar",
            lambda e, o=out_th.ap, i=in_th.ap, bb=ab, ss=asc, ac=aac:
            e.activation(o, i, func, bias=bb, scale=ss, accum_out=ac),
            reads=r, writes=wr,
            fence=_small(out_th.ap) or aac is not None)
    if accum is not None:
        st.S.op("scalar",
                lambda e, o=accum.ap: e.mul(o, o, 1.0),
                reads=[accum.buf], writes=[accum.buf], fence=True)


def copy(st, out_th, in_th):
    st.S.op("vector", lambda e, o=out_th.ap, i=in_th.ap: e.tensor_copy(o, i),
            reads=[in_th.buf], writes=[out_th.buf],
            fence=_small(out_th.ap))


def red(st, out_th, in_th, op, absval=None):
    st.S.op("vector",
            lambda e, o=out_th.ap, i=in_th.ap:
            e.tensor_reduce(o, i, mybir.AxisListType.X, op,
                            apply_absolute_value=absval),
            reads=[in_th.buf], writes=[out_th.buf], fence=True)


def recip(st, out_th, in_th):
    act(st, out_th, in_th, AF.Ln)
    act(st, out_th, out_th, AF.Exp, scale=-1.0)


def memset(st, th, val):
    st.S.op("vector", lambda e, o=th.ap, v=float(val): e.memset(o, v),
            reads=[], writes=[th.buf], fence=_small(th.ap))


def pbs(st, i, n, off=0):
    p = st.PB[i]
    return TH(p.ap[:, off:off + n], p.buf)


def pbts(st, i, n, off=0):
    p = st.PBT[i]
    return TH(p.ap[:, off:off + n], p.buf)


def bcast_dram_row(st, name, width, out_th):
    lvl = int(os.environ.get("MK_BCAST_LVL", "3"))
    ones = st.sb["ones_row"]
    stage = st.rowstage
    src = st.di[name]
    for i in range((width + 511) // 512):
        w = min(512, width - i * 512)
        if lvl >= 1:
            ld_dram(st, "sync", TH(stage.ap[0:1, 0:w], stage.buf),
                    src[0:1, i * 512:i * 512 + w])
        p = pbs(st, 5, w)
        if lvl >= 2:
            st.S.op("tensor",
                    lambda e, o=p.ap, l=ones.ap[0:1, 0:128],
                    r=stage.ap[0:1, 0:w]:
                    e.matmul(o, l, r, start=True, stop=True),
                    reads=[ones.buf, stage.buf], writes=[p.buf])
        if lvl >= 3:
            st.S.op("scalar",
                    lambda e, o=out_th.ap[:, i * 512:i * 512 + w], ip=p.ap:
                    e.copy(o, ip),
                    reads=[p.buf], writes=[out_th.buf])


def bias_mm(st, p_th, row_th, n, row_off=0, transpose_roles=False):
    ones = st.sb["ones_row"]
    if transpose_roles:
        st.S.op("tensor",
                lambda e, o=p_th.ap, l=row_th.ap[0:1, row_off:row_off + 128],
                r=ones.ap[0:1, 0:n]:
                e.matmul(o, l, r, start=False, stop=True),
                reads=[row_th.buf, ones.buf], writes=[p_th.buf])
    else:
        st.S.op("tensor",
                lambda e, o=p_th.ap, l=ones.ap[0:1, 0:128],
                r=row_th.ap[0:1, row_off:row_off + n]:
                e.matmul(o, l, r, start=False, stop=True),
                reads=[row_th.buf, ones.buf], writes=[p_th.buf])


def batch_rstd(st, var_th, n, eps, out_th, scale_pre=1.0):
    tmp = st.asm.alloc(n, "rstd_tmp")
    ts(st, tmp, var_th, np.float32(scale_pre), ALU.mult, np.float32(eps),
       ALU.add)
    act(st, tmp, tmp, AF.Ln)
    act(st, out_th, tmp, AF.Exp, scale=-0.5)


def stats_all(st, width, tiles, mean_th, rstd_th, eps):
    n = len(tiles)
    sx = st.asm.alloc(n, "sx")
    sxx = st.asm.alloc(n, "sxx")
    junk = TH(st.junk.ap[:, 0:width], st.junk.buf)
    for i, t in enumerate(tiles):
        red(st, sx[:, i:i + 1], t, ALU.add)
        act(st, junk, t, AF.Square, accum=sxx[:, i:i + 1])
    ts(st, mean_th, sx, np.float32(1.0 / width), ALU.mult)
    ex2 = st.asm.alloc(n, "ex2")
    ts(st, ex2, sxx, np.float32(1.0 / width), ALU.mult)
    m2 = st.asm.alloc(n, "m2")
    tt(st, m2, mean_th, mean_th, ALU.mult)
    tt(st, ex2, ex2, m2, ALU.subtract)
    batch_rstd(st, ex2, n, eps, rstd_th)
    st.asm.pos -= 4 * n


def ffn_common_start(st):
    st.mean = st.asm.alloc(16, "mean")
    st.rstd = st.asm.alloc(16, "rstd")
    if os.environ.get("MK_STATS", "1") == "1":
        stats_all(st, D, st.xt, st.mean, st.rstd, 1e-5)
    else:
        memset(st, st.mean, 0.0)
        memset(st, st.rstd, 1.0)


def quant_chain(st, src_th, width, xq_th, isx_col, tscr):
    mx = st.asm.alloc(1, "mx")
    red(st, mx, src_th, ALU.max, absval=True)
    ts(st, mx, mx, np.float32(EPS5), ALU.max)
    sxc = st.asm.alloc(1, "sxc")
    recip(st, sxc, mx)
    ts(st, sxc, sxc, np.float32(127.0), ALU.mult)
    ts(st, isx_col, mx, np.float32(1.0 / 127.0), ALU.mult)
    t = TH(tscr.ap[:, 0:width], tscr.buf)
    ts(st, t, src_th, sxc, ALU.mult, MAGIC, ALU.add)
    ts(st, t, t, MAGIC, ALU.subtract, np.float32(127.0), ALU.min)
    ts(st, xq_th, t, np.float32(-128.0), ALU.max)
    st.asm.pos -= 2


def quant_weight(st, src, dstd, dbuf, rows, cols, sw_col, swr_col):
    a, f = st.abf, st.af32
    a.push(); f.push()
    nt = rows // 128
    csz = min(2048, cols)
    nch = cols // csz
    wtile = f.alloc(csz, "qw_w")
    tq = f.alloc(csz, "qw_t")
    qb = a.alloc(csz, "qw_q")
    absac = st.asm.alloc(nt, "absac")
    junk = TH(st.junk.ap[:, 0:1024], st.junk.buf)
    acc1 = st.asm.alloc(1, "acc1")
    for i in range(nt):
        for ch in range(nch):
            ld_dram(st, "sync", wtile,
                    src[i * 128:(i + 1) * 128, ch * csz:(ch + 1) * csz])
            for off in range(0, csz, 1024):
                for sgn in (1.0, -1.0):
                    act(st, junk, TH(wtile.ap[:, off:off + 1024], wtile.buf),
                        AF.Relu, scale=sgn, accum=acc1)
                    if ch == 0 and off == 0 and sgn == 1.0:
                        copy(st, TH(absac.ap[:, i:i + 1], absac.buf), acc1)
                    else:
                        tt(st, TH(absac.ap[:, i:i + 1], absac.buf),
                           TH(absac.ap[:, i:i + 1], absac.buf), acc1, ALU.add)
    onescol = TH(st.sb["utri"].ap[:, 127:128], st.sb["utri"].buf)
    pt = TH(st.PB[5].ap[0:1, 0:nt], st.PB[5].buf)
    st.S.op("tensor", lambda e, o=pt.ap, l=onescol.ap, r=absac.ap:
            e.matmul(o, l, r, start=True, stop=True),
            reads=[onescol.buf, absac.buf], writes=[st.PB[5].buf])
    tot = st.asm.alloc(1, "tot")
    tot1 = TH(tot.ap[0:1, 0:1], tot.buf)
    st.S.op("vector", lambda e, o=tot1.ap, i=pt.ap:
            e.tensor_reduce(o, i, mybir.AxisListType.X, ALU.add),
            reads=[st.PB[5].buf], writes=[tot.buf], fence=True)
    ts(st, tot1, tot1, np.float32(1.0 / (rows * cols)), ALU.mult,
       np.float32(EPS5), ALU.max)
    swr1 = st.asm.alloc(1, "swr1")
    sw1 = st.asm.alloc(1, "sw1")
    copy(st, TH(swr1.ap[0:1, 0:1], swr1.buf), tot1)
    recip(st, TH(sw1.ap[0:1, 0:1], sw1.buf), tot1)
    ones = st.sb["ones_row"]
    for src1, dstc in ((sw1, sw_col), (swr1, swr_col)):
        p = pbs(st, 5, 1)
        st.S.op("tensor",
                lambda e, o=p.ap, l=ones.ap[0:1, 0:128],
                r=src1.ap[0:1, 0:1]:
                e.matmul(o, l, r, start=True, stop=True),
                reads=[ones.buf, src1.buf], writes=[p.buf])
        copy(st, dstc, p)
    for i in range(nt):
        for ch in range(nch):
            ld_dram(st, "sync", wtile,
                    src[i * 128:(i + 1) * 128, ch * csz:(ch + 1) * csz])
            ts(st, tq, wtile, sw_col, ALU.mult, MAGIC, ALU.add)
            ts(st, tq, tq, MAGIC, ALU.subtract, np.float32(1.0), ALU.min)
            ts(st, qb, tq, np.float32(-1.0), ALU.max)
            st_dram(st, "gpsimd",
                    dstd[i * 128:(i + 1) * 128, ch * csz:(ch + 1) * csz],
                    qb, dbuf)
    st.asm.pos -= nt + 4
    a.pop(); f.pop()



def ffn0_phase(st, first):
    S, a, f = st.S, st.abf, st.af32
    a.push(); f.push(); st.asm.push()
    di = st.di
    if first and os.environ.get("MK_FFN0_QW", "1") == "1":
        quant_weight(st, di["l0f_w1T"], st.wq1, st.db["wq1"], D, H,
                     st.wsc[:, 0:1], st.wsc[:, 1:2])
        quant_weight(st, di["l0f_w2T"], st.wq2, st.db["wq2"], H, D,
                     st.wsc[:, 2:3], st.wsc[:, 3:4])
    swr1 = TH(st.wsc.ap[:, 1:2], st.wsc.buf)
    swr2 = TH(st.wsc.ap[:, 3:4], st.wsc.buf)
    ffn_common_start(st)
    bln1w = a.alloc(D, "bln1w"); bln1b = a.alloc(D, "bln1b")
    bln2w = a.alloc(H, "bln2w"); bln2b = a.alloc(H, "bln2b")
    if os.environ.get("MK_BCAST", "1") == "1":
        bcast_dram_row(st, "l0f_ln1w", D, bln1w)
        bcast_dram_row(st, "l0f_ln1b", D, bln1b)
        bcast_dram_row(st, "l0f_ln2w", H, bln2w)
        bcast_dram_row(st, "l0f_ln2b", H, bln2b)
    zb = [a.alloc(D, f"z{i}") for i in range(2)]
    hsup = a.alloc(4 * H, "hsup")
    xqfm = a.alloc(8 * 512, "xqfm")
    ghqT = xqfm
    wbuf = [a.alloc(8 * 1024, f"wb{i}") for i in range(2)]
    tscr = f.alloc(H, "tscr")
    uscr = f.alloc(512, "uscr")
    uscr2 = f.alloc(512, "uscr2")
    PB, PBT = st.PB, st.PBT

    wq1v = st.wq1.rearrange("(dt p) h -> p dt h", p=128)
    wq2v = st.wq2.rearrange("(ht p) dd -> p ht dd", p=128)

    for sbk in range(int(os.environ.get("MK_FFN0_NSB", "4"))):
        st.asm.push()
        tl = [sbk * 4 + i for i in range(4)]
        m2c = st.asm.alloc(4, "m2c")
        v2c = st.asm.alloc(4, "v2c")
        sxi = st.asm.alloc(4, "sxi")
        sc1 = st.asm.alloc(4, "sc1")
        sc2 = st.asm.alloc(4, "sc2")
        shs = st.asm.alloc(4, "shs")
        sqs = st.asm.alloc(4, "sqs")
        junkD = TH(st.junk.ap[:, 0:D], st.junk.buf)
        for i, tt_ in enumerate(tl):
            z = zb[i % 2]
            ts(st, z, st.xt[tt_], st.mean[:, tt_:tt_ + 1], ALU.subtract,
               st.rstd[:, tt_:tt_ + 1], ALU.mult)
            red(st, TH(m2c.ap[:, i:i + 1], m2c.buf), z, ALU.add)
            act(st, junkD, z, AF.Square, accum=v2c[:, i:i + 1])
        ts(st, m2c, m2c, np.float32(1.0 / D), ALU.mult)
        ts(st, v2c, v2c, np.float32(1.0 / D), ALU.mult)
        mm2_ = st.asm.alloc(4, "mm2_")
        tt(st, mm2_, m2c, m2c, ALU.mult)
        tt(st, v2c, v2c, mm2_, ALU.subtract)
        batch_rstd(st, v2c, 4, 1e-5, v2c)
        for i, tt_ in enumerate(tl):
            z = zb[i % 2]
            ts(st, z, st.xt[tt_], st.mean[:, tt_:tt_ + 1], ALU.subtract,
               st.rstd[:, tt_:tt_ + 1], ALU.mult)
            ts(st, z, z, m2c[:, i:i + 1], ALU.subtract, v2c[:, i:i + 1],
               ALU.mult)
            tt(st, z, z, bln1w, ALU.mult)
            tt(st, z, z, bln1b, ALU.add)
            quant_chain(st, z, D, z, sxi[:, i:i + 1], tscr)
            for dt in range(8):
                tpose(st, pbts(st, 0, 128, dt * 128),
                      z[:, dt * 128:(dt + 1) * 128])
            dst = TH(xqfm.ap.rearrange("p (dt q) -> p dt q", q=512)
                     [:, :, i * 128:(i + 1) * 128], xqfm.buf)
            copy(st, dst, TH(PBT[0].ap.rearrange("p (dt q) -> p dt q", q=128),
                             PBT[0].buf))
            tt(st, TH(sc1.ap[:, i:i + 1], sc1.buf),
               TH(sxi.ap[:, i:i + 1], sxi.buf), swr1, ALU.mult)
        chs = st.asm.alloc(1, "chs")
        junk5 = TH(st.junk.ap[:, 0:512], st.junk.buf)
        if sbk == 0:
            st.bb1w = a.alloc(1024, "bb1w")
        bb1w = st.bb1w
        bb2 = TH(bb1w.ap[:, 0:512], bb1w.buf)
        for hcp in range(4):
            wb = wbuf[hcp % 2]
            ld_dram(st, "sync", wb,
                    wq1v[:, :, hcp * 1024:(hcp + 1) * 1024])
            ld_dram(st, "sync", TH(st.rowstage.ap[0:1, 0:512],
                                   st.rowstage.buf),
                    di["l0f_b1"][0:1, hcp * 1024:hcp * 1024 + 512])
            p5 = pbs(st, 5, 512)
            st.S.op("tensor", lambda e, o=p5.ap,
                    l=st.sb["ones_row"].ap[0:1, 0:128],
                    r=st.rowstage.ap[0:1, 0:512]:
                    e.matmul(o, l, r, start=True, stop=True),
                    reads=[st.sb["ones_row"].buf, st.rowstage.buf],
                    writes=[p5.buf])
            copy(st, TH(bb1w.ap[:, 0:512], bb1w.buf), p5)
            ld_dram(st, "sync", TH(st.rowstage.ap[0:1, 0:512],
                                   st.rowstage.buf),
                    di["l0f_b1"][0:1, hcp * 1024 + 512:hcp * 1024 + 1024])
            p5 = pbs(st, 5, 512)
            st.S.op("tensor", lambda e, o=p5.ap,
                    l=st.sb["ones_row"].ap[0:1, 0:128],
                    r=st.rowstage.ap[0:1, 0:512]:
                    e.matmul(o, l, r, start=True, stop=True),
                    reads=[st.sb["ones_row"].buf, st.rowstage.buf],
                    writes=[p5.buf])
            copy(st, TH(bb1w.ap[:, 512:1024], bb1w.buf), p5)
            for i in range(4):
                pb0 = pbs(st, 0 if i % 2 == 0 else 2, 512)
                pb1 = pbs(st, 1 if i % 2 == 0 else 3, 512)
                for dt in range(8):
                    l = TH(xqfm.ap[:, dt * 512 + i * 128:
                                   dt * 512 + (i + 1) * 128], xqfm.buf)
                    mm(st, pb0, l,
                       TH(wb.ap[:, dt * 1024:dt * 1024 + 512], wb.buf),
                       start=(dt == 0), stop=(dt == 7))
                    mm(st, pb1, l,
                       TH(wb.ap[:, dt * 1024 + 512:(dt + 1) * 1024], wb.buf),
                       start=(dt == 0), stop=(dt == 7))
                for sub, pbk, usc in ((0, pb0, uscr), (1, pb1, uscr2)):
                    hc = hcp * 2 + sub
                    usct = TH(usc.ap[:, 0:512], usc.buf)
                    stt(st, usct, pbk, TH(sc1.ap[:, i:i + 1], sc1.buf),
                        ALU.mult, ALU.add,
                        TH(bb1w.ap[:, sub * 512:(sub + 1) * 512], bb1w.buf))
                    hdst = TH(hsup.ap[:, i * H + hc * 512:
                                      i * H + (hc + 1) * 512], hsup.buf)
                    act(st, hdst, usct, AF.Gelu, accum=chs)
                    if hc == 0:
                        copy(st, TH(shs.ap[:, i:i + 1], shs.buf), chs)
                    else:
                        tt(st, TH(shs.ap[:, i:i + 1], shs.buf),
                           TH(shs.ap[:, i:i + 1], shs.buf), chs, ALU.add)
                    act(st, junk5, hdst, AF.Square, accum=chs)
                    if hc == 0:
                        copy(st, TH(sqs.ap[:, i:i + 1], sqs.buf), chs)
                    else:
                        tt(st, TH(sqs.ap[:, i:i + 1], sqs.buf),
                           TH(sqs.ap[:, i:i + 1], sqs.buf), chs, ALU.add)
        if sbk == 0 and os.environ.get("MK_DBG", "0") == "1":
            st_dram(st, "gpsimd", st.dbg1[:, :], xqfm, Buf("dbg1"))
            st_dram(st, "gpsimd", st.dbg2[:, :],
                    TH(hsup.ap[:, 0:H], hsup.buf), Buf("dbg2"))
            d3 = f.alloc(64, "d3scr")
            copy(st, TH(d3.ap[:, 0:12], d3.buf), st.wsc)
            copy(st, TH(d3.ap[:, 12:28], d3.buf), st.mean)
            copy(st, TH(d3.ap[:, 28:44], d3.buf), st.rstd)
            copy(st, TH(d3.ap[:, 44:48], d3.buf), m2c)
            copy(st, TH(d3.ap[:, 48:52], d3.buf), v2c)
            copy(st, TH(d3.ap[:, 52:56], d3.buf), sc1)
            st_dram(st, "gpsimd", st.dbg3[:, :], d3, Buf("dbg3"))
        ts(st, shs, shs, np.float32(1.0 / H), ALU.mult)
        ts(st, sqs, sqs, np.float32(1.0 / H), ALU.mult)
        mh2 = st.asm.alloc(4, "mh2")
        tt(st, mh2, shs, shs, ALU.mult)
        tt(st, sqs, sqs, mh2, ALU.subtract)
        batch_rstd(st, sqs, 4, 1e-5, sqs)
        shi = st.asm.alloc(4, "shi")
        for i in range(4):
            hv = TH(hsup.ap[:, i * H:(i + 1) * H], hsup.buf)
            ts(st, hv, hv, shs[:, i:i + 1], ALU.subtract, sqs[:, i:i + 1],
               ALU.mult)
            tt(st, hv, hv, bln2w, ALU.mult)
            tt(st, hv, hv, bln2b, ALU.add)
            quant_chain(st, hv, H, hv, shi[:, i:i + 1], tscr)
            tt(st, TH(sc2.ap[:, i:i + 1], sc2.buf),
               TH(shi.ap[:, i:i + 1], shi.buf), swr2, ALU.mult)
        for dc in range(2):
            ld_dram(st, "sync", TH(st.rowstage.ap[0:1, 0:512],
                                   st.rowstage.buf),
                    di["l0f_b2"][0:1, dc * 512:(dc + 1) * 512])
            p5 = pbs(st, 5, 512)
            st.S.op("tensor", lambda e, o=p5.ap,
                    l=st.sb["ones_row"].ap[0:1, 0:128],
                    r=st.rowstage.ap[0:1, 0:512]:
                    e.matmul(o, l, r, start=True, stop=True),
                    reads=[st.sb["ones_row"].buf, st.rowstage.buf],
                    writes=[p5.buf])
            copy(st, bb2, p5)
            wb2 = TH(wbuf[dc % 2].ap[:, 0:4096], wbuf[dc % 2].buf)
            for htg in range(4):
                ld_dram(st, "sync", wb2,
                        wq2v[:, htg * 8:(htg + 1) * 8,
                             dc * 512:(dc + 1) * 512])
                for i in range(4):
                    for hh in range(8):
                        ht = htg * 8 + hh
                        tpose(st, pbts(st, 1, 128, hh * 128),
                              TH(hsup.ap[:, i * H + ht * 128:
                                         i * H + (ht + 1) * 128], hsup.buf))
                    dst = TH(ghqT.ap.rearrange("p (g q) -> p g q", q=512)
                             [:, :, i * 128:(i + 1) * 128], ghqT.buf)
                    copy(st, dst, TH(
                        PBT[1].ap.rearrange("p (g q) -> p g q", q=128),
                        PBT[1].buf))
                for i in range(4):
                    pbk = pbs(st, 2 + i, 512)
                    for hh in range(8):
                        mm(st, pbk,
                           TH(ghqT.ap[:, hh * 512 + i * 128:
                                      hh * 512 + (i + 1) * 128], ghqT.buf),
                           TH(wb2.ap[:, hh * 512:(hh + 1) * 512], wb2.buf),
                           start=(htg == 0 and hh == 0),
                           stop=(htg == 3 and hh == 7))
            for i, tt_ in enumerate(tl):
                pbk = pbs(st, 2 + i, 512)
                stt(st, uscr, pbk, TH(sc2.ap[:, i:i + 1], sc2.buf), ALU.mult,
                    ALU.add, bb2)
                xd = st.xt[tt_][:, dc * 512:(dc + 1) * 512]
                stt(st, xd, uscr, np.float32(0.5), ALU.mult, ALU.add, xd)
        st.asm.pop()
    st.asm.pop(); a.pop(); f.pop()



def att0_phase(st):
    S, a, f = st.S, st.abf, st.af32
    a.push(); f.push(); st.asm.push()
    di = st.di
    PB, PBT = st.PB, st.PBT
    ones = st.sb["ones_row"]
    utri = st.sb["utri"]
    stril = st.sb["stril"]
    neg_utri = st.sb["neg_utri"]
    ffn_common_start(st)

    def apc(name, arena=a):
        return [arena.alloc(128, f"{name}{i}") for i in range(NTILE)]
    qT, k_n, bK, bV, WtT, UT, DU, Rc, Ph, gate = (
        apc("qT"), apc("k_n"), apc("bK"), apc("bV"), apc("WtT"), apc("UT"),
        apc("DU"), apc("Rc"), apc("Ph"), apc("gate"))
    graw = apc("graw")
    GUt = apc("GUt")
    kraw = f.alloc(128, "kraw")
    zab = [f.alloc(2, f"zab{i}") for i in range(NTILE)]
    crow = [f.alloc(1, f"crow{i}") for i in range(NTILE)]
    crw = [f.alloc(128, f"crw{i}") for i in range(NTILE)]
    la = st.asm.alloc(16, "la")
    beta = st.asm.alloc(16, "beta")
    ssk = st.asm.alloc(16, "ssk")
    gam = [f.alloc(1, f"gam{i}") for i in range(NTILE)]
    gamL = [f.alloc(1, f"gamL{i}") for i in range(NTILE)]
    Dcol = [f.alloc(1, f"Dcol{i}") for i in range(NTILE)]
    woT = a.alloc(D, "woT")
    woT_f = f.alloc(D, "woTf")
    ld_dram(st, "sync", woT_f, di["l0a_WoT"][:, :])
    copy(st, woT, woT_f)
    f.pos -= D

    a.push(); f.push()
    xfm = a.alloc(8 * TPC, "xfm")
    zscr = a.alloc(D, "zscr")
    for tt_ in range(NTILE):
        ts(st, zscr, st.xt[tt_], st.mean[:, tt_:tt_ + 1], ALU.subtract,
           st.rstd[:, tt_:tt_ + 1], ALU.mult)
        for dt in range(8):
            tpose(st, pbts(st, 0, 128, dt * 128),
                  zscr[:, dt * 128:(dt + 1) * 128])
        dst = TH(xfm.ap.rearrange("p (dt q) -> p dt q", q=TPC)
                 [:, :, tt_ * 128:(tt_ + 1) * 128], xfm.buf)
        copy(st, dst, TH(PBT[0].ap.rearrange("p (dt q) -> p dt q", q=128),
                         PBT[0].buf))
    pw = {}
    for nm, wd in [("l0a_WqT", C), ("l0a_WkT", C), ("l0a_WvT", C),
                   ("l0a_WgT", C), ("l0a_WabT", 2)]:
        wf = f.alloc(wd * 8, "pwf")
        v = di[nm].rearrange("(dt p) w -> p dt w", p=128)
        ld_dram(st, "sync",
                TH(wf.ap.rearrange("p (dt w) -> p dt w", w=wd), wf.buf),
                v[:, :, :])
        wb_ = a.alloc(wd * 8, "pwb")
        copy(st, wb_, wf)
        pw[nm] = wb_
        f.pos -= wd * 8

    for ch in range(NTILE):
        xsl = TH(xfm.ap.rearrange("p (dt q) -> p dt q", q=TPC)
                 [:, :, ch * 128:(ch + 1) * 128], xfm.buf)
        p0 = pbs(st, 0, 128)
        for dt in range(8):
            mm(st, p0, TH(pw["l0a_WqT"].ap[:, dt * C:(dt + 1) * C],
                          pw["l0a_WqT"].buf),
               TH(xsl.ap[:, dt, :], xsl.buf), start=(dt == 0), stop=False)
        bias_mm(st, p0, st.sb["l0a_bq"], 128, transpose_roles=True)
        copy(st, qT[ch], p0)
        p1 = pbs(st, 1, 128)
        for dt in range(8):
            mm(st, p1, TH(xsl.ap[:, dt, :], xsl.buf),
               TH(pw["l0a_WkT"].ap[:, dt * C:(dt + 1) * C],
                  pw["l0a_WkT"].buf), start=(dt == 0), stop=False)
        bias_mm(st, p1, st.sb["l0a_bk"], 128)
        copy(st, kraw, p1)
        act(st, TH(st.junk.ap[:, 0:128], st.junk.buf), kraw, AF.Square,
            accum=ssk[:, ch:ch + 1])
        copy(st, k_n[ch], kraw)
        p2 = pbs(st, 2, 128)
        for dt in range(8):
            mm(st, p2, TH(xsl.ap[:, dt, :], xsl.buf),
               TH(pw["l0a_WvT"].ap[:, dt * C:(dt + 1) * C],
                  pw["l0a_WvT"].buf), start=(dt == 0), stop=False)
        bias_mm(st, p2, st.sb["l0a_bv"], 128)
        copy(st, bV[ch], p2)
        p3 = pbs(st, 3, 128)
        for dt in range(8):
            mm(st, p3, TH(xsl.ap[:, dt, :], xsl.buf),
               TH(pw["l0a_WgT"].ap[:, dt * C:(dt + 1) * C],
                  pw["l0a_WgT"].buf), start=(dt == 0), stop=False)
        bias_mm(st, p3, st.sb["l0a_bg"], 128)
        copy(st, graw[ch], p3)
        p4 = pbs(st, 4, 2)
        for dt in range(8):
            mm(st, p4, TH(xsl.ap[:, dt, :], xsl.buf),
               TH(pw["l0a_WabT"].ap[:, dt * 2:(dt + 1) * 2],
                  pw["l0a_WabT"].buf), start=(dt == 0), stop=False)
        bias_mm(st, p4, st.sb["l0a_bab"], 2)
        copy(st, zab[ch], p4)
    st.S.barrier()
    a.pop(); f.pop()
    for ch in range(NTILE):
        act(st, gate[ch], graw[ch], AF.Sigmoid)
        act(st, TH(beta.ap[:, ch:ch + 1], beta.buf),
            TH(zab[ch].ap[:, 1:2], zab[ch].buf), AF.Sigmoid)
    for ch in range(NTILE):
        act(st, TH(la.ap[:, ch:ch + 1], la.buf),
            TH(zab[ch].ap[:, 0:1], zab[ch].buf), AF.Sigmoid)
    act(st, la, la, AF.Ln)
    kn_s = st.asm.alloc(16, "kn_s")
    kn_t = st.asm.alloc(16, "kn_t")
    ts(st, kn_t, ssk, np.float32(1.0), ALU.mult, np.float32(1e-30), ALU.max)
    act(st, kn_t, kn_t, AF.Ln)
    act(st, kn_t, kn_t, AF.Exp, scale=-0.5)
    ts(st, kn_s, kn_t, np.float32(1e12), ALU.min)
    for ch in range(NTILE):
        ts(st, k_n[ch], k_n[ch], kn_s[:, ch:ch + 1], ALU.mult)
        ts(st, bK[ch], k_n[ch], TH(beta.ap[:, ch:ch + 1], beta.buf), ALU.mult)
        ts(st, bV[ch], bV[ch], TH(beta.ap[:, ch:ch + 1], beta.buf), ALU.mult)
    Gt_args = []
    for ch in range(NTILE):
        lac = TH(la.ap[:, ch:ch + 1], la.buf)
        p0 = pbs(st, 0, 1)
        st.S.op("tensor", lambda e, o=p0.ap, l=utri.ap, r=lac.ap:
                e.matmul(o, l, r, start=True, stop=True),
                reads=[utri.buf, la.buf], writes=[p0.buf])
        copy(st, crow[ch], p0)
        p1 = TH(PB[1].ap[0:1, 0:128], PB[1].buf)
        st.S.op("tensor", lambda e, o=p1.ap, l=lac.ap, r=utri.ap:
                e.matmul(o, l, r, start=True, stop=True),
                reads=[utri.buf, la.buf], writes=[PB[1].buf])
        cr1 = TH(crw[ch].ap[0:1, 0:128], crw[ch].buf)
        copy(st, cr1, p1)
        p2 = pbs(st, 2, 128)
        st.S.op("tensor", lambda e, o=p2.ap, l=ones.ap[0:1, 0:128], r=cr1.ap:
                e.matmul(o, l, r, start=True, stop=True),
                reads=[ones.buf, crw[ch].buf], writes=[p2.buf])
        garg = f.alloc(128, f"garg{ch}")
        ts(st, garg, p2, crow[ch], ALU.subtract, np.float32(0.0), ALU.min)
        Gt_args.append(garg)
    for ch in range(NTILE):
        act(st, GUt[ch], Gt_args[ch], AF.Exp)
        tt(st, GUt[ch], GUt[ch], utri, ALU.mult)
    f.pos -= NTILE * 128
    for ch in range(NTILE):
        act(st, gam[ch], crow[ch], AF.Exp)
        cr1 = TH(crw[ch].ap[0:1, 0:128], crw[ch].buf)
        p0 = pbs(st, 0, 1)
        st.S.op("tensor", lambda e, o=p0.ap, l=ones.ap[0:1, 0:128],
                r=cr1.ap[0:1, 127:128]:
                e.matmul(o, l, r, start=True, stop=True),
                reads=[ones.buf, crw[ch].buf], writes=[p0.buf])
        cl = f.alloc(1, "cl")
        copy(st, cl, p0)
        dar = f.alloc(1, "dar")
        tt(st, dar, cl, crow[ch], ALU.subtract)
        act(st, Dcol[ch], dar, AF.Exp)
        act(st, gamL[ch], cl, AF.Exp)
        f.pos -= 2
    kTn = a.alloc(128, "kTn")
    Sm = a.alloc(128, "Sm"); SmT = a.alloc(128, "SmT")
    accm = a.alloc(128, "accm"); accT = a.alloc(128, "accT")
    Mt = a.alloc(128, "Mt")
    M2t = a.alloc(128, "M2t")
    WtS = a.alloc(128, "WtS")
    Us = a.alloc(128, "Us")
    gI = f.alloc(128, "gI")
    ngl = f.alloc(1, "ngl")
    for ch in range(NTILE):
        tpose(st, pbts(st, 0, 128), k_n[ch])
        copy(st, kTn, pbts(st, 0, 128))
        p0 = pbs(st, 0, 128)
        mm(st, p0, kTn, kTn, True, True)
        stt(st, Sm, p0, TH(beta.ap[:, ch:ch + 1], beta.buf), ALU.mult,
            ALU.mult, stril)
        ts(st, Sm, Sm, np.float32(-1.0), ALU.mult)
        tt(st, accm, Sm, st.sb["ident_bf"], ALU.add)
        for it in range(6):
            tpose(st, pbts(st, 0, 128), Sm)
            copy(st, SmT, pbts(st, 0, 128))
            p1 = pbs(st, 1, 128)
            mm(st, p1, SmT, Sm, True, True)
            copy(st, Sm, p1)
            tpose(st, pbts(st, 0, 128), accm)
            copy(st, accT, pbts(st, 0, 128))
            p1 = pbs(st, 1, 128)
            mm(st, p1, accT, Sm, True, True)
            tt(st, accm, accm, p1, ALU.add)
        tpose(st, pbts(st, 0, 128), accm)
        copy(st, Mt, pbts(st, 0, 128))
        p1 = pbs(st, 1, 128)
        mm(st, p1, bK[ch], Mt, True, True)
        copy(st, WtT[ch], p1)
        tt(st, M2t, Mt, GUt[ch], ALU.mult)
        p2 = pbs(st, 2, 128)
        mm(st, p2, bV[ch], M2t, True, True)
        copy(st, UT[ch], p2)
        tpose(st, pbts(st, 0, 128), UT[ch])
        copy(st, Us, pbts(st, 0, 128))
        ts(st, DU[ch], Us, Dcol[ch], ALU.mult)
        p3 = pbs(st, 3, 128)
        mm(st, p3, k_n[ch], DU[ch], True, True)
        copy(st, Rc[ch], p3)
        tpose(st, pbts(st, 0, 128), WtT[ch])
        copy(st, WtS, pbts(st, 0, 128))
        p4 = pbs(st, 4, 128)
        mm(st, p4, k_n[ch], WtS, True, True)
        ts(st, gI, st.sb["ident_bf"], gamL[ch], ALU.mult)
        ts(st, ngl, gamL[ch], np.float32(-1.0), ALU.mult)
        stt(st, Ph[ch], p4, ngl, ALU.mult, ALU.add, gI)
    PhC = [f.alloc(128, f"PhC{b}") for b in range(2)]
    RcC = [f.alloc(128, f"RcC{b}") for b in range(2)]
    tmpb = a.alloc(128, "tmpb")
    tmpb2 = a.alloc(128, "tmpb2")
    for b in range(2):
        chs_ = range(8) if b == 0 else range(8, 16)
        for j, ch in enumerate(chs_):
            if j == 0:
                copy(st, PhC[b], Ph[ch])
                copy(st, RcC[b], Rc[ch])
                continue
            tpose(st, pbts(st, 0, 128), Ph[ch])
            copy(st, tmpb, pbts(st, 0, 128))
            copy(st, tmpb2, PhC[b])
            p0 = pbs(st, 0, 128)
            mm(st, p0, tmpb, tmpb2, True, True)
            copy(st, PhC[b], p0)
            copy(st, tmpb2, RcC[b])
            p1 = pbs(st, 1, 128)
            mm(st, p1, tmpb, tmpb2, True, True)
            stt(st, RcC[b], p1, np.float32(1.0), ALU.mult, ALU.add, Rc[ch])
    for b in range(2):
        st_dram(st, "gpsimd", st.st_in[2 * b + 0, :, :], PhC[b],
                st.db["st_in"])
        st_dram(st, "gpsimd", st.st_in[2 * b + 1, :, :], RcC[b],
                st.db["st_in"])
    S.op("gpsimd", lambda e: e.collective_compute(
        "AllGather", ALU.bypass, replica_groups=[list(range(NCORE))],
        ins=[st.st_in[:]], outs=[st.st_out[:]]),
        reads=[st.db["st_in"]], writes=[st.db["st_out"]], always_inc=True)
    gst = a.alloc(NCORE * 4 * 128, "gst")
    gstage = f.alloc(512, "gstage")
    for r_ in range(NCORE):
        S.op("sync", lambda e, d=gstage.ap.rearrange(
            "p (k q) -> p k q", q=128), r2=r_: e.dma_start(
            out=d, in_=bass.AP(st.st_out, r2 * 65536,
                               [[128, 128], [16384, 4], [1, 128]])),
            reads=[st.db["st_out"]], writes=[gstage.buf], dma=True)
        copy(st, TH(gst.ap[:, r_ * 512:(r_ + 1) * 512], gst.buf), gstage)
    f.pos -= 512

    def gPh(r, b):
        return TH(gst.ap[:, (r * 4 + 2 * b) * 128:(r * 4 + 2 * b + 1) * 128],
                  gst.buf)

    def gRc(r, b):
        return TH(gst.ap[:, (r * 4 + 2 * b + 1) * 128:
                         (r * 4 + 2 * b + 2) * 128], gst.buf)
    Astate = [f.alloc(128, f"Ast{b}") for b in range(2)]
    Tk = f.alloc(8 * 128, "Tk")
    sel = st.sb["sel"]
    for b in range(2):
        order = list(range(8)) if b == 0 else list(range(7, -1, -1))
        cur = TH(Tk.ap[:, 0:128], Tk.buf)
        memset(st, cur, 0.0)
        for posi in range(1, 8):
            prev_rank = order[posi - 1]
            prv = TH(Tk.ap[:, (posi - 1) * 128:posi * 128], Tk.buf)
            nxt = TH(Tk.ap[:, posi * 128:(posi + 1) * 128], Tk.buf)
            tpose(st, pbts(st, 0, 128), gPh(prev_rank, b))
            copy(st, tmpb, pbts(st, 0, 128))
            copy(st, tmpb2, prv)
            p0 = pbs(st, 0, 128)
            mm(st, p0, tmpb, tmpb2, True, True)
            stt(st, nxt, p0, np.float32(1.0), ALU.mult, ALU.add,
                gRc(prev_rank, b))
        firstk = True
        for kk in range(8):
            posi = kk if b == 0 else 7 - kk
            tkk = TH(Tk.ap[:, posi * 128:(posi + 1) * 128], Tk.buf)
            if firstk:
                ts(st, Astate[b], tkk, TH(sel.ap[:, kk:kk + 1], sel.buf),
                   ALU.mult)
                firstk = False
            else:
                stt(st, Astate[b], tkk, TH(sel.ap[:, kk:kk + 1], sel.buf),
                    ALU.mult, ALU.add, Astate[b])
    At = a.alloc(128, "At")
    Ab = a.alloc(128, "Ab")
    ZT = a.alloc(128, "ZT")
    Y1n = a.alloc(128, "Y1n")
    Y2w = a.alloc(128, "Y2w")
    Zn = a.alloc(128, "Zn")
    zg = a.alloc(128, "zg")
    og = a.alloc(128, "og")
    ogT = a.alloc(128, "ogT")
    WAb = a.alloc(128, "WAb")
    oA = f.alloc(128, "oA")
    ou = f.alloc(128, "ou")
    for ch in range(NTILE):
        b = 0 if ch < 8 else 1
        A = Astate[b]
        copy(st, Ab, A)
        tpose(st, pbts(st, 0, 128), Ab)
        copy(st, At, pbts(st, 0, 128))
        p0 = pbs(st, 0, 128)
        mm(st, p0, At, qT[ch], True, True)
        copy(st, ZT, p0)
        p1 = pbs(st, 1, 128)
        mm(st, p1, WtT[ch], ZT, True, True)
        stt(st, Y1n, p1, np.float32(1.0), ALU.mult, ALU.mult, neg_utri)
        p2 = pbs(st, 2, 128)
        mm(st, p2, UT[ch], qT[ch], True, True)
        stt(st, Y2w, p2, np.float32(1.0), ALU.mult, ALU.mult, GUt[ch])
        tpose(st, pbts(st, 0, 128), ZT)
        copy(st, Zn, pbts(st, 0, 128))
        ts(st, zg, Zn, gam[ch], ALU.mult)
        p3 = pbs(st, 3, 128)
        mm(st, p3, Y1n, k_n[ch], True, True)
        stt(st, oA, p3, gam[ch], ALU.mult, ALU.add, zg)
        p4 = pbs(st, 4, 128)
        mm(st, p4, Y2w, k_n[ch], True, True)
        stt(st, ou, p4, np.float32(1.0), ALU.mult, ALU.add, oA)
        tt(st, og, ou, gate[ch], ALU.mult)
        tpose(st, pbts(st, 0, 128), og)
        copy(st, ogT, pbts(st, 0, 128))
        for dc in range(2):
            pbk = pbs(st, dc, 512)
            mm(st, pbk, ogT, TH(woT.ap[:, dc * 512:(dc + 1) * 512], woT.buf),
               True, False)
            bias_mm(st, pbk, st.sb["l0a_bo"], 512, row_off=dc * 512)
            xd = st.xt[ch][:, dc * 512:(dc + 1) * 512]
            tt(st, xd, pbk, xd, ALU.add)
        p5 = pbs(st, 5, 128)
        mm(st, p5, WtT[ch], Ab, True, True)
        copy(st, WAb, p5)
        p5b = pbs(st, 5, 128)
        mm(st, p5b, k_n[ch], WAb, True, True)
        stt(st, oA, p5b, np.float32(-1.0), ALU.mult, ALU.add, A)
        stt(st, A, oA, gamL[ch], ALU.mult, ALU.add, Rc[ch])
    st.asm.pop(); a.pop(); f.pop()


def bias_mm_off(st, p_th, row_th, n, row_off):
    bias_mm(st, p_th, row_th, n, row_off=row_off)



def ffn1_phase(st, first):
    S, a, f = st.S, st.abf, st.af32
    a.push(); f.push(); st.asm.push()
    di = st.di
    PB, PBT = st.PB, st.PBT
    if first:
        quant_weight(st, di["l1f_wgT"], st.wqg, st.db["wqg"], D, H,
                     st.wsc[:, 4:5], st.wsc[:, 5:6])
        quant_weight(st, di["l1f_wuT"], st.wqu, st.db["wqu"], D, H,
                     st.wsc[:, 6:7], st.wsc[:, 7:8])
        quant_weight(st, di["l1f_wdT"], st.wqd, st.db["wqd"], H, D,
                     st.wsc[:, 8:9], st.wsc[:, 9:10])
    swgr = TH(st.wsc.ap[:, 5:6], st.wsc.buf)
    swur = TH(st.wsc.ap[:, 7:8], st.wsc.buf)
    swdr = TH(st.wsc.ap[:, 9:10], st.wsc.buf)
    ffn_common_start(st)
    zb = [a.alloc(D, f"z{i}") for i in range(2)]
    hsup = a.alloc(4 * H, "hsup")
    xqfm = a.alloc(8 * 512, "xqfm")
    ghqT = a.alloc(8 * 512, "ghqT")
    wbufg = [a.alloc(8 * 512, f"wbg{i}") for i in range(2)]
    wbufu = [a.alloc(8 * 512, f"wbu{i}") for i in range(2)]
    sgb = a.alloc(512, "sgb")
    tscr = f.alloc(H, "tscr")
    uscr = f.alloc(512, "uscr")
    wqgv = st.wqg.rearrange("(dt p) h -> p dt h", p=128)
    wquv = st.wqu.rearrange("(dt p) h -> p dt h", p=128)
    wqdv = st.wqd.rearrange("(ht p) dd -> p ht dd", p=128)

    for sbk in range(4):
        st.asm.push()
        tl = [sbk * 4 + i for i in range(4)]
        sxi = st.asm.alloc(4, "sxi")
        scg = st.asm.alloc(4, "scg")
        scu = st.asm.alloc(4, "scu")
        for i, tt_ in enumerate(tl):
            z = zb[i % 2]
            ts(st, z, st.xt[tt_], st.mean[:, tt_:tt_ + 1], ALU.subtract,
               st.rstd[:, tt_:tt_ + 1], ALU.mult)
            quant_chain(st, z, D, z, sxi[:, i:i + 1], tscr)
            for dt in range(8):
                tpose(st, pbts(st, 0, 128, dt * 128),
                      z[:, dt * 128:(dt + 1) * 128])
            dst = TH(xqfm.ap.rearrange("p (dt q) -> p dt q", q=512)
                     [:, :, i * 128:(i + 1) * 128], xqfm.buf)
            copy(st, dst, TH(PBT[0].ap.rearrange("p (dt q) -> p dt q", q=128),
                             PBT[0].buf))
            tt(st, TH(scg.ap[:, i:i + 1], scg.buf),
               TH(sxi.ap[:, i:i + 1], sxi.buf), swgr, ALU.mult)
            tt(st, TH(scu.ap[:, i:i + 1], scu.buf),
               TH(sxi.ap[:, i:i + 1], sxi.buf), swur, ALU.mult)
        for hc in range(8):
            wg = wbufg[hc % 2]
            wu = wbufu[hc % 2]
            ld_dram(st, "sync", wg, wqgv[:, :, hc * 512:(hc + 1) * 512])
            ld_dram(st, "sync", wu, wquv[:, :, hc * 512:(hc + 1) * 512])
            for i in range(4):
                pg = pbs(st, i % 2, 512)
                pu = pbs(st, 2 + i % 2, 512)
                for dt in range(8):
                    mm(st, pg,
                       TH(xqfm.ap[:, dt * 512 + i * 128:
                                  dt * 512 + (i + 1) * 128], xqfm.buf),
                       TH(wg.ap[:, dt * 512:(dt + 1) * 512], wg.buf),
                       start=(dt == 0), stop=(dt == 7))
                act(st, sgb, pg, AF.Silu,
                    scale=TH(scg.ap[:, i:i + 1], scg.buf))
                for dt in range(8):
                    mm(st, pu,
                       TH(xqfm.ap[:, dt * 512 + i * 128:
                                  dt * 512 + (i + 1) * 128], xqfm.buf),
                       TH(wu.ap[:, dt * 512:(dt + 1) * 512], wu.buf),
                       start=(dt == 0), stop=(dt == 7))
                ts(st, uscr, pu, TH(scu.ap[:, i:i + 1], scu.buf), ALU.mult)
                hdst = TH(hsup.ap[:, i * H + hc * 512:i * H + (hc + 1) * 512],
                          hsup.buf)
                tt(st, hdst, sgb, uscr, ALU.mult)
        shi = st.asm.alloc(4, "shi")
        sc2 = st.asm.alloc(4, "sc2")
        for i in range(4):
            hv = TH(hsup.ap[:, i * H:(i + 1) * H], hsup.buf)
            quant_chain(st, hv, H, hv, shi[:, i:i + 1], tscr)
            tt(st, TH(sc2.ap[:, i:i + 1], sc2.buf),
               TH(shi.ap[:, i:i + 1], shi.buf), swdr, ALU.mult)
        for dc in range(2):
            wb2 = wbufg[dc % 2]
            for htg in range(4):
                ld_dram(st, "sync", wb2,
                        wqdv[:, htg * 8:(htg + 1) * 8,
                             dc * 512:(dc + 1) * 512])
                for i in range(4):
                    for hh in range(8):
                        ht = htg * 8 + hh
                        tpose(st, pbts(st, 1, 128, hh * 128),
                              TH(hsup.ap[:, i * H + ht * 128:
                                         i * H + (ht + 1) * 128], hsup.buf))
                    dst = TH(ghqT.ap.rearrange("p (g q) -> p g q", q=512)
                             [:, :, i * 128:(i + 1) * 128], ghqT.buf)
                    copy(st, dst, TH(
                        PBT[1].ap.rearrange("p (g q) -> p g q", q=128),
                        PBT[1].buf))
                for i in range(4):
                    pbk = pbs(st, 2 + i, 512)
                    for hh in range(8):
                        mm(st, pbk,
                           TH(ghqT.ap[:, hh * 512 + i * 128:
                                      hh * 512 + (i + 1) * 128], ghqT.buf),
                           TH(wb2.ap[:, hh * 512:(hh + 1) * 512], wb2.buf),
                           start=(htg == 0 and hh == 0),
                           stop=(htg == 3 and hh == 7))
            for i, tt_ in enumerate(tl):
                pbk = pbs(st, 2 + i, 512)
                ts(st, uscr, pbk, TH(sc2.ap[:, i:i + 1], sc2.buf), ALU.mult)
                xd = st.xt[tt_][:, dc * 512:(dc + 1) * 512]
                stt(st, xd, uscr, np.float32(0.5), ALU.mult, ALU.add, xd)
        st.asm.pop()
    st.asm.pop(); a.pop(); f.pop()



_RREG = {}


def _reg_dma(e, st, sbuf_ap, dram, tbl, idx, pattern, extra=0, out_mode=False):
    assert extra == 0
    key = id(e)
    if key not in _RREG:
        _RREG[key] = e.register("roff_shared").__enter__()
    rg = _RREG[key]
    e.reg_load(rg, st.sb[tbl].ap[0:1, idx:idx + 1])
    if out_mode:
        return e.dma_start(out=bass.AP(dram, rg, pattern), in_=sbuf_ap)
    return e.dma_start(out=sbuf_ap, in_=bass.AP(dram, rg, pattern))


def fetch_q(st, r, dst_th):
    st.S.op("sync", lambda e, d=dst_th.ap, r_=r:
            _reg_dma(e, st, d, st.q_out, "offq", r_, [[128, 128], [1, 128]]),
            reads=[st.db["q_out"], st.sb["offq"].buf], writes=[dst_th.buf],
            dma=True, dsem="qf")


def att1_phase(st):
    S, a, f = st.S, st.abf, st.af32
    a.push(); f.push(); st.asm.push()
    di = st.di
    PB, PBT = st.PB, st.PBT
    ffn_common_start(st)
    a.push(); f.push()
    bgln = a.alloc(D, "bgln")
    bcast_dram_row(st, "l1a_gln", D, bgln)
    xfm = a.alloc(8 * TPC, "xfm")
    zscr = f.alloc(D, "zscr")
    zb2 = a.alloc(D, "zb2")
    rms = st.asm.alloc(16, "rms")
    for tt_ in range(NTILE):
        ts(st, zscr, st.xt[tt_], st.mean[:, tt_:tt_ + 1], ALU.subtract,
           st.rstd[:, tt_:tt_ + 1], ALU.mult)
        sszc = st.asm.alloc(1, "sszc")
        act(st, TH(st.junk.ap[:, 0:D], st.junk.buf), zscr, AF.Square,
            accum=sszc)
        batch_rstd(st, sszc, 1, 1e-6, TH(rms.ap[:, tt_:tt_ + 1], rms.buf),
                   scale_pre=1.0 / D)
        st.asm.pos -= 1
        stt(st, zb2, zscr, TH(rms.ap[:, tt_:tt_ + 1], rms.buf), ALU.mult,
            ALU.mult, bgln)
        for dt in range(8):
            tpose(st, pbts(st, 0, 128, dt * 128),
                  zb2[:, dt * 128:(dt + 1) * 128])
        dst = TH(xfm.ap.rearrange("p (dt q) -> p dt q", q=TPC)
                 [:, :, tt_ * 128:(tt_ + 1) * 128], xfm.buf)
        copy(st, dst, TH(PBT[0].ap.rearrange("p (dt q) -> p dt q", q=128),
                         PBT[0].buf))
    st_dram(st, "gpsimd", st.xfm_sp[:, :], xfm, st.db["xfm_sp"])
    wq_b = a.alloc(8 * C, "wq_b")
    wk_b = a.alloc(8 * C, "wk_b")
    wr_b = a.alloc(8 * C, "wr_b")
    for nm, dstw in (("l1a_Wq", wq_b), ("l1a_Wk", wk_b), ("l1a_Wr", wr_b)):
        wf = f.alloc(8 * C, "wf_")
        v = di[nm].rearrange("(dt p) w -> p dt w", p=128)
        ld_dram(st, "sync",
                TH(wf.ap.rearrange("p (dt w) -> p dt w", w=C), wf.buf),
                v[:, :, :])
        copy(st, dstw, wf)
        f.pos -= 8 * C
    wvr = a.alloc(8 * V1, "wvr")
    wv_f = f.alloc(V1, "wv_f")
    wvv = di["l1a_Wv"].rearrange("(dt p) w -> p dt w", p=128)
    for dt in range(8):
        ld_dram(st, "sync", wv_f, wvv[:, dt, :])
        copy(st, TH(wvr.ap[:, dt * V1:(dt + 1) * V1], wvr.buf), wv_f)
    f.pos -= V1
    qTt = a.alloc(128, "qTt")
    kTt = a.alloc(128, "kTt")
    vtile = a.alloc(V1, "vtile")
    for tt_ in range(NTILE):
        xsl = TH(xfm.ap.rearrange("p (dt q) -> p dt q", q=TPC)
                 [:, :, tt_ * 128:(tt_ + 1) * 128], xfm.buf)
        p0 = pbs(st, 0, 128)
        for dt in range(8):
            mm(st, p0, TH(wq_b.ap[:, dt * C:(dt + 1) * C], wq_b.buf),
               TH(xsl.ap[:, dt, :], xsl.buf), start=(dt == 0), stop=(dt == 7))
        ts(st, qTt, p0, INV_TAU, ALU.mult)
        st_dram(st, "gpsimd", st.q_in[tt_, :, :], qTt, st.db["q_in"])
        p1 = pbs(st, 1, 128)
        for dt in range(8):
            mm(st, p1, TH(wk_b.ap[:, dt * C:(dt + 1) * C], wk_b.buf),
               TH(xsl.ap[:, dt, :], xsl.buf), start=(dt == 0), stop=(dt == 7))
        copy(st, kTt, p1)
        b = 0 if tt_ < 8 else 1
        col = (tt_ % 8) * 128
        S.op("gpsimd", lambda e, s=kTt.ap,
             o=bass.AP(st.kv_in, b * 131072 + col, [[1024, 128], [1, 128]]):
             e.dma_start(out=o, in_=s),
             reads=[kTt.buf], writes=[st.db["kv_in"]], dma=True)
        for vc in range(4):
            pbk = pbs(st, 2 + (vc % 2), 512)
            for dt in range(8):
                mm(st, pbk, TH(xsl.ap[:, dt, :], xsl.buf),
                   TH(wvr.ap[:, dt * V1 + vc * 512:dt * V1 + (vc + 1) * 512],
                      wvr.buf),
                   start=(dt == 0), stop=(dt == 7))
            copy(st, TH(vtile.ap[:, vc * 512:(vc + 1) * 512], vtile.buf), pbk)
        S.op("gpsimd", lambda e, s=vtile.ap,
             o=bass.AP(st.kv_in, KV_VOFF + b * (1024 * V1) + col * V1,
                       [[V1, 128], [1, V1]]):
             e.dma_start(out=o, in_=s),
             reads=[vtile.buf], writes=[st.db["kv_in"]], dma=True)
    lvl0 = os.environ.get("MK_A1", "full")
    if lvl0 not in ("a", "ac2"):
        S.op("gpsimd", lambda e: e.collective_compute(
            "AllGather", ALU.bypass, replica_groups=[list(range(NCORE))],
            ins=[st.q_in[:]], outs=[st.q_out[:]]),
            reads=[st.db["q_in"]], writes=[st.db["q_out"]],
            always_inc=True)
    if lvl0 not in ("a", "ac1"):
        S.op("gpsimd", lambda e: e.collective_compute(
            "AllGather", ALU.bypass, replica_groups=[list(range(NCORE))],
            ins=[st.kv_in[:]], outs=[st.kv_out[:]]),
            reads=[st.db["kv_in"]], writes=[st.db["kv_out"]],
            always_inc=True)
    if os.environ.get("MK_NOXLR", "0") == "1":
        S.barrier()
        a.pop(); f.pop()
        lvl = os.environ.get("MK_A1", "full")
        if lvl in ("acf", "full"):
            flash(st)
            S.barrier()
        if lvl == "full":
            att1_return(st)
        st.asm.pop(); a.pop(); f.pop()
        st.S.barrier()
        return
    embv = di["embT"].rearrange("(dt p) m -> p dt m", p=128)
    emb_b = [a.alloc(512, f"embb{i}") for i in range(2)]
    xrt_s = a.alloc(512, "xrt_s")
    for mc in range(XLCOLS // 512):
        eb = emb_b[mc % 2]
        p4 = pbs(st, 4, 512)
        for dt in range(8):
            ld_dram(st, "sync", eb, embv[:, dt, mc * 512:(mc + 1) * 512])
            mm(st, p4, TH(wr_b.ap[:, dt * C:(dt + 1) * C], wr_b.buf),
               eb, start=(dt == 0), stop=(dt == 7))
        copy(st, xrt_s, p4)
        S.op("gpsimd", lambda e, s=xrt_s.ap,
             o=bass.AP(st.xlrT, mc * 512, [[XLCOLS, 128], [1, 512]]):
             e.dma_start(out=o, in_=s),
             reads=[xrt_s.buf], writes=[st.db["xlrT"]], dma=True)
    S.barrier()
    a.pop(); f.pop()
    lvl = os.environ.get("MK_A1", "full")
    if lvl in ("acf", "full"):
        flash(st)
        S.barrier()
    if lvl == "full":
        att1_return(st)
    st.asm.pop(); a.pop(); f.pop()


def flash(st):
    S, a, f = st.S, st.abf, st.af32
    a.push(); f.push()
    PB, PBT = st.PB, st.PBT
    qTu = a.alloc(128, "qTu")
    qTv = a.alloc(128, "qTv")
    qTr = [a.alloc(128, f"qTr{i}") for i in range(2)]
    xlrsl = a.alloc(STRIP_MAX, "xlrsl")
    PIECE = int(os.environ.get("MK_PIECE", "5"))
    shh = [a.alloc(512 * PIECE, f"striph{i}") for i in range(2)]
    sarena = a.alloc(16 * 512, "sarena")
    NVB = int(os.environ.get("MK_NVB", "4"))
    Vb = [a.alloc(2 * V1, f"Vb{i}") for i in range(NVB)]
    kTb = [a.alloc(512, f"kTb{i}") for i in range(2)]
    if os.environ.get("MK_ONEPCH", "0") == "1":
        pch = a.alloc(512, "Pch")
        scb = [pch, pch]
        Pe = [pch, pch]
    else:
        scb = [a.alloc(512, f"scb{i}") for i in range(2)]
        Pe = [a.alloc(512, f"Pe{i}") for i in range(2)]
    PTb = [a.alloc(512, f"PTb{i}") for i in range(2)]
    umask = st.af32.alloc(512, "umask")
    thet = st.asm.alloc(1, "thet")
    wvout = a.alloc(V1, "wvout")
    runm = st.asm.alloc(1, "runm")
    negm = st.asm.alloc(1, "negm")
    cmax = st.asm.alloc(1, "cmax")
    lsum = st.asm.alloc(1, "lsum")
    lpart = st.asm.alloc(1, "lpart")
    rl = st.asm.alloc(1, "rl")
    sdb = [st.db["strip0"], st.db["strip1"]]

    def build_strip(r):
        rp = r % 8
        pitch = STRIP_PITCH[rp]
        slot = r % 2
        nmc = pitch // 512
        q = qTr[r % 2]
        fetch_q(st, r, q)
        ts(st, qTv, q, st.sb["l1a_xlv"], ALU.add)
        S.op("gpsimd", lambda e, d=xlrsl.ap[:, 0:pitch], r_=r:
             _reg_dma(e, st, d, st.xlrT, "offxlr", r_,
                      [[XLCOLS, 128], [1, pitch]]),
             reads=[st.db["xlrT"], st.sb["offxlr"].buf],
             writes=[xlrsl.buf], dma=True, dsem="xl")
        for pc0 in range(0, nmc, PIECE):
            pc1 = min(nmc, pc0 + PIECE)
            sbh = shh[(pc0 // PIECE) % 2]
            for mc in range(pc0, pc1):
                p5 = pbs(st, 5, 512)
                mm(st, p5, qTv,
                   TH(xlrsl.ap[:, mc * 512:(mc + 1) * 512], xlrsl.buf),
                   True, True)
                copy(st, TH(sbh.ap[:, (mc - pc0) * 512:(mc - pc0 + 1) * 512],
                            sbh.buf), p5)
            S.op("gpsimd", lambda e, s=sbh.ap[:, 0:(pc1 - pc0) * 512],
                 o=bass.AP(st.stripd, slot * 128 * STRIP_MAX + pc0 * 512,
                           [[pitch, 128], [1, (pc1 - pc0) * 512]]):
                 e.dma_start(out=o, in_=s),
                 reads=[sbh.buf], writes=[sdb[slot]], dma=True, dsem="sw")

    def v_load(r, kh):
        b = 0 if r < 8 else 1
        g0 = kh * 256
        rank = (g0 // 1024) if b == 0 else (7 - g0 // 1024)
        vin = (rank * KVL + KV_VOFF + b * (1024 * V1) + (g0 % 1024) * V1)
        vb = Vb[kh % NVB]
        S.op("gpsimd",
             lambda e, d=vb.ap.rearrange("p (s w) -> p s w", w=V1),
             o=vin:
             e.dma_start(out=d, in_=bass.AP(
                 st.kv_out, o, [[V1, 128], [128 * V1, 2], [1, V1]])),
             reads=[st.db["kv_out"]], writes=[vb.buf], dma=True)

    def k_load(r, kc):
        b = 0 if r < 8 else 1
        g0 = kc * 512
        rank = (g0 // 1024) if b == 0 else (7 - g0 // 1024)
        kin = rank * KVL + b * 131072 + (g0 % 1024)
        kb = kTb[kc % 2]
        S.op(os.environ.get("MK_KTQ", "sync"), lambda e, d=kb.ap, o=kin:
             e.dma_start(out=d, in_=bass.AP(
                 st.kv_out, o, [[1024, 128], [1, 512]])),
             reads=[st.db["kv_out"]], writes=[kb.buf], dma=True, dsem="kt")

    def strip_read(slot, pitch, kc):
        sc = scb[kc % 2]
        S.op("scalar", lambda e, d=sc.ap, o=(
            slot * 128 * STRIP_MAX + kc * 512 + 127), pp=pitch:
            e.dma_start(out=d, in_=bass.AP(
                st.stripd, o, [[pp - 1, 128], [1, 512]])),
            reads=[sdb[slot]], writes=[sc.buf], dma=True, dsem="sr")

    build_strip(0)
    for r in range(16):
        rp = r % 8
        b = 0 if r < 8 else 1
        pitch = STRIP_PITCH[rp]
        slot = r % 2
        W512 = 2 * rp + 2
        KH = 2 * W512
        VDEPTH = int(os.environ.get("MK_VDEPTH", "4"))
        KTPRE = os.environ.get("MK_KTPRE", "1") == "1"
        for kh in range(min(VDEPTH, KH)):
            v_load(r, kh)
        ts(st, qTu, qTr[r % 2], st.sb["l1a_xlu"], ALU.add)
        memset(st, runm, -3.0e38)
        SCBPRE = os.environ.get("MK_SCBPRE", "1") == "1"
        if KTPRE:
            k_load(r, 0)
        if SCBPRE:
            strip_read(slot, pitch, 0)
        for kc in range(W512):
            if KTPRE:
                if kc + 1 < W512:
                    k_load(r, kc + 1)
            else:
                k_load(r, kc)
            if SCBPRE and kc + 1 < W512:
                strip_read(slot, pitch, kc + 1)
            if not SCBPRE:
                strip_read(slot, pitch, kc)
            p4 = pbs(st, 4, 512)
            mm(st, p4, qTu, kTb[kc % 2], True, True)
            sdst = TH(sarena.ap[:, kc * 512:(kc + 1) * 512], sarena.buf)
            stt(st, sdst, p4, np.float32(1.0), ALU.mult, ALU.add, scb[kc % 2])
            if kc >= W512 - 2:
                ts(st, thet, TH(st.sb["gi"].ap[:, r:r + 1], st.sb["gi"].buf),
                   float(-kc * 512), ALU.add)
                ts(st, umask, st.sb["iota512"], thet, ALU.subtract,
                   np.float32(0.0), ALU.max)
                stt(st, sdst, umask, NEG30, ALU.mult, ALU.add, sdst)
            red(st, cmax, sdst, ALU.max)
            tt(st, runm, runm, cmax, ALU.max)
        if r < 15:
            build_strip(r + 1)
        ts(st, negm, runm, np.float32(-1.0), ALU.mult)
        for kh in range(KH):
            vb = Vb[kh % NVB]
            if VDEPTH == 0:
                v_load(r, kh)
            PEPAR = 1 if os.environ.get("MK_PEPAR", "1") == "1" else 2
            if kh % 2 == 0:
                kc = kh // 2
                pe = Pe[(kc % 2) // PEPAR]
                ssrc = TH(sarena.ap[:, kc * 512:(kc + 1) * 512], sarena.buf)
                act(st, pe, ssrc, AF.Exp, bias=negm, accum=lpart)
                if kc == 0:
                    copy(st, lsum, lpart)
                else:
                    tt(st, lsum, lsum, lpart, ALU.add)
                for kt in range(4):
                    tpose(st, pbts(st, (kc % 2) // PEPAR, 128, kt * 128),
                          TH(pe.ap[:, kt * 128:(kt + 1) * 128], pe.buf))
                pp_ = (kc % 2) // PEPAR
                copy(st, PTb[pp_],
                     TH(PBT[pp_].ap[:, 0:512], PBT[pp_].buf))
            ptb = PTb[((kh // 2) % 2) // PEPAR]
            for kt2 in range(2):
                kt = (kh % 2) * 2 + kt2
                lt = TH(ptb.ap[:, kt * 128:(kt + 1) * 128], ptb.buf)
                for vc in range(4):
                    mm(st, pbs(st, vc, 512), lt,
                       TH(vb.ap[:, kt2 * V1 + vc * 512:
                                kt2 * V1 + (vc + 1) * 512], vb.buf),
                       start=(kh == 0 and kt2 == 0),
                       stop=(kh == KH - 1 and kt2 == 1))
            if VDEPTH > 0 and kh + VDEPTH < KH:
                v_load(r, kh + VDEPTH)
        recip(st, rl, lsum)
        for vc in range(4):
            ts(st, TH(wvout.ap[:, vc * 512:(vc + 1) * 512], wvout.buf),
               pbs(st, vc, 512), rl, ALU.mult)
        st_dram(st, "gpsimd", st.wv_in[r, :, :], wvout, st.db["wv_in"],
                dsem="wv")
    S.op("gpsimd", lambda e: e.collective_compute(
        "AllGather", ALU.bypass, replica_groups=[list(range(NCORE))],
        ins=[st.wv_in[:]], outs=[st.wv_out[:]]),
        reads=[st.db["wv_in"]], writes=[st.db["wv_out"]], always_inc=True)
    st.asm.pos -= 7
    st.af32.pos -= 512
    a.pop(); f.pop()


def att1_return(st):
    S, a, f = st.S, st.abf, st.af32
    di = st.di
    PB, PBT = st.PB, st.PBT
    a.push(); f.push()
    xfm = a.alloc(8 * TPC, "xfm_r")
    ld_dram(st, "sync", xfm, st.xfm_sp[:, :], extra_reads=[st.db["xfm_sp"]])
    wgr = a.alloc(8 * V1, "wgr")
    wf = f.alloc(V1, "wgf")
    wgv = di["l1a_Wg"].rearrange("(dt p) w -> p dt w", p=128)
    for dt in range(8):
        ld_dram(st, "sync", wf, wgv[:, dt, :])
        copy(st, TH(wgr.ap[:, dt * V1:(dt + 1) * V1], wgr.buf), wf)
    f.pos -= V1
    wvt = a.alloc(V1, "wvt")
    sgt = a.alloc(V1, "sgt")
    gT = a.alloc(1024, "gT")
    for tt_ in range(NTILE):
        S.op("scalar", lambda e, d=wvt.ap, t=tt_:
             _reg_dma(e, st, d, st.wv_out, "offwv", t, [[V1, 128], [1, V1]]),
             reads=[st.db["wv_out"], st.sb["offwv"].buf], writes=[wvt.buf],
             dma=True)
        xsl = TH(xfm.ap.rearrange("p (dt q) -> p dt q", q=TPC)
                 [:, :, tt_ * 128:(tt_ + 1) * 128], xfm.buf)
        for vc in range(4):
            pbk = pbs(st, 4 + (vc % 2), 512)
            for dt in range(8):
                mm(st, pbk, TH(xsl.ap[:, dt, :], xsl.buf),
                   TH(wgr.ap[:, dt * V1 + vc * 512:dt * V1 + (vc + 1) * 512],
                      wgr.buf),
                   start=(dt == 0), stop=(dt == 7))
            act(st, TH(sgt.ap[:, vc * 512:(vc + 1) * 512], sgt.buf), pbk,
                AF.Silu)
        tt(st, sgt, wvt, sgt, ALU.mult)
        for half in range(2):
            for vh in range(8):
                vt = half * 8 + vh
                tpose(st, pbts(st, 0, 128, vh * 128),
                      TH(sgt.ap[:, vt * 128:(vt + 1) * 128], sgt.buf))
            copy(st, gT, TH(PBT[0].ap[:, 0:1024], PBT[0].buf))
            st_dram(st, "gpsimd",
                    st.wv_in[tt_, :, half * 1024:(half + 1) * 1024], gT,
                    st.db["wv_in"])
    S.barrier()
    a.pop(); f.pop()
    a.push(); f.push()
    wor = a.alloc(16 * D, "wor")
    wf = f.alloc(D, "wof")
    wov = di["l1a_Wo"].rearrange("(vt p) w -> p vt w", p=128)
    for vt in range(16):
        ld_dram(st, "sync", wf, wov[:, vt, :])
        copy(st, TH(wor.ap[:, vt * D:(vt + 1) * D], wor.buf), wf)
    f.pos -= D
    gtile = a.alloc(V1, "gtile")
    for tt_ in range(NTILE):
        ld_dram(st, "sync", gtile, st.wv_in[tt_, :, :],
                extra_reads=[st.db["wv_in"]])
        for dc in range(2):
            pbk = pbs(st, dc, 512)
            for vt in range(16):
                mm(st, pbk,
                   TH(gtile.ap[:, vt * 128:(vt + 1) * 128], gtile.buf),
                   TH(wor.ap[:, vt * D + dc * 512:vt * D + (dc + 1) * 512],
                      wor.buf),
                   start=(vt == 0), stop=(vt == 15))
            xd = st.xt[tt_][:, dc * 512:(dc + 1) * 512]
            tt(st, xd, pbk, xd, ALU.add)
    a.pop(); f.pop()



def build_nc():
    nc = bass.Bass("TRN2", target_bir_lowering=False,
                   detect_race_conditions=False)
    st = declare_io(nc)
    S = Sched()
    st.S = S
    ctx = contextlib.ExitStack()
    with ctx:
        _build_body(nc, st, ctx)
        S.emit(nc, ctx)
    return nc


def _build_body(nc, st, ctx):
    di = st.di
    af32 = Arena(ctx.enter_context(
        nc.sbuf_tensor("af32", [128, 25856], F32)), 25856, "af32")
    abf = Arena(ctx.enter_context(
        nc.sbuf_tensor("abf", [128, 52480], BF16)), 52480, "abf")
    asm = Arena(ctx.enter_context(
        nc.sbuf_tensor("asm", [128, 512], F32)), 512, "asm")
    ai = Arena(ctx.enter_context(
        nc.sbuf_tensor("ai32", [1, 512], I32)), 512, "ai")
    st.af32, st.abf, st.asm, st.ai = af32, abf, asm, ai

    X = af32.alloc(NTILE * D, "X")
    st.X = X
    st.xt = [X[:, tt_ * D:(tt_ + 1) * D] for tt_ in range(NTILE)]

    pb = [ctx.enter_context(nc.psum_tensor(f"pb{i}", [128, 512], F32))
          for i in range(6)]
    pbt = [ctx.enter_context(nc.psum_tensor(f"pbt{i}", [128, 1024], BF16))
           for i in range(2)]
    st.PB = [TH(p[:, :], Buf(f"pb{i}")) for i, p in enumerate(pb)]
    st.PBT = [TH(p[:, :], Buf(f"pbt{i}")) for i, p in enumerate(pbt)]

    sb = {}
    sb["ident_bf"] = abf.alloc(128, "ident_bf")
    sb["utri"] = af32.alloc(128, "utri")
    sb["neg_utri"] = af32.alloc(128, "neg_utri")
    sb["stril"] = af32.alloc(128, "stril")
    sb["sel"] = af32.alloc(8, "sel")
    sb["ones_row"] = af32.row(512, "ones_row")
    for nm, wdt in [("l0a_bq", C), ("l0a_bk", C), ("l0a_bv", C),
                    ("l0a_bg", C), ("l0a_bab", 2), ("l0a_bo", D)]:
        sb[nm] = af32.row(wdt, nm)
    sb["l1a_xlu"] = af32.alloc(1, "xlu")
    sb["l1a_xlv"] = af32.alloc(1, "xlv")
    for nm, nn in [("offq", 16), ("offwv", 16), ("offxlr", 16),
                   ("offpad", 16)]:
        sb[nm] = TH(ai.t[0:1, ai.pos:ai.pos + nn], Buf(nm))
        ai.pos += nn
    sb["gi"] = af32.alloc(16, "gi")
    sb["iota512"] = af32.alloc(512, "iota512")
    st.sb = sb
    st.junk = abf.alloc(1024, "junk")
    st.rowstage = af32.row(512, "rowstage")
    st.wsc = af32.alloc(12, "wsc")
    st.magic_col = af32.alloc(1, "magic_col")
    memset(st, st.magic_col, MAGIC)

    S = st.S
    for name in ["ident_bf", "utri", "neg_utri", "stril", "sel",
                 "l1a_xlu", "l1a_xlv", "gi", "iota512"]:
        src = di[name]
        dst = sb[name]
        S.op("sync", lambda e, d=dst.ap, s=src[:]: e.dma_start(out=d, in_=s),
             writes=[dst.buf], dma=True)
    for name in ["ones_row", "l0a_bq", "l0a_bk", "l0a_bv", "l0a_bg",
                 "l0a_bab", "l0a_bo"]:
        src = di[name]
        dst = sb[name]
        S.op("sync", lambda e, d=dst.ap, s=src[0:1, :]:
             e.dma_start(out=d, in_=s), writes=[dst.buf], dma=True)
    for name in ["offq", "offwv", "offxlr", "offpad"]:
        src = di[name]
        dst = sb[name]
        S.op("sync", lambda e, d=dst.ap, s=src[0:1, :]:
             e.dma_start(out=d, in_=s), writes=[dst.buf], dma=True)

    xv = di["x_shard"].rearrange("(t p) d -> p t d", p=128)
    S.op("sync", lambda e: e.dma_start(
        out=af32.t[:, 0:NTILE * D].rearrange("p (t d) -> p t d", d=D),
        in_=xv[:, :, :]), writes=[X.buf], dma=True)

    S.barrier()
    if "ffn0a" in PHASES:
        ffn0_phase(st, first=True)
    if "att0" in PHASES:
        att0_phase(st)
    if "ffn0b" in PHASES:
        ffn0_phase(st, first=False)
    if "ffn1a" in PHASES:
        ffn1_phase(st, first=True)
    if "att1" in PHASES:
        att1_phase(st)
    if "ffn1b" in PHASES:
        ffn1_phase(st, first=False)

    yv = st.y.rearrange("(t p) d -> p t d", p=128)
    S.op("gpsimd", lambda e: e.dma_start(
        out=yv[:, :, :],
        in_=af32.t[:, 0:NTILE * D].rearrange("p (t d) -> p t d", d=D)),
        reads=[X.buf], writes=[st.db["y"]], dma=True)
    S.barrier()


def kernel(**inputs):
    in_maps = _shard_inputs(inputs)
    if "prog" not in _NC_CACHE:
        _NC_CACHE["prog"] = build_nc()
    nc = _NC_CACHE["prog"]
    res = run_bass_kernel_spmd(nc, in_maps, core_ids=list(range(NCORE)))
    return _unshard_output(res.results)


if __name__ == "__main__":
    nc = build_nc()
    print("build ok")


# version 06
# speedup vs baseline: 1.0548x; 1.0548x over previous
import os
import sys
import math
import contextlib

sys.path.insert(0, "/opt/trn_rl_repo")

import numpy as np
import ml_dtypes

import concourse.bass as bass
import concourse.mybir as mybir
from concourse.bass_utils import run_bass_kernel_spmd

F32 = mybir.dt.float32
BF16 = mybir.dt.bfloat16
I32 = mybir.dt.int32
AF = mybir.ActivationFunctionType
ALU = mybir.AluOpType
BF = ml_dtypes.bfloat16

B, T, D = 2, 8192, 1024
H = 4096
C = 128
V1 = 2048
NCORE = 8
TPC = 2048
NTILE = 16
EPS5 = np.float32(1e-5)
MAGIC = np.float32(1.5 * 2.0 ** 23)
NEG30 = np.float32(-1.0e30)
INV_TAU = np.float32(1.0 / math.sqrt(C))

XL_PAD = 2560
XLCOLS = T + XL_PAD
STRIP_PITCH = [(2 * rp + 2 + 3) * 512 for rp in range(8)]
STRIP_MAX = max(STRIP_PITCH)
KVL = 2 * 128 * 1024 + 2 * 1024 * V1
KV_VOFF = 2 * 128 * 1024

PHASES = os.environ.get(
    "MK_PHASES", "ffn0a,att0,ffn0b,ffn1a,att1,ffn1b").split(",")

_NC_CACHE = {}

ENGS = ("sync", "tensor", "vector", "scalar", "gpsimd")


class Buf:
    __slots__ = ("w", "rs", "name")

    def __init__(self, name=""):
        self.w = None
        self.rs = []
        self.name = name


class Rec:
    __slots__ = ("fn", "waits", "inc", "dma", "tok")

    def __init__(self, fn, waits, dma, tok):
        self.fn = fn
        self.waits = waits
        self.inc = False
        self.dma = dma
        self.tok = tok


class Sched:
    def __init__(self):
        self.recs = {e: [] for e in ENGS}
        self.ndma = {e: 0 for e in ENGS}
        self.nx = {}
        self.all = {}
        self.fencep = {e: None for e in ENGS}
        self.lastc = {e: None for e in ENGS}
        self.lastd = {e: None for e in ENGS}
        self.barp = {e: [] for e in ENGS}

    def op(self, eng, fn, reads=(), writes=(), dma=False, fence=False,
           always_inc=False, dsem=None):
        if os.environ.get("MK_FENCE_ALL", "0") == "1" and not dma:
            fence = True
        if dma and dsem is not None:
            tok = ("x", dsem, self.nx.get(dsem, 0))
            self.nx[dsem] = self.nx.get(dsem, 0) + 1
        elif dma:
            tok = ("d", eng, self.ndma[eng])
            self.ndma[eng] += 1
        else:
            tok = ("c", eng, len(self.recs[eng]))
        deps = set()
        for b in reads:
            if b.w is not None:
                deps.add(b.w)
        for b in writes:
            if b.w is not None:
                deps.add(b.w)
            for t in b.rs:
                deps.add(t)
        forced = []
        if self.barp[eng]:
            forced.extend(self.barp[eng])
            self.barp[eng] = []
        if self.fencep[eng] is not None:
            forced.append(self.fencep[eng])
            self.fencep[eng] = None
        if fence and self.lastc[eng] is not None:
            forced.append(self.lastc[eng])
        waits = []
        for t in deps:
            kind, e, i = t
            if kind == "c" and e == eng:
                continue
            waits.append(t)
            self.all[t].inc = True
        for t in forced:
            if t in waits:
                continue
            waits.append(t)
            self.all[t].inc = True
        rec = Rec(fn, waits, dma, tok)
        if always_inc:
            rec.inc = True
        self.recs[eng].append(rec)
        self.all[tok] = rec
        if dma:
            self.lastd[eng] = tok
        else:
            self.lastc[eng] = tok
        if fence:
            self.fencep[eng] = tok
        for b in reads:
            b.rs.append(tok)
        for b in writes:
            b.w = tok
            b.rs = []
        return tok

    def barrier(self):
        toks = []
        for e in ENGS:
            if self.lastc[e] is not None:
                toks.append(self.lastc[e])
            if self.lastd[e] is not None:
                toks.append(self.lastd[e])
        for e in ENGS:
            self.barp[e] = list(set(self.barp[e]) | set(toks))

    def emit(self, nc, ctx):
        sems = {}
        for e in ENGS:
            sems[("c", e)] = ctx.enter_context(nc.semaphore(f"sem_c_{e}"))
            sems[("d", e)] = ctx.enter_context(nc.semaphore(f"sem_d_{e}"))
        for nm in self.nx:
            sems[("x", nm)] = ctx.enter_context(nc.semaphore(f"sem_x_{nm}"))
        incidx = {}
        for e in ENGS:
            n = 0
            for r in self.recs[e]:
                if not r.dma:
                    if r.inc:
                        n += 1
                    incidx[r.tok] = n

        def target(t):
            kind, e, i = t
            if kind == "d":
                return sems[("d", e)], (i + 1) * 16
            if kind == "x":
                return sems[("x", e)], (i + 1) * 16
            return sems[("c", e)], incidx[t]

        with nc.Block() as block:
            for e in ENGS:
                def make(e):
                    def body(eng):
                        hw = {}
                        for r in self.recs[e]:
                            for t in sorted(r.waits):
                                sem, val = target(t)
                                kk = id(sem)
                                if hw.get(kk, -1) >= val:
                                    continue
                                hw[kk] = val
                                eng.wait_ge(sem, val)
                            if r.fn is None:
                                if r.inc:
                                    eng.nop().then_inc(sems[("c", e)], 1)
                                continue
                            h = r.fn(eng)
                            if r.dma:
                                if r.tok[0] == "x":
                                    h.then_inc(sems[("x", r.tok[1])], 16)
                                else:
                                    h.then_inc(sems[("d", e)], 16)
                            elif r.inc:
                                h.then_inc(sems[("c", e)], 1)
                    return body
                getattr(block, e)(make(e))


class TH:
    __slots__ = ("ap", "buf")

    def __init__(self, ap, buf):
        self.ap = ap
        self.buf = buf

    def __getitem__(self, sl):
        return TH(self.ap[sl], self.buf)


class Arena:
    def __init__(self, tensor, cols, name):
        self.t = tensor
        self.cols = cols
        self.pos = 0
        self.marks = []
        self.name = name
        self.recs = []

    def _mkbuf(self, n, name):
        start, end = self.pos, self.pos + n
        b = Buf(name)
        keep = []
        for (s0, e0, ob) in self.recs:
            if s0 < end and start < e0:
                if ob.w is not None:
                    b.rs.append(ob.w)
                b.rs.extend(ob.rs)
            else:
                keep.append((s0, e0, ob))
        keep.append((start, end, b))
        self.recs = keep
        self.pos = end
        return b

    def alloc(self, n, name=""):
        assert self.pos + n <= self.cols, \
            f"{self.name} overflow at {name}: {self.pos}+{n}>{self.cols}"
        start = self.pos
        b = self._mkbuf(n, name)
        return TH(self.t[:, start:start + n], b)

    def row(self, n, name=""):
        assert self.pos + n <= self.cols
        start = self.pos
        b = self._mkbuf(n, name)
        return TH(self.t[0:1, start:start + n], b)

    def push(self):
        self.marks.append(self.pos)

    def pop(self):
        self.pos = self.marks.pop()



def _host_tables():
    t = {}
    t["ident_bf"] = np.eye(128, dtype=np.float32).astype(BF)
    r = np.arange(128)
    t["utri"] = (r[:, None] <= r[None, :]).astype(np.float32)
    t["neg_utri"] = -t["utri"]
    t["stril"] = (r[:, None] > r[None, :]).astype(np.float32)
    t["ones_row"] = np.ones((1, 512), np.float32)
    t["neg30"] = np.full((128, 1152), NEG30, np.float32).astype(BF)
    t["iota512"] = np.broadcast_to(np.arange(512, dtype=np.float32),
                                   (128, 512)).copy()
    pos = np.arange(T, dtype=np.float32)
    inv_lams = 1.0 / 10000.0 ** (np.arange(0, D, 2, dtype=np.float32) / D)
    pre = pos[:, None] * inv_lams[None, :]
    emb = np.concatenate([np.sin(pre), np.cos(pre)], axis=-1)
    emb = np.flip(emb, axis=0)
    embT = np.zeros((D, XLCOLS), np.float32)
    embT[:, :T] = emb.T
    t["embT"] = embT.astype(BF)
    return t


def _host_percore(c):
    d = {}
    onehot = np.zeros((128, 8), np.float32)
    onehot[:, c] = 1.0
    d["sel"] = onehot
    offq, offwv, offxlr, offpad = [], [], [], []
    CONTRIBQ, TILEQ = 16 * 128 * 128, 128 * 128
    CONTRIBW, TILEW = 16 * 128 * V1, 128 * V1
    for r in range(16):
        rp = r % 8
        if r < 8:
            offq.append(r * CONTRIBQ + c * TILEQ)
        else:
            u = r - 8
            offq.append((7 - u) * CONTRIBQ + (8 + c) * TILEQ)
        I0 = (8 * rp + c) * 128
        offxlr.append((T - 1) - (I0 + 127))
        offpad.append(0)
    for j in range(8):
        offwv.append(j * CONTRIBW + c * TILEW)
    for j in range(8):
        offwv.append(j * CONTRIBW + (8 + (7 - c)) * TILEW)
    d["offq"] = np.array(offq, np.int32).reshape(1, 16)
    d["offwv"] = np.array(offwv, np.int32).reshape(1, 16)
    d["offxlr"] = np.array(offxlr, np.int32).reshape(1, 16)
    d["offpad"] = np.array(offpad, np.int32).reshape(1, 16)
    gi = np.zeros((128, 16), np.float32)
    for r in range(16):
        I0 = (8 * (r % 8) + c) * 128
        gi[:, r] = I0 + np.arange(128)
    d["gi"] = gi
    return d


def _shard_inputs(inp):
    tabs = _host_tables()
    x = np.ascontiguousarray(inp["L0_attn_x"].astype(np.float32))
    w = {}
    w["l0f_w1T"] = inp["L0_ffn_w1"].T
    w["l0f_w2T"] = inp["L0_ffn_w2"].T
    w["l0f_b1"] = inp["L0_ffn_b1"].reshape(1, -1)
    w["l0f_b2"] = inp["L0_ffn_b2"].reshape(1, -1)
    w["l0f_ln1w"] = inp["L0_ffn_ln1_w"].reshape(1, -1)
    w["l0f_ln1b"] = inp["L0_ffn_ln1_b"].reshape(1, -1)
    w["l0f_ln2w"] = inp["L0_ffn_ln2_w"].reshape(1, -1)
    w["l0f_ln2b"] = inp["L0_ffn_ln2_b"].reshape(1, -1)
    w["l0a_WqT"] = inp["L0_attn_Wq"].T
    w["l0a_WkT"] = inp["L0_attn_Wk"].T
    w["l0a_WvT"] = inp["L0_attn_Wv"].T
    w["l0a_WgT"] = inp["L0_attn_Wg"].T
    w["l0a_WabT"] = np.concatenate([inp["L0_attn_Wa"], inp["L0_attn_Wb"]], 0).T
    w["l0a_WoT"] = inp["L0_attn_Wo"].T
    w["l0a_bq"] = inp["L0_attn_bq"].reshape(1, -1)
    w["l0a_bk"] = inp["L0_attn_bk"].reshape(1, -1)
    w["l0a_bv"] = inp["L0_attn_bv"].reshape(1, -1)
    w["l0a_bg"] = inp["L0_attn_bg"].reshape(1, -1)
    w["l0a_bab"] = np.concatenate(
        [inp["L0_attn_ba"], inp["L0_attn_bb"]]).reshape(1, 2)
    w["l0a_bo"] = inp["L0_attn_bo"].reshape(1, -1)
    w["l1f_wgT"] = inp["L1_ffn_w_gate"].T
    w["l1f_wuT"] = inp["L1_ffn_w_up"].T
    w["l1f_wdT"] = inp["L1_ffn_w_down"].T
    w["l1a_gln"] = inp["L1_attn_g_ln"].reshape(1, -1)
    w["l1a_Wq"] = inp["L1_attn_W_q"]
    w["l1a_Wk"] = inp["L1_attn_W_k"]
    w["l1a_Wv"] = inp["L1_attn_W_v"]
    w["l1a_Wg"] = inp["L1_attn_W_g"]
    w["l1a_Wr"] = inp["L1_attn_W_r"]
    w["l1a_xlu"] = inp["L1_attn_xl_u"].reshape(-1, 1)
    w["l1a_xlv"] = inp["L1_attn_xl_v"].reshape(-1, 1)
    w["l1a_Wo"] = inp["L1_attn_W_o"]
    in_maps = []
    for c in range(NCORE):
        m = {}
        b0 = x[0, 1024 * c:1024 * (c + 1)]
        b1 = x[1, 1024 * (7 - c):1024 * (8 - c)]
        m["x_shard"] = np.ascontiguousarray(np.concatenate([b0, b1], 0))
        for k2, v in w.items():
            m[k2] = np.ascontiguousarray(v)
        for k2, v in tabs.items():
            m[k2] = np.ascontiguousarray(v)
        m.update(_host_percore(c))
        in_maps.append(m)
    return in_maps


def _unshard_output(results):
    y = np.zeros((B, T, D), np.float32)
    for c in range(NCORE):
        ys = results[c]["y_shard"]
        y[0, 1024 * c:1024 * (c + 1)] = ys[:1024]
        y[1, 1024 * (7 - c):1024 * (8 - c)] = ys[1024:]
    return y



class St:
    pass


def declare_io(nc):
    st = St()
    st.nc = nc
    di = {}

    def inp(name, shape, dt=F32):
        di[name] = nc.dram_tensor(name, list(shape), dt, kind="ExternalInput")

    inp("x_shard", (TPC, D))
    inp("l0f_w1T", (D, H)); inp("l0f_w2T", (H, D))
    inp("l0f_b1", (1, H)); inp("l0f_b2", (1, D))
    inp("l0f_ln1w", (1, D)); inp("l0f_ln1b", (1, D))
    inp("l0f_ln2w", (1, H)); inp("l0f_ln2b", (1, H))
    inp("l0a_WqT", (D, C)); inp("l0a_WkT", (D, C)); inp("l0a_WvT", (D, C))
    inp("l0a_WgT", (D, C)); inp("l0a_WabT", (D, 2)); inp("l0a_WoT", (C, D))
    inp("l0a_bq", (1, C)); inp("l0a_bk", (1, C)); inp("l0a_bv", (1, C))
    inp("l0a_bg", (1, C)); inp("l0a_bab", (1, 2)); inp("l0a_bo", (1, D))
    inp("l1f_wgT", (D, H)); inp("l1f_wuT", (D, H)); inp("l1f_wdT", (H, D))
    inp("l1a_gln", (1, D))
    inp("l1a_Wq", (D, C)); inp("l1a_Wk", (D, C))
    inp("l1a_Wv", (D, V1)); inp("l1a_Wg", (D, V1)); inp("l1a_Wr", (D, C))
    inp("l1a_xlu", (C, 1)); inp("l1a_xlv", (C, 1)); inp("l1a_Wo", (V1, D))
    inp("ident_bf", (128, 128), BF16)
    inp("utri", (128, 128)); inp("neg_utri", (128, 128))
    inp("stril", (128, 128))
    inp("ones_row", (1, 512)); inp("neg30", (128, 1152), BF16)
    inp("embT", (D, XLCOLS), BF16)
    inp("sel", (128, 8))
    inp("offq", (1, 16), I32); inp("offwv", (1, 16), I32)
    inp("offxlr", (1, 16), I32); inp("offpad", (1, 16), I32)
    inp("gi", (128, 16)); inp("iota512", (128, 512))
    st.di = di
    st.y = nc.dram_tensor("y_shard", [TPC, D], F32, kind="ExternalOutput")
    if os.environ.get("MK_DBG", "0") == "1":
        st.dbg1 = nc.dram_tensor("dbg1", [128, 4096], F32,
                                 kind="ExternalOutput")
        st.dbg2 = nc.dram_tensor("dbg2", [128, 4096], F32,
                                 kind="ExternalOutput")
        st.dbg3 = nc.dram_tensor("dbg3", [128, 64], F32,
                                 kind="ExternalOutput")

    st.wq1 = nc.dram_tensor("wq1", [D, H], BF16)
    st.wq2 = nc.dram_tensor("wq2", [H, D], BF16)
    st.wqg = nc.dram_tensor("wqg", [D, H], BF16)
    st.wqu = nc.dram_tensor("wqu", [D, H], BF16)
    st.wqd = nc.dram_tensor("wqd", [H, D], BF16)
    st.xlrT = nc.dram_tensor("xlrT", [C * XLCOLS], BF16)
    st.stripd = nc.dram_tensor("stripd", [2 * 128 * STRIP_MAX], BF16)
    st.xfm_sp = nc.dram_tensor("xfm_sp", [128, 8 * TPC], BF16)
    st.st_in = nc.dram_tensor("st_in", [4, 128, 128], F32)
    st.st_out = nc.dram_tensor("st_out", [NCORE, 4, 128, 128], F32)
    st.q_in = nc.dram_tensor("q_in", [16, 128, 128], BF16)
    st.q_out = nc.dram_tensor("q_out", [NCORE * 16 * 128 * 128], BF16)
    st.kv_in = nc.dram_tensor("kv_in", [KVL], BF16)
    st.kv_out = nc.dram_tensor("kv_out", [NCORE * KVL], BF16)
    st.wv_in = nc.dram_tensor("wv_in", [16, 128, V1], BF16)
    st.wv_out = nc.dram_tensor("wv_out", [NCORE * 16 * 128 * V1], BF16)
    st.db = {nm: Buf(nm) for nm in
             ["wq1", "wq2", "wqg", "wqu", "wqd", "xlrT", "strip0", "strip1",
              "st_in", "st_out", "q_in", "q_out", "kv_in", "kv_out", "wv_in",
              "wv_out", "y", "xfm_sp"]}
    return st



def _fd(ap):
    try:
        dims = list(ap.ap)
    except Exception:
        return 10 ** 9
    n = 1
    for st_, cnt in dims[1:]:
        n *= cnt
    if len(dims) == 1:
        n = dims[0][1]
    return n


def _small(out_ap):
    return _fd(out_ap) <= 32


def ld_dram(st, q, dst_th, src_ap, extra_reads=()):
    st.S.op(q, lambda e, d=dst_th.ap, s=src_ap: e.dma_start(out=d, in_=s),
            reads=list(extra_reads), writes=[dst_th.buf], dma=True)


def st_dram(st, q, dst_ap, src_th, dbuf, extra_reads=(), dsem=None):
    st.S.op(q, lambda e, d=dst_ap, s=src_th.ap: e.dma_start(out=d, in_=s),
            reads=[src_th.buf] + list(extra_reads), writes=[dbuf], dma=True,
            dsem=dsem)


def mm(st, out_th, lhsT_th, rhs_th, start, stop):
    st.S.op("tensor",
            lambda e, o=out_th.ap, l=lhsT_th.ap, r=rhs_th.ap:
            e.matmul(o, l, r, start=start, stop=stop),
            reads=[lhsT_th.buf, rhs_th.buf], writes=[out_th.buf])


def tpose(st, out_th, in_th):
    idb = st.sb["ident_bf"]
    st.S.op("tensor",
            lambda e, o=out_th.ap, i=in_th.ap, d=idb.ap:
            e.matmul(o, i, d, start=True, stop=True, is_transpose=True),
            reads=[in_th.buf, idb.buf], writes=[out_th.buf])


def ts(st, out_th, in_th, s1, op0, s2=None, op1=ALU.bypass, accum=None):
    r = [in_th.buf]
    a1 = s1.ap if isinstance(s1, TH) else (None if s1 is None else float(s1))
    if isinstance(s1, TH):
        r.append(s1.buf)
    a2 = s2.ap if isinstance(s2, TH) else (None if s2 is None else float(s2))
    if isinstance(s2, TH):
        r.append(s2.buf)
    wr = [out_th.buf]
    aac = None
    if accum is not None:
        aac = accum.ap
        wr.append(accum.buf)
    st.S.op("vector",
            lambda e, o=out_th.ap, i=in_th.ap, x1=a1, x2=a2, ac=aac:
            e.tensor_scalar(o, i, x1, x2, op0, op1, accum_out=ac),
            reads=r, writes=wr, fence=_small(out_th.ap) or aac is not None)


def tt(st, out_th, a_th, b_th, op):
    st.S.op("vector",
            lambda e, o=out_th.ap, a=a_th.ap, b=b_th.ap:
            e.tensor_tensor(o, a, b, op),
            reads=[a_th.buf, b_th.buf], writes=[out_th.buf],
            fence=_small(out_th.ap))


def stt(st, out_th, in0_th, s, op0, op1, in1_th):
    r = [in0_th.buf, in1_th.buf]
    a = s.ap if isinstance(s, TH) else float(s)
    if isinstance(s, TH):
        r.append(s.buf)
    st.S.op("vector",
            lambda e, o=out_th.ap, i0=in0_th.ap, x=a, i1=in1_th.ap:
            e.scalar_tensor_tensor(o, i0, x, i1, op0, op1),
            reads=r, writes=[out_th.buf])


def act(st, out_th, in_th, func, bias=0.0, scale=1.0, accum=None):
    r = [in_th.buf]
    ab = bias.ap if isinstance(bias, TH) else float(bias)
    if isinstance(bias, TH):
        r.append(bias.buf)
    asc = scale.ap if isinstance(scale, TH) else float(scale)
    if isinstance(scale, TH):
        r.append(scale.buf)
    wr = [out_th.buf]
    aac = None
    if accum is not None:
        aac = accum.ap
        wr.append(accum.buf)
    st.S.op("scalar",
            lambda e, o=out_th.ap, i=in_th.ap, bb=ab, ss=asc, ac=aac:
            e.activation(o, i, func, bias=bb, scale=ss, accum_out=ac),
            reads=r, writes=wr,
            fence=_small(out_th.ap) or aac is not None)
    if accum is not None:
        st.S.op("scalar",
                lambda e, o=accum.ap: e.mul(o, o, 1.0),
                reads=[accum.buf], writes=[accum.buf], fence=True)


def copy(st, out_th, in_th):
    st.S.op("vector", lambda e, o=out_th.ap, i=in_th.ap: e.tensor_copy(o, i),
            reads=[in_th.buf], writes=[out_th.buf],
            fence=_small(out_th.ap))


def red(st, out_th, in_th, op, absval=None):
    st.S.op("vector",
            lambda e, o=out_th.ap, i=in_th.ap:
            e.tensor_reduce(o, i, mybir.AxisListType.X, op,
                            apply_absolute_value=absval),
            reads=[in_th.buf], writes=[out_th.buf], fence=True)


def recip(st, out_th, in_th):
    act(st, out_th, in_th, AF.Ln)
    act(st, out_th, out_th, AF.Exp, scale=-1.0)


def memset(st, th, val):
    st.S.op("vector", lambda e, o=th.ap, v=float(val): e.memset(o, v),
            reads=[], writes=[th.buf], fence=_small(th.ap))


def pbs(st, i, n, off=0):
    p = st.PB[i]
    return TH(p.ap[:, off:off + n], p.buf)


def pbts(st, i, n, off=0):
    p = st.PBT[i]
    return TH(p.ap[:, off:off + n], p.buf)


def bcast_dram_row(st, name, width, out_th):
    lvl = int(os.environ.get("MK_BCAST_LVL", "3"))
    ones = st.sb["ones_row"]
    stage = st.rowstage
    src = st.di[name]
    for i in range((width + 511) // 512):
        w = min(512, width - i * 512)
        if lvl >= 1:
            ld_dram(st, "sync", TH(stage.ap[0:1, 0:w], stage.buf),
                    src[0:1, i * 512:i * 512 + w])
        p = pbs(st, 5, w)
        if lvl >= 2:
            st.S.op("tensor",
                    lambda e, o=p.ap, l=ones.ap[0:1, 0:128],
                    r=stage.ap[0:1, 0:w]:
                    e.matmul(o, l, r, start=True, stop=True),
                    reads=[ones.buf, stage.buf], writes=[p.buf])
        if lvl >= 3:
            st.S.op("scalar",
                    lambda e, o=out_th.ap[:, i * 512:i * 512 + w], ip=p.ap:
                    e.copy(o, ip),
                    reads=[p.buf], writes=[out_th.buf])


def bias_mm(st, p_th, row_th, n, row_off=0, transpose_roles=False):
    ones = st.sb["ones_row"]
    if transpose_roles:
        st.S.op("tensor",
                lambda e, o=p_th.ap, l=row_th.ap[0:1, row_off:row_off + 128],
                r=ones.ap[0:1, 0:n]:
                e.matmul(o, l, r, start=False, stop=True),
                reads=[row_th.buf, ones.buf], writes=[p_th.buf])
    else:
        st.S.op("tensor",
                lambda e, o=p_th.ap, l=ones.ap[0:1, 0:128],
                r=row_th.ap[0:1, row_off:row_off + n]:
                e.matmul(o, l, r, start=False, stop=True),
                reads=[row_th.buf, ones.buf], writes=[p_th.buf])


def batch_rstd(st, var_th, n, eps, out_th, scale_pre=1.0):
    tmp = st.asm.alloc(n, "rstd_tmp")
    ts(st, tmp, var_th, np.float32(scale_pre), ALU.mult, np.float32(eps),
       ALU.add)
    act(st, tmp, tmp, AF.Ln)
    act(st, out_th, tmp, AF.Exp, scale=-0.5)


def stats_all(st, width, tiles, mean_th, rstd_th, eps):
    n = len(tiles)
    sx = st.asm.alloc(n, "sx")
    sxx = st.asm.alloc(n, "sxx")
    junk = TH(st.junk.ap[:, 0:width], st.junk.buf)
    for i, t in enumerate(tiles):
        red(st, sx[:, i:i + 1], t, ALU.add)
        act(st, junk, t, AF.Square, accum=sxx[:, i:i + 1])
    ts(st, mean_th, sx, np.float32(1.0 / width), ALU.mult)
    ex2 = st.asm.alloc(n, "ex2")
    ts(st, ex2, sxx, np.float32(1.0 / width), ALU.mult)
    m2 = st.asm.alloc(n, "m2")
    tt(st, m2, mean_th, mean_th, ALU.mult)
    tt(st, ex2, ex2, m2, ALU.subtract)
    batch_rstd(st, ex2, n, eps, rstd_th)
    st.asm.pos -= 4 * n


def ffn_common_start(st):
    st.mean = st.asm.alloc(16, "mean")
    st.rstd = st.asm.alloc(16, "rstd")
    if os.environ.get("MK_STATS", "1") == "1":
        stats_all(st, D, st.xt, st.mean, st.rstd, 1e-5)
    else:
        memset(st, st.mean, 0.0)
        memset(st, st.rstd, 1.0)


def quant_chain(st, src_th, width, xq_th, isx_col, tscr):
    mx = st.asm.alloc(1, "mx")
    red(st, mx, src_th, ALU.max, absval=True)
    ts(st, mx, mx, np.float32(EPS5), ALU.max)
    sxc = st.asm.alloc(1, "sxc")
    recip(st, sxc, mx)
    ts(st, sxc, sxc, np.float32(127.0), ALU.mult)
    ts(st, isx_col, mx, np.float32(1.0 / 127.0), ALU.mult)
    t = TH(tscr.ap[:, 0:width], tscr.buf)
    ts(st, t, src_th, sxc, ALU.mult, MAGIC, ALU.add)
    ts(st, t, t, MAGIC, ALU.subtract, np.float32(127.0), ALU.min)
    ts(st, xq_th, t, np.float32(-128.0), ALU.max)
    st.asm.pos -= 2


def quant_weight(st, src, dstd, dbuf, rows, cols, sw_col, swr_col):
    a, f = st.abf, st.af32
    a.push(); f.push()
    nt = rows // 128
    csz = min(1024, cols)
    nch = cols // csz
    wtile = [f.alloc(csz, f"qw_w{j}") for j in range(2)]
    tq = f.alloc(csz, "qw_t")
    qb = [a.alloc(csz, f"qw_q{j}") for j in range(2)]
    absac = st.asm.alloc(nt, "absac")
    junk = TH(st.junk.ap[:, 0:1024], st.junk.buf)
    acc1 = st.asm.alloc(1, "acc1")
    ld_dram(st, "sync", wtile[0], src[0:128, 0:csz])
    for i in range(nt):
        for ch in range(nch):
            k = i * nch + ch
            if k + 1 < nt * nch:
                i2, c2 = divmod(k + 1, nch)
                ld_dram(st, "sync", wtile[(k + 1) % 2],
                        src[i2 * 128:(i2 + 1) * 128, c2 * csz:(c2 + 1) * csz])
            wt = wtile[k % 2]
            for off in range(0, csz, 1024):
                for sgn in (1.0, -1.0):
                    act(st, junk, TH(wt.ap[:, off:off + 1024], wt.buf),
                        AF.Relu, scale=sgn, accum=acc1)
                    if ch == 0 and off == 0 and sgn == 1.0:
                        copy(st, TH(absac.ap[:, i:i + 1], absac.buf), acc1)
                    else:
                        tt(st, TH(absac.ap[:, i:i + 1], absac.buf),
                           TH(absac.ap[:, i:i + 1], absac.buf), acc1, ALU.add)
    onescol = TH(st.sb["utri"].ap[:, 127:128], st.sb["utri"].buf)
    pt = TH(st.PB[5].ap[0:1, 0:nt], st.PB[5].buf)
    st.S.op("tensor", lambda e, o=pt.ap, l=onescol.ap, r=absac.ap:
            e.matmul(o, l, r, start=True, stop=True),
            reads=[onescol.buf, absac.buf], writes=[st.PB[5].buf])
    tot = st.asm.alloc(1, "tot")
    tot1 = TH(tot.ap[0:1, 0:1], tot.buf)
    st.S.op("vector", lambda e, o=tot1.ap, i=pt.ap:
            e.tensor_reduce(o, i, mybir.AxisListType.X, ALU.add),
            reads=[st.PB[5].buf], writes=[tot.buf], fence=True)
    ts(st, tot1, tot1, np.float32(1.0 / (rows * cols)), ALU.mult,
       np.float32(EPS5), ALU.max)
    swr1 = st.asm.alloc(1, "swr1")
    sw1 = st.asm.alloc(1, "sw1")
    copy(st, TH(swr1.ap[0:1, 0:1], swr1.buf), tot1)
    recip(st, TH(sw1.ap[0:1, 0:1], sw1.buf), tot1)
    ones = st.sb["ones_row"]
    for src1, dstc in ((sw1, sw_col), (swr1, swr_col)):
        p = pbs(st, 5, 1)
        st.S.op("tensor",
                lambda e, o=p.ap, l=ones.ap[0:1, 0:128],
                r=src1.ap[0:1, 0:1]:
                e.matmul(o, l, r, start=True, stop=True),
                reads=[ones.buf, src1.buf], writes=[p.buf])
        copy(st, dstc, p)
    ld_dram(st, "sync", wtile[0], src[0:128, 0:csz])
    for i in range(nt):
        for ch in range(nch):
            k = i * nch + ch
            if k + 1 < nt * nch:
                i2, c2 = divmod(k + 1, nch)
                ld_dram(st, "sync", wtile[(k + 1) % 2],
                        src[i2 * 128:(i2 + 1) * 128, c2 * csz:(c2 + 1) * csz])
            ts(st, tq, wtile[k % 2], sw_col, ALU.mult, MAGIC, ALU.add)
            ts(st, tq, tq, MAGIC, ALU.subtract, np.float32(1.0), ALU.min)
            ts(st, qb[k % 2], tq, np.float32(-1.0), ALU.max)
            st_dram(st, "gpsimd",
                    dstd[i * 128:(i + 1) * 128, ch * csz:(ch + 1) * csz],
                    qb[k % 2], dbuf)
    st.asm.pos -= nt + 4
    a.pop(); f.pop()



def ffn0_phase(st, first):
    S, a, f = st.S, st.abf, st.af32
    a.push(); f.push(); st.asm.push()
    di = st.di
    if first and os.environ.get("MK_FFN0_QW", "1") == "1":
        quant_weight(st, di["l0f_w1T"], st.wq1, st.db["wq1"], D, H,
                     st.wsc[:, 0:1], st.wsc[:, 1:2])
        quant_weight(st, di["l0f_w2T"], st.wq2, st.db["wq2"], H, D,
                     st.wsc[:, 2:3], st.wsc[:, 3:4])
    swr1 = TH(st.wsc.ap[:, 1:2], st.wsc.buf)
    swr2 = TH(st.wsc.ap[:, 3:4], st.wsc.buf)
    ffn_common_start(st)
    bln1w = a.alloc(D, "bln1w"); bln1b = a.alloc(D, "bln1b")
    bln2w = a.alloc(H, "bln2w"); bln2b = a.alloc(H, "bln2b")
    if os.environ.get("MK_BCAST", "1") == "1":
        bcast_dram_row(st, "l0f_ln1w", D, bln1w)
        bcast_dram_row(st, "l0f_ln1b", D, bln1b)
        bcast_dram_row(st, "l0f_ln2w", H, bln2w)
        bcast_dram_row(st, "l0f_ln2b", H, bln2b)
    zb = [a.alloc(D, f"z{i}") for i in range(2)]
    hsup = a.alloc(4 * H, "hsup")
    xqfm = a.alloc(8 * 512, "xqfm")
    ghqT = xqfm
    wbuf = [a.alloc(8 * 1024, f"wb{i}") for i in range(2)]
    tscr = f.alloc(H, "tscr")
    uscr = f.alloc(512, "uscr")
    uscr2 = f.alloc(512, "uscr2")
    PB, PBT = st.PB, st.PBT

    wq1v = st.wq1.rearrange("(dt p) h -> p dt h", p=128)
    wq2v = st.wq2.rearrange("(ht p) dd -> p ht dd", p=128)

    for sbk in range(int(os.environ.get("MK_FFN0_NSB", "4"))):
        st.asm.push()
        tl = [sbk * 4 + i for i in range(4)]
        m2c = st.asm.alloc(4, "m2c")
        v2c = st.asm.alloc(4, "v2c")
        sxi = st.asm.alloc(4, "sxi")
        sc1 = st.asm.alloc(4, "sc1")
        sc2 = st.asm.alloc(4, "sc2")
        shs = st.asm.alloc(4, "shs")
        sqs = st.asm.alloc(4, "sqs")
        junkD = TH(st.junk.ap[:, 0:D], st.junk.buf)
        for i, tt_ in enumerate(tl):
            z = zb[i % 2]
            ts(st, z, st.xt[tt_], st.mean[:, tt_:tt_ + 1], ALU.subtract,
               st.rstd[:, tt_:tt_ + 1], ALU.mult)
            red(st, TH(m2c.ap[:, i:i + 1], m2c.buf), z, ALU.add)
            act(st, junkD, z, AF.Square, accum=v2c[:, i:i + 1])
        ts(st, m2c, m2c, np.float32(1.0 / D), ALU.mult)
        ts(st, v2c, v2c, np.float32(1.0 / D), ALU.mult)
        mm2_ = st.asm.alloc(4, "mm2_")
        tt(st, mm2_, m2c, m2c, ALU.mult)
        tt(st, v2c, v2c, mm2_, ALU.subtract)
        batch_rstd(st, v2c, 4, 1e-5, v2c)
        for i, tt_ in enumerate(tl):
            z = zb[i % 2]
            ts(st, z, st.xt[tt_], st.mean[:, tt_:tt_ + 1], ALU.subtract,
               st.rstd[:, tt_:tt_ + 1], ALU.mult)
            ts(st, z, z, m2c[:, i:i + 1], ALU.subtract, v2c[:, i:i + 1],
               ALU.mult)
            tt(st, z, z, bln1w, ALU.mult)
            tt(st, z, z, bln1b, ALU.add)
            quant_chain(st, z, D, z, sxi[:, i:i + 1], tscr)
            for dt in range(8):
                tpose(st, pbts(st, 0, 128, dt * 128),
                      z[:, dt * 128:(dt + 1) * 128])
            dst = TH(xqfm.ap.rearrange("p (dt q) -> p dt q", q=512)
                     [:, :, i * 128:(i + 1) * 128], xqfm.buf)
            copy(st, dst, TH(PBT[0].ap.rearrange("p (dt q) -> p dt q", q=128),
                             PBT[0].buf))
            tt(st, TH(sc1.ap[:, i:i + 1], sc1.buf),
               TH(sxi.ap[:, i:i + 1], sxi.buf), swr1, ALU.mult)
        chs = st.asm.alloc(1, "chs")
        junk5 = TH(st.junk.ap[:, 0:512], st.junk.buf)
        if sbk == 0:
            st.bb1w = a.alloc(1024, "bb1w")
        bb1w = st.bb1w
        bb2 = TH(bb1w.ap[:, 0:512], bb1w.buf)
        for hcp in range(4):
            wb = wbuf[hcp % 2]
            ld_dram(st, "sync", wb,
                    wq1v[:, :, hcp * 1024:(hcp + 1) * 1024])
            ld_dram(st, "sync", TH(st.rowstage.ap[0:1, 0:512],
                                   st.rowstage.buf),
                    di["l0f_b1"][0:1, hcp * 1024:hcp * 1024 + 512])
            p5 = pbs(st, 5, 512)
            st.S.op("tensor", lambda e, o=p5.ap,
                    l=st.sb["ones_row"].ap[0:1, 0:128],
                    r=st.rowstage.ap[0:1, 0:512]:
                    e.matmul(o, l, r, start=True, stop=True),
                    reads=[st.sb["ones_row"].buf, st.rowstage.buf],
                    writes=[p5.buf])
            copy(st, TH(bb1w.ap[:, 0:512], bb1w.buf), p5)
            ld_dram(st, "sync", TH(st.rowstage.ap[0:1, 0:512],
                                   st.rowstage.buf),
                    di["l0f_b1"][0:1, hcp * 1024 + 512:hcp * 1024 + 1024])
            p5 = pbs(st, 5, 512)
            st.S.op("tensor", lambda e, o=p5.ap,
                    l=st.sb["ones_row"].ap[0:1, 0:128],
                    r=st.rowstage.ap[0:1, 0:512]:
                    e.matmul(o, l, r, start=True, stop=True),
                    reads=[st.sb["ones_row"].buf, st.rowstage.buf],
                    writes=[p5.buf])
            copy(st, TH(bb1w.ap[:, 512:1024], bb1w.buf), p5)
            for i in range(4):
                pb0 = pbs(st, 0 if i % 2 == 0 else 2, 512)
                pb1 = pbs(st, 1 if i % 2 == 0 else 3, 512)
                for dt in range(8):
                    l = TH(xqfm.ap[:, dt * 512 + i * 128:
                                   dt * 512 + (i + 1) * 128], xqfm.buf)
                    mm(st, pb0, l,
                       TH(wb.ap[:, dt * 1024:dt * 1024 + 512], wb.buf),
                       start=(dt == 0), stop=(dt == 7))
                    mm(st, pb1, l,
                       TH(wb.ap[:, dt * 1024 + 512:(dt + 1) * 1024], wb.buf),
                       start=(dt == 0), stop=(dt == 7))
                for sub, pbk, usc in ((0, pb0, uscr), (1, pb1, uscr2)):
                    hc = hcp * 2 + sub
                    usct = TH(usc.ap[:, 0:512], usc.buf)
                    stt(st, usct, pbk, TH(sc1.ap[:, i:i + 1], sc1.buf),
                        ALU.mult, ALU.add,
                        TH(bb1w.ap[:, sub * 512:(sub + 1) * 512], bb1w.buf))
                    hdst = TH(hsup.ap[:, i * H + hc * 512:
                                      i * H + (hc + 1) * 512], hsup.buf)
                    act(st, hdst, usct, AF.Gelu, accum=chs)
                    if hc == 0:
                        copy(st, TH(shs.ap[:, i:i + 1], shs.buf), chs)
                    else:
                        tt(st, TH(shs.ap[:, i:i + 1], shs.buf),
                           TH(shs.ap[:, i:i + 1], shs.buf), chs, ALU.add)
                    act(st, junk5, hdst, AF.Square, accum=chs)
                    if hc == 0:
                        copy(st, TH(sqs.ap[:, i:i + 1], sqs.buf), chs)
                    else:
                        tt(st, TH(sqs.ap[:, i:i + 1], sqs.buf),
                           TH(sqs.ap[:, i:i + 1], sqs.buf), chs, ALU.add)
        if sbk == 0 and os.environ.get("MK_DBG", "0") == "1":
            st_dram(st, "gpsimd", st.dbg1[:, :], xqfm, Buf("dbg1"))
            st_dram(st, "gpsimd", st.dbg2[:, :],
                    TH(hsup.ap[:, 0:H], hsup.buf), Buf("dbg2"))
            d3 = f.alloc(64, "d3scr")
            copy(st, TH(d3.ap[:, 0:12], d3.buf), st.wsc)
            copy(st, TH(d3.ap[:, 12:28], d3.buf), st.mean)
            copy(st, TH(d3.ap[:, 28:44], d3.buf), st.rstd)
            copy(st, TH(d3.ap[:, 44:48], d3.buf), m2c)
            copy(st, TH(d3.ap[:, 48:52], d3.buf), v2c)
            copy(st, TH(d3.ap[:, 52:56], d3.buf), sc1)
            st_dram(st, "gpsimd", st.dbg3[:, :], d3, Buf("dbg3"))
        ts(st, shs, shs, np.float32(1.0 / H), ALU.mult)
        ts(st, sqs, sqs, np.float32(1.0 / H), ALU.mult)
        mh2 = st.asm.alloc(4, "mh2")
        tt(st, mh2, shs, shs, ALU.mult)
        tt(st, sqs, sqs, mh2, ALU.subtract)
        batch_rstd(st, sqs, 4, 1e-5, sqs)
        shi = st.asm.alloc(4, "shi")
        for i in range(4):
            hv = TH(hsup.ap[:, i * H:(i + 1) * H], hsup.buf)
            ts(st, hv, hv, shs[:, i:i + 1], ALU.subtract, sqs[:, i:i + 1],
               ALU.mult)
            tt(st, hv, hv, bln2w, ALU.mult)
            tt(st, hv, hv, bln2b, ALU.add)
            quant_chain(st, hv, H, hv, shi[:, i:i + 1], tscr)
            tt(st, TH(sc2.ap[:, i:i + 1], sc2.buf),
               TH(shi.ap[:, i:i + 1], shi.buf), swr2, ALU.mult)
        for dc in range(2):
            ld_dram(st, "sync", TH(st.rowstage.ap[0:1, 0:512],
                                   st.rowstage.buf),
                    di["l0f_b2"][0:1, dc * 512:(dc + 1) * 512])
            p5 = pbs(st, 5, 512)
            st.S.op("tensor", lambda e, o=p5.ap,
                    l=st.sb["ones_row"].ap[0:1, 0:128],
                    r=st.rowstage.ap[0:1, 0:512]:
                    e.matmul(o, l, r, start=True, stop=True),
                    reads=[st.sb["ones_row"].buf, st.rowstage.buf],
                    writes=[p5.buf])
            copy(st, bb2, p5)
            wb2 = TH(wbuf[dc % 2].ap[:, 0:4096], wbuf[dc % 2].buf)
            for htg in range(4):
                ld_dram(st, "sync", wb2,
                        wq2v[:, htg * 8:(htg + 1) * 8,
                             dc * 512:(dc + 1) * 512])
                for i in range(4):
                    for hh in range(8):
                        ht = htg * 8 + hh
                        tpose(st, pbts(st, 1, 128, hh * 128),
                              TH(hsup.ap[:, i * H + ht * 128:
                                         i * H + (ht + 1) * 128], hsup.buf))
                    dst = TH(ghqT.ap.rearrange("p (g q) -> p g q", q=512)
                             [:, :, i * 128:(i + 1) * 128], ghqT.buf)
                    copy(st, dst, TH(
                        PBT[1].ap.rearrange("p (g q) -> p g q", q=128),
                        PBT[1].buf))
                for i in range(4):
                    pbk = pbs(st, 2 + i, 512)
                    for hh in range(8):
                        mm(st, pbk,
                           TH(ghqT.ap[:, hh * 512 + i * 128:
                                      hh * 512 + (i + 1) * 128], ghqT.buf),
                           TH(wb2.ap[:, hh * 512:(hh + 1) * 512], wb2.buf),
                           start=(htg == 0 and hh == 0),
                           stop=(htg == 3 and hh == 7))
            for i, tt_ in enumerate(tl):
                pbk = pbs(st, 2 + i, 512)
                stt(st, uscr, pbk, TH(sc2.ap[:, i:i + 1], sc2.buf), ALU.mult,
                    ALU.add, bb2)
                xd = st.xt[tt_][:, dc * 512:(dc + 1) * 512]
                stt(st, xd, uscr, np.float32(0.5), ALU.mult, ALU.add, xd)
        st.asm.pop()
    st.asm.pop(); a.pop(); f.pop()



def att0_phase(st):
    S, a, f = st.S, st.abf, st.af32
    a.push(); f.push(); st.asm.push()
    di = st.di
    PB, PBT = st.PB, st.PBT
    ones = st.sb["ones_row"]
    utri = st.sb["utri"]
    stril = st.sb["stril"]
    neg_utri = st.sb["neg_utri"]
    ffn_common_start(st)

    def apc(name, arena=a):
        return [arena.alloc(128, f"{name}{i}") for i in range(NTILE)]
    qT, k_n, bK, bV, WtT, UT, DU, Rc, Ph, gate = (
        apc("qT"), apc("k_n"), apc("bK"), apc("bV"), apc("WtT"), apc("UT"),
        apc("DU"), apc("Rc"), apc("Ph"), apc("gate"))
    graw = apc("graw")
    GUt = apc("GUt")
    kraw = f.alloc(128, "kraw")
    zab = [f.alloc(2, f"zab{i}") for i in range(NTILE)]
    crow = [f.alloc(1, f"crow{i}") for i in range(NTILE)]
    crw = [f.alloc(128, f"crw{i}") for i in range(NTILE)]
    la = st.asm.alloc(16, "la")
    beta = st.asm.alloc(16, "beta")
    ssk = st.asm.alloc(16, "ssk")
    gam = [f.alloc(1, f"gam{i}") for i in range(NTILE)]
    gamL = [f.alloc(1, f"gamL{i}") for i in range(NTILE)]
    Dcol = [f.alloc(1, f"Dcol{i}") for i in range(NTILE)]
    woT = a.alloc(D, "woT")
    woT_f = f.alloc(D, "woTf")
    ld_dram(st, "sync", woT_f, di["l0a_WoT"][:, :])
    copy(st, woT, woT_f)
    f.pos -= D

    a.push(); f.push()
    xfm = a.alloc(8 * TPC, "xfm")
    zscr = a.alloc(D, "zscr")
    for tt_ in range(NTILE):
        ts(st, zscr, st.xt[tt_], st.mean[:, tt_:tt_ + 1], ALU.subtract,
           st.rstd[:, tt_:tt_ + 1], ALU.mult)
        for dt in range(8):
            tpose(st, pbts(st, 0, 128, dt * 128),
                  zscr[:, dt * 128:(dt + 1) * 128])
        dst = TH(xfm.ap.rearrange("p (dt q) -> p dt q", q=TPC)
                 [:, :, tt_ * 128:(tt_ + 1) * 128], xfm.buf)
        copy(st, dst, TH(PBT[0].ap.rearrange("p (dt q) -> p dt q", q=128),
                         PBT[0].buf))
    pw = {}
    for nm, wd in [("l0a_WqT", C), ("l0a_WkT", C), ("l0a_WvT", C),
                   ("l0a_WgT", C), ("l0a_WabT", 2)]:
        wf = f.alloc(wd * 8, "pwf")
        v = di[nm].rearrange("(dt p) w -> p dt w", p=128)
        ld_dram(st, "sync",
                TH(wf.ap.rearrange("p (dt w) -> p dt w", w=wd), wf.buf),
                v[:, :, :])
        wb_ = a.alloc(wd * 8, "pwb")
        copy(st, wb_, wf)
        pw[nm] = wb_
        f.pos -= wd * 8

    for ch in range(NTILE):
        xsl = TH(xfm.ap.rearrange("p (dt q) -> p dt q", q=TPC)
                 [:, :, ch * 128:(ch + 1) * 128], xfm.buf)
        p0 = pbs(st, 0, 128)
        for dt in range(8):
            mm(st, p0, TH(pw["l0a_WqT"].ap[:, dt * C:(dt + 1) * C],
                          pw["l0a_WqT"].buf),
               TH(xsl.ap[:, dt, :], xsl.buf), start=(dt == 0), stop=False)
        bias_mm(st, p0, st.sb["l0a_bq"], 128, transpose_roles=True)
        copy(st, qT[ch], p0)
        p1 = pbs(st, 1, 128)
        for dt in range(8):
            mm(st, p1, TH(xsl.ap[:, dt, :], xsl.buf),
               TH(pw["l0a_WkT"].ap[:, dt * C:(dt + 1) * C],
                  pw["l0a_WkT"].buf), start=(dt == 0), stop=False)
        bias_mm(st, p1, st.sb["l0a_bk"], 128)
        copy(st, kraw, p1)
        act(st, TH(st.junk.ap[:, 0:128], st.junk.buf), kraw, AF.Square,
            accum=ssk[:, ch:ch + 1])
        copy(st, k_n[ch], kraw)
        p2 = pbs(st, 2, 128)
        for dt in range(8):
            mm(st, p2, TH(xsl.ap[:, dt, :], xsl.buf),
               TH(pw["l0a_WvT"].ap[:, dt * C:(dt + 1) * C],
                  pw["l0a_WvT"].buf), start=(dt == 0), stop=False)
        bias_mm(st, p2, st.sb["l0a_bv"], 128)
        copy(st, bV[ch], p2)
        p3 = pbs(st, 3, 128)
        for dt in range(8):
            mm(st, p3, TH(xsl.ap[:, dt, :], xsl.buf),
               TH(pw["l0a_WgT"].ap[:, dt * C:(dt + 1) * C],
                  pw["l0a_WgT"].buf), start=(dt == 0), stop=False)
        bias_mm(st, p3, st.sb["l0a_bg"], 128)
        copy(st, graw[ch], p3)
        p4 = pbs(st, 4, 2)
        for dt in range(8):
            mm(st, p4, TH(xsl.ap[:, dt, :], xsl.buf),
               TH(pw["l0a_WabT"].ap[:, dt * 2:(dt + 1) * 2],
                  pw["l0a_WabT"].buf), start=(dt == 0), stop=False)
        bias_mm(st, p4, st.sb["l0a_bab"], 2)
        copy(st, zab[ch], p4)
    st.S.barrier()
    a.pop(); f.pop()
    for ch in range(NTILE):
        act(st, gate[ch], graw[ch], AF.Sigmoid)
        act(st, TH(beta.ap[:, ch:ch + 1], beta.buf),
            TH(zab[ch].ap[:, 1:2], zab[ch].buf), AF.Sigmoid)
    for ch in range(NTILE):
        act(st, TH(la.ap[:, ch:ch + 1], la.buf),
            TH(zab[ch].ap[:, 0:1], zab[ch].buf), AF.Sigmoid)
    act(st, la, la, AF.Ln)
    kn_s = st.asm.alloc(16, "kn_s")
    kn_t = st.asm.alloc(16, "kn_t")
    ts(st, kn_t, ssk, np.float32(1.0), ALU.mult, np.float32(1e-30), ALU.max)
    act(st, kn_t, kn_t, AF.Ln)
    act(st, kn_t, kn_t, AF.Exp, scale=-0.5)
    ts(st, kn_s, kn_t, np.float32(1e12), ALU.min)
    for ch in range(NTILE):
        ts(st, k_n[ch], k_n[ch], kn_s[:, ch:ch + 1], ALU.mult)
        ts(st, bK[ch], k_n[ch], TH(beta.ap[:, ch:ch + 1], beta.buf), ALU.mult)
        ts(st, bV[ch], bV[ch], TH(beta.ap[:, ch:ch + 1], beta.buf), ALU.mult)
    Gt_args = []
    for ch in range(NTILE):
        lac = TH(la.ap[:, ch:ch + 1], la.buf)
        p0 = pbs(st, 0, 1)
        st.S.op("tensor", lambda e, o=p0.ap, l=utri.ap, r=lac.ap:
                e.matmul(o, l, r, start=True, stop=True),
                reads=[utri.buf, la.buf], writes=[p0.buf])
        copy(st, crow[ch], p0)
        p1 = TH(PB[1].ap[0:1, 0:128], PB[1].buf)
        st.S.op("tensor", lambda e, o=p1.ap, l=lac.ap, r=utri.ap:
                e.matmul(o, l, r, start=True, stop=True),
                reads=[utri.buf, la.buf], writes=[PB[1].buf])
        cr1 = TH(crw[ch].ap[0:1, 0:128], crw[ch].buf)
        copy(st, cr1, p1)
        p2 = pbs(st, 2, 128)
        st.S.op("tensor", lambda e, o=p2.ap, l=ones.ap[0:1, 0:128], r=cr1.ap:
                e.matmul(o, l, r, start=True, stop=True),
                reads=[ones.buf, crw[ch].buf], writes=[p2.buf])
        garg = f.alloc(128, f"garg{ch}")
        ts(st, garg, p2, crow[ch], ALU.subtract, np.float32(0.0), ALU.min)
        Gt_args.append(garg)
    for ch in range(NTILE):
        act(st, GUt[ch], Gt_args[ch], AF.Exp)
        tt(st, GUt[ch], GUt[ch], utri, ALU.mult)
    f.pos -= NTILE * 128
    for ch in range(NTILE):
        act(st, gam[ch], crow[ch], AF.Exp)
        cr1 = TH(crw[ch].ap[0:1, 0:128], crw[ch].buf)
        p0 = pbs(st, 0, 1)
        st.S.op("tensor", lambda e, o=p0.ap, l=ones.ap[0:1, 0:128],
                r=cr1.ap[0:1, 127:128]:
                e.matmul(o, l, r, start=True, stop=True),
                reads=[ones.buf, crw[ch].buf], writes=[p0.buf])
        cl = f.alloc(1, "cl")
        copy(st, cl, p0)
        dar = f.alloc(1, "dar")
        tt(st, dar, cl, crow[ch], ALU.subtract)
        act(st, Dcol[ch], dar, AF.Exp)
        act(st, gamL[ch], cl, AF.Exp)
        f.pos -= 2
    kTn = a.alloc(128, "kTn")
    Sm = a.alloc(128, "Sm"); SmT = a.alloc(128, "SmT")
    accm = a.alloc(128, "accm"); accT = a.alloc(128, "accT")
    Mt = a.alloc(128, "Mt")
    M2t = a.alloc(128, "M2t")
    WtS = a.alloc(128, "WtS")
    Us = a.alloc(128, "Us")
    gI = f.alloc(128, "gI")
    ngl = f.alloc(1, "ngl")
    for ch in range(NTILE):
        tpose(st, pbts(st, 0, 128), k_n[ch])
        copy(st, kTn, pbts(st, 0, 128))
        p0 = pbs(st, 0, 128)
        mm(st, p0, kTn, kTn, True, True)
        stt(st, Sm, p0, TH(beta.ap[:, ch:ch + 1], beta.buf), ALU.mult,
            ALU.mult, stril)
        ts(st, Sm, Sm, np.float32(-1.0), ALU.mult)
        tt(st, accm, Sm, st.sb["ident_bf"], ALU.add)
        for it in range(6):
            tpose(st, pbts(st, 0, 128), Sm)
            copy(st, SmT, pbts(st, 0, 128))
            p1 = pbs(st, 1, 128)
            mm(st, p1, SmT, Sm, True, True)
            copy(st, Sm, p1)
            tpose(st, pbts(st, 0, 128), accm)
            copy(st, accT, pbts(st, 0, 128))
            p1 = pbs(st, 1, 128)
            mm(st, p1, accT, Sm, True, True)
            tt(st, accm, accm, p1, ALU.add)
        tpose(st, pbts(st, 0, 128), accm)
        copy(st, Mt, pbts(st, 0, 128))
        p1 = pbs(st, 1, 128)
        mm(st, p1, bK[ch], Mt, True, True)
        copy(st, WtT[ch], p1)
        tt(st, M2t, Mt, GUt[ch], ALU.mult)
        p2 = pbs(st, 2, 128)
        mm(st, p2, bV[ch], M2t, True, True)
        copy(st, UT[ch], p2)
        tpose(st, pbts(st, 0, 128), UT[ch])
        copy(st, Us, pbts(st, 0, 128))
        ts(st, DU[ch], Us, Dcol[ch], ALU.mult)
        p3 = pbs(st, 3, 128)
        mm(st, p3, k_n[ch], DU[ch], True, True)
        copy(st, Rc[ch], p3)
        tpose(st, pbts(st, 0, 128), WtT[ch])
        copy(st, WtS, pbts(st, 0, 128))
        p4 = pbs(st, 4, 128)
        mm(st, p4, k_n[ch], WtS, True, True)
        ts(st, gI, st.sb["ident_bf"], gamL[ch], ALU.mult)
        ts(st, ngl, gamL[ch], np.float32(-1.0), ALU.mult)
        stt(st, Ph[ch], p4, ngl, ALU.mult, ALU.add, gI)
    PhC = [f.alloc(128, f"PhC{b}") for b in range(2)]
    RcC = [f.alloc(128, f"RcC{b}") for b in range(2)]
    tmpb = a.alloc(128, "tmpb")
    tmpb2 = a.alloc(128, "tmpb2")
    for b in range(2):
        chs_ = range(8) if b == 0 else range(8, 16)
        for j, ch in enumerate(chs_):
            if j == 0:
                copy(st, PhC[b], Ph[ch])
                copy(st, RcC[b], Rc[ch])
                continue
            tpose(st, pbts(st, 0, 128), Ph[ch])
            copy(st, tmpb, pbts(st, 0, 128))
            copy(st, tmpb2, PhC[b])
            p0 = pbs(st, 0, 128)
            mm(st, p0, tmpb, tmpb2, True, True)
            copy(st, PhC[b], p0)
            copy(st, tmpb2, RcC[b])
            p1 = pbs(st, 1, 128)
            mm(st, p1, tmpb, tmpb2, True, True)
            stt(st, RcC[b], p1, np.float32(1.0), ALU.mult, ALU.add, Rc[ch])
    for b in range(2):
        st_dram(st, "gpsimd", st.st_in[2 * b + 0, :, :], PhC[b],
                st.db["st_in"])
        st_dram(st, "gpsimd", st.st_in[2 * b + 1, :, :], RcC[b],
                st.db["st_in"])
    S.op("gpsimd", lambda e: e.collective_compute(
        "AllGather", ALU.bypass, replica_groups=[list(range(NCORE))],
        ins=[st.st_in[:]], outs=[st.st_out[:]]),
        reads=[st.db["st_in"]], writes=[st.db["st_out"]], always_inc=True)
    gst = a.alloc(NCORE * 4 * 128, "gst")
    gstage = f.alloc(512, "gstage")
    for r_ in range(NCORE):
        S.op("sync", lambda e, d=gstage.ap.rearrange(
            "p (k q) -> p k q", q=128), r2=r_: e.dma_start(
            out=d, in_=bass.AP(st.st_out, r2 * 65536,
                               [[128, 128], [16384, 4], [1, 128]])),
            reads=[st.db["st_out"]], writes=[gstage.buf], dma=True)
        copy(st, TH(gst.ap[:, r_ * 512:(r_ + 1) * 512], gst.buf), gstage)
    f.pos -= 512

    def gPh(r, b):
        return TH(gst.ap[:, (r * 4 + 2 * b) * 128:(r * 4 + 2 * b + 1) * 128],
                  gst.buf)

    def gRc(r, b):
        return TH(gst.ap[:, (r * 4 + 2 * b + 1) * 128:
                         (r * 4 + 2 * b + 2) * 128], gst.buf)
    Astate = [f.alloc(128, f"Ast{b}") for b in range(2)]
    Tk = f.alloc(8 * 128, "Tk")
    sel = st.sb["sel"]
    for b in range(2):
        order = list(range(8)) if b == 0 else list(range(7, -1, -1))
        cur = TH(Tk.ap[:, 0:128], Tk.buf)
        memset(st, cur, 0.0)
        for posi in range(1, 8):
            prev_rank = order[posi - 1]
            prv = TH(Tk.ap[:, (posi - 1) * 128:posi * 128], Tk.buf)
            nxt = TH(Tk.ap[:, posi * 128:(posi + 1) * 128], Tk.buf)
            tpose(st, pbts(st, 0, 128), gPh(prev_rank, b))
            copy(st, tmpb, pbts(st, 0, 128))
            copy(st, tmpb2, prv)
            p0 = pbs(st, 0, 128)
            mm(st, p0, tmpb, tmpb2, True, True)
            stt(st, nxt, p0, np.float32(1.0), ALU.mult, ALU.add,
                gRc(prev_rank, b))
        firstk = True
        for kk in range(8):
            posi = kk if b == 0 else 7 - kk
            tkk = TH(Tk.ap[:, posi * 128:(posi + 1) * 128], Tk.buf)
            if firstk:
                ts(st, Astate[b], tkk, TH(sel.ap[:, kk:kk + 1], sel.buf),
                   ALU.mult)
                firstk = False
            else:
                stt(st, Astate[b], tkk, TH(sel.ap[:, kk:kk + 1], sel.buf),
                    ALU.mult, ALU.add, Astate[b])
    At = a.alloc(128, "At")
    Ab = a.alloc(128, "Ab")
    ZT = a.alloc(128, "ZT")
    Y1n = a.alloc(128, "Y1n")
    Y2w = a.alloc(128, "Y2w")
    Zn = a.alloc(128, "Zn")
    zg = a.alloc(128, "zg")
    og = a.alloc(128, "og")
    ogT = a.alloc(128, "ogT")
    WAb = a.alloc(128, "WAb")
    oA = f.alloc(128, "oA")
    ou = f.alloc(128, "ou")
    for ch in range(NTILE):
        b = 0 if ch < 8 else 1
        A = Astate[b]
        copy(st, Ab, A)
        tpose(st, pbts(st, 0, 128), Ab)
        copy(st, At, pbts(st, 0, 128))
        p0 = pbs(st, 0, 128)
        mm(st, p0, At, qT[ch], True, True)
        copy(st, ZT, p0)
        p1 = pbs(st, 1, 128)
        mm(st, p1, WtT[ch], ZT, True, True)
        stt(st, Y1n, p1, np.float32(1.0), ALU.mult, ALU.mult, neg_utri)
        p2 = pbs(st, 2, 128)
        mm(st, p2, UT[ch], qT[ch], True, True)
        stt(st, Y2w, p2, np.float32(1.0), ALU.mult, ALU.mult, GUt[ch])
        tpose(st, pbts(st, 0, 128), ZT)
        copy(st, Zn, pbts(st, 0, 128))
        ts(st, zg, Zn, gam[ch], ALU.mult)
        p3 = pbs(st, 3, 128)
        mm(st, p3, Y1n, k_n[ch], True, True)
        stt(st, oA, p3, gam[ch], ALU.mult, ALU.add, zg)
        p4 = pbs(st, 4, 128)
        mm(st, p4, Y2w, k_n[ch], True, True)
        stt(st, ou, p4, np.float32(1.0), ALU.mult, ALU.add, oA)
        tt(st, og, ou, gate[ch], ALU.mult)
        tpose(st, pbts(st, 0, 128), og)
        copy(st, ogT, pbts(st, 0, 128))
        for dc in range(2):
            pbk = pbs(st, dc, 512)
            mm(st, pbk, ogT, TH(woT.ap[:, dc * 512:(dc + 1) * 512], woT.buf),
               True, False)
            bias_mm(st, pbk, st.sb["l0a_bo"], 512, row_off=dc * 512)
            xd = st.xt[ch][:, dc * 512:(dc + 1) * 512]
            tt(st, xd, pbk, xd, ALU.add)
        p5 = pbs(st, 5, 128)
        mm(st, p5, WtT[ch], Ab, True, True)
        copy(st, WAb, p5)
        p5b = pbs(st, 5, 128)
        mm(st, p5b, k_n[ch], WAb, True, True)
        stt(st, oA, p5b, np.float32(-1.0), ALU.mult, ALU.add, A)
        stt(st, A, oA, gamL[ch], ALU.mult, ALU.add, Rc[ch])
    st.asm.pop(); a.pop(); f.pop()


def bias_mm_off(st, p_th, row_th, n, row_off):
    bias_mm(st, p_th, row_th, n, row_off=row_off)



def ffn1_phase(st, first):
    S, a, f = st.S, st.abf, st.af32
    a.push(); f.push(); st.asm.push()
    di = st.di
    PB, PBT = st.PB, st.PBT
    if first:
        quant_weight(st, di["l1f_wgT"], st.wqg, st.db["wqg"], D, H,
                     st.wsc[:, 4:5], st.wsc[:, 5:6])
        quant_weight(st, di["l1f_wuT"], st.wqu, st.db["wqu"], D, H,
                     st.wsc[:, 6:7], st.wsc[:, 7:8])
        quant_weight(st, di["l1f_wdT"], st.wqd, st.db["wqd"], H, D,
                     st.wsc[:, 8:9], st.wsc[:, 9:10])
    swgr = TH(st.wsc.ap[:, 5:6], st.wsc.buf)
    swur = TH(st.wsc.ap[:, 7:8], st.wsc.buf)
    swdr = TH(st.wsc.ap[:, 9:10], st.wsc.buf)
    ffn_common_start(st)
    zb = [a.alloc(D, f"z{i}") for i in range(2)]
    hsup = a.alloc(4 * H, "hsup")
    xqfm = a.alloc(8 * 512, "xqfm")
    ghqT = a.alloc(8 * 512, "ghqT")
    wbufg = [a.alloc(8 * 512, f"wbg{i}") for i in range(2)]
    wbufu = [a.alloc(8 * 512, f"wbu{i}") for i in range(2)]
    sgb = a.alloc(512, "sgb")
    tscr = f.alloc(H, "tscr")
    uscr = f.alloc(512, "uscr")
    wqgv = st.wqg.rearrange("(dt p) h -> p dt h", p=128)
    wquv = st.wqu.rearrange("(dt p) h -> p dt h", p=128)
    wqdv = st.wqd.rearrange("(ht p) dd -> p ht dd", p=128)

    for sbk in range(4):
        st.asm.push()
        tl = [sbk * 4 + i for i in range(4)]
        sxi = st.asm.alloc(4, "sxi")
        scg = st.asm.alloc(4, "scg")
        scu = st.asm.alloc(4, "scu")
        for i, tt_ in enumerate(tl):
            z = zb[i % 2]
            ts(st, z, st.xt[tt_], st.mean[:, tt_:tt_ + 1], ALU.subtract,
               st.rstd[:, tt_:tt_ + 1], ALU.mult)
            quant_chain(st, z, D, z, sxi[:, i:i + 1], tscr)
            for dt in range(8):
                tpose(st, pbts(st, 0, 128, dt * 128),
                      z[:, dt * 128:(dt + 1) * 128])
            dst = TH(xqfm.ap.rearrange("p (dt q) -> p dt q", q=512)
                     [:, :, i * 128:(i + 1) * 128], xqfm.buf)
            copy(st, dst, TH(PBT[0].ap.rearrange("p (dt q) -> p dt q", q=128),
                             PBT[0].buf))
            tt(st, TH(scg.ap[:, i:i + 1], scg.buf),
               TH(sxi.ap[:, i:i + 1], sxi.buf), swgr, ALU.mult)
            tt(st, TH(scu.ap[:, i:i + 1], scu.buf),
               TH(sxi.ap[:, i:i + 1], sxi.buf), swur, ALU.mult)
        for hc in range(8):
            wg = wbufg[hc % 2]
            wu = wbufu[hc % 2]
            ld_dram(st, "sync", wg, wqgv[:, :, hc * 512:(hc + 1) * 512])
            ld_dram(st, "sync", wu, wquv[:, :, hc * 512:(hc + 1) * 512])
            for i in range(4):
                pg = pbs(st, i % 2, 512)
                pu = pbs(st, 2 + i % 2, 512)
                for dt in range(8):
                    mm(st, pg,
                       TH(xqfm.ap[:, dt * 512 + i * 128:
                                  dt * 512 + (i + 1) * 128], xqfm.buf),
                       TH(wg.ap[:, dt * 512:(dt + 1) * 512], wg.buf),
                       start=(dt == 0), stop=(dt == 7))
                act(st, sgb, pg, AF.Silu,
                    scale=TH(scg.ap[:, i:i + 1], scg.buf))
                for dt in range(8):
                    mm(st, pu,
                       TH(xqfm.ap[:, dt * 512 + i * 128:
                                  dt * 512 + (i + 1) * 128], xqfm.buf),
                       TH(wu.ap[:, dt * 512:(dt + 1) * 512], wu.buf),
                       start=(dt == 0), stop=(dt == 7))
                ts(st, uscr, pu, TH(scu.ap[:, i:i + 1], scu.buf), ALU.mult)
                hdst = TH(hsup.ap[:, i * H + hc * 512:i * H + (hc + 1) * 512],
                          hsup.buf)
                tt(st, hdst, sgb, uscr, ALU.mult)
        shi = st.asm.alloc(4, "shi")
        sc2 = st.asm.alloc(4, "sc2")
        for i in range(4):
            hv = TH(hsup.ap[:, i * H:(i + 1) * H], hsup.buf)
            quant_chain(st, hv, H, hv, shi[:, i:i + 1], tscr)
            tt(st, TH(sc2.ap[:, i:i + 1], sc2.buf),
               TH(shi.ap[:, i:i + 1], shi.buf), swdr, ALU.mult)
        for dc in range(2):
            wb2 = wbufg[dc % 2]
            for htg in range(4):
                ld_dram(st, "sync", wb2,
                        wqdv[:, htg * 8:(htg + 1) * 8,
                             dc * 512:(dc + 1) * 512])
                for i in range(4):
                    for hh in range(8):
                        ht = htg * 8 + hh
                        tpose(st, pbts(st, 1, 128, hh * 128),
                              TH(hsup.ap[:, i * H + ht * 128:
                                         i * H + (ht + 1) * 128], hsup.buf))
                    dst = TH(ghqT.ap.rearrange("p (g q) -> p g q", q=512)
                             [:, :, i * 128:(i + 1) * 128], ghqT.buf)
                    copy(st, dst, TH(
                        PBT[1].ap.rearrange("p (g q) -> p g q", q=128),
                        PBT[1].buf))
                for i in range(4):
                    pbk = pbs(st, 2 + i, 512)
                    for hh in range(8):
                        mm(st, pbk,
                           TH(ghqT.ap[:, hh * 512 + i * 128:
                                      hh * 512 + (i + 1) * 128], ghqT.buf),
                           TH(wb2.ap[:, hh * 512:(hh + 1) * 512], wb2.buf),
                           start=(htg == 0 and hh == 0),
                           stop=(htg == 3 and hh == 7))
            for i, tt_ in enumerate(tl):
                pbk = pbs(st, 2 + i, 512)
                ts(st, uscr, pbk, TH(sc2.ap[:, i:i + 1], sc2.buf), ALU.mult)
                xd = st.xt[tt_][:, dc * 512:(dc + 1) * 512]
                stt(st, xd, uscr, np.float32(0.5), ALU.mult, ALU.add, xd)
        st.asm.pop()
    st.asm.pop(); a.pop(); f.pop()



_RREG = {}


def _reg_dma(e, st, sbuf_ap, dram, tbl, idx, pattern, extra=0, out_mode=False):
    assert extra == 0
    key = id(e)
    if key not in _RREG:
        _RREG[key] = e.register("roff_shared").__enter__()
    rg = _RREG[key]
    e.reg_load(rg, st.sb[tbl].ap[0:1, idx:idx + 1])
    if out_mode:
        return e.dma_start(out=bass.AP(dram, rg, pattern), in_=sbuf_ap)
    return e.dma_start(out=sbuf_ap, in_=bass.AP(dram, rg, pattern))


def fetch_q(st, r, dst_th):
    st.S.op("sync", lambda e, d=dst_th.ap, r_=r:
            _reg_dma(e, st, d, st.q_out, "offq", r_, [[128, 128], [1, 128]]),
            reads=[st.db["q_out"], st.sb["offq"].buf], writes=[dst_th.buf],
            dma=True, dsem="qf")


def att1_phase(st):
    S, a, f = st.S, st.abf, st.af32
    a.push(); f.push(); st.asm.push()
    di = st.di
    PB, PBT = st.PB, st.PBT
    ffn_common_start(st)
    a.push(); f.push()
    bgln = a.alloc(D, "bgln")
    bcast_dram_row(st, "l1a_gln", D, bgln)
    xfm = a.alloc(8 * TPC, "xfm")
    zscr = f.alloc(D, "zscr")
    zb2 = a.alloc(D, "zb2")
    rms = st.asm.alloc(16, "rms")
    for tt_ in range(NTILE):
        ts(st, zscr, st.xt[tt_], st.mean[:, tt_:tt_ + 1], ALU.subtract,
           st.rstd[:, tt_:tt_ + 1], ALU.mult)
        sszc = st.asm.alloc(1, "sszc")
        act(st, TH(st.junk.ap[:, 0:D], st.junk.buf), zscr, AF.Square,
            accum=sszc)
        batch_rstd(st, sszc, 1, 1e-6, TH(rms.ap[:, tt_:tt_ + 1], rms.buf),
                   scale_pre=1.0 / D)
        st.asm.pos -= 1
        stt(st, zb2, zscr, TH(rms.ap[:, tt_:tt_ + 1], rms.buf), ALU.mult,
            ALU.mult, bgln)
        for dt in range(8):
            tpose(st, pbts(st, 0, 128, dt * 128),
                  zb2[:, dt * 128:(dt + 1) * 128])
        dst = TH(xfm.ap.rearrange("p (dt q) -> p dt q", q=TPC)
                 [:, :, tt_ * 128:(tt_ + 1) * 128], xfm.buf)
        copy(st, dst, TH(PBT[0].ap.rearrange("p (dt q) -> p dt q", q=128),
                         PBT[0].buf))
    st_dram(st, "gpsimd", st.xfm_sp[:, :], xfm, st.db["xfm_sp"])
    wq_b = a.alloc(8 * C, "wq_b")
    wk_b = a.alloc(8 * C, "wk_b")
    wr_b = a.alloc(8 * C, "wr_b")
    for nm, dstw in (("l1a_Wq", wq_b), ("l1a_Wk", wk_b), ("l1a_Wr", wr_b)):
        wf = f.alloc(8 * C, "wf_")
        v = di[nm].rearrange("(dt p) w -> p dt w", p=128)
        ld_dram(st, "sync",
                TH(wf.ap.rearrange("p (dt w) -> p dt w", w=C), wf.buf),
                v[:, :, :])
        copy(st, dstw, wf)
        f.pos -= 8 * C
    wvr = a.alloc(8 * V1, "wvr")
    wv_f = f.alloc(V1, "wv_f")
    wvv = di["l1a_Wv"].rearrange("(dt p) w -> p dt w", p=128)
    for dt in range(8):
        ld_dram(st, "sync", wv_f, wvv[:, dt, :])
        copy(st, TH(wvr.ap[:, dt * V1:(dt + 1) * V1], wvr.buf), wv_f)
    f.pos -= V1
    qTt = a.alloc(128, "qTt")
    kTt = a.alloc(128, "kTt")
    vtile = a.alloc(V1, "vtile")
    for tt_ in range(NTILE):
        xsl = TH(xfm.ap.rearrange("p (dt q) -> p dt q", q=TPC)
                 [:, :, tt_ * 128:(tt_ + 1) * 128], xfm.buf)
        p0 = pbs(st, 0, 128)
        for dt in range(8):
            mm(st, p0, TH(wq_b.ap[:, dt * C:(dt + 1) * C], wq_b.buf),
               TH(xsl.ap[:, dt, :], xsl.buf), start=(dt == 0), stop=(dt == 7))
        ts(st, qTt, p0, INV_TAU, ALU.mult)
        st_dram(st, "gpsimd", st.q_in[tt_, :, :], qTt, st.db["q_in"])
        p1 = pbs(st, 1, 128)
        for dt in range(8):
            mm(st, p1, TH(wk_b.ap[:, dt * C:(dt + 1) * C], wk_b.buf),
               TH(xsl.ap[:, dt, :], xsl.buf), start=(dt == 0), stop=(dt == 7))
        copy(st, kTt, p1)
        b = 0 if tt_ < 8 else 1
        col = (tt_ % 8) * 128
        S.op("gpsimd", lambda e, s=kTt.ap,
             o=bass.AP(st.kv_in, b * 131072 + col, [[1024, 128], [1, 128]]):
             e.dma_start(out=o, in_=s),
             reads=[kTt.buf], writes=[st.db["kv_in"]], dma=True)
        for vc in range(4):
            pbk = pbs(st, 2 + (vc % 2), 512)
            for dt in range(8):
                mm(st, pbk, TH(xsl.ap[:, dt, :], xsl.buf),
                   TH(wvr.ap[:, dt * V1 + vc * 512:dt * V1 + (vc + 1) * 512],
                      wvr.buf),
                   start=(dt == 0), stop=(dt == 7))
            copy(st, TH(vtile.ap[:, vc * 512:(vc + 1) * 512], vtile.buf), pbk)
        S.op("gpsimd", lambda e, s=vtile.ap,
             o=bass.AP(st.kv_in, KV_VOFF + b * (1024 * V1) + col * V1,
                       [[V1, 128], [1, V1]]):
             e.dma_start(out=o, in_=s),
             reads=[vtile.buf], writes=[st.db["kv_in"]], dma=True)
    lvl0 = os.environ.get("MK_A1", "full")
    if lvl0 not in ("a", "ac2"):
        S.op("gpsimd", lambda e: e.collective_compute(
            "AllGather", ALU.bypass, replica_groups=[list(range(NCORE))],
            ins=[st.q_in[:]], outs=[st.q_out[:]]),
            reads=[st.db["q_in"]], writes=[st.db["q_out"]],
            always_inc=True)
    if lvl0 not in ("a", "ac1"):
        S.op("gpsimd", lambda e: e.collective_compute(
            "AllGather", ALU.bypass, replica_groups=[list(range(NCORE))],
            ins=[st.kv_in[:]], outs=[st.kv_out[:]]),
            reads=[st.db["kv_in"]], writes=[st.db["kv_out"]],
            always_inc=True)
    if os.environ.get("MK_NOXLR", "0") == "1":
        S.barrier()
        a.pop(); f.pop()
        lvl = os.environ.get("MK_A1", "full")
        if lvl in ("acf", "full"):
            flash(st)
            S.barrier()
        if lvl == "full":
            att1_return(st)
        st.asm.pop(); a.pop(); f.pop()
        st.S.barrier()
        return
    embv = di["embT"].rearrange("(dt p) m -> p dt m", p=128)
    emb_b = [a.alloc(512, f"embb{i}") for i in range(3)]
    xrt_s = [a.alloc(512, f"xrt_s{i}") for i in range(2)]
    ld_dram(st, "sync", emb_b[0], embv[:, 0, 0:512])
    for mc in range(XLCOLS // 512):
        p4 = pbs(st, 4, 512)
        for dt in range(8):
            i = mc * 8 + dt
            if i + 1 < 8 * (XLCOLS // 512):
                mc2, dt2 = divmod(i + 1, 8)
                ld_dram(st, "sync", emb_b[(i + 1) % 3],
                        embv[:, dt2, mc2 * 512:(mc2 + 1) * 512])
            mm(st, p4, TH(wr_b.ap[:, dt * C:(dt + 1) * C], wr_b.buf),
               emb_b[i % 3], start=(dt == 0), stop=(dt == 7))
        xs = xrt_s[mc % 2]
        copy(st, xs, p4)
        S.op("gpsimd", lambda e, s=xs.ap,
             o=bass.AP(st.xlrT, mc * 512, [[XLCOLS, 128], [1, 512]]):
             e.dma_start(out=o, in_=s),
             reads=[xs.buf], writes=[st.db["xlrT"]], dma=True)
    S.barrier()
    a.pop(); f.pop()
    lvl = os.environ.get("MK_A1", "full")
    if lvl in ("acf", "full"):
        flash(st)
        S.barrier()
    if lvl == "full":
        att1_return(st)
    st.asm.pop(); a.pop(); f.pop()


def flash(st):
    S, a, f = st.S, st.abf, st.af32
    a.push(); f.push()
    PB, PBT = st.PB, st.PBT
    qTu = a.alloc(128, "qTu")
    qTv = a.alloc(128, "qTv")
    qTr = [a.alloc(128, f"qTr{i}") for i in range(2)]
    xlrsl = a.alloc(STRIP_MAX, "xlrsl")
    PIECE = int(os.environ.get("MK_PIECE", "5"))
    shh = [a.alloc(512 * PIECE, f"striph{i}") for i in range(2)]
    sarena = a.alloc(16 * 512, "sarena")
    NVB = int(os.environ.get("MK_NVB", "4"))
    Vb = [a.alloc(2 * V1, f"Vb{i}") for i in range(NVB)]
    kTb = [a.alloc(512, f"kTb{i}") for i in range(2)]
    if os.environ.get("MK_ONEPCH", "0") == "1":
        pch = a.alloc(512, "Pch")
        scb = [pch, pch]
        Pe = [pch, pch]
    else:
        scb = [a.alloc(512, f"scb{i}") for i in range(2)]
        Pe = [a.alloc(512, f"Pe{i}") for i in range(2)]
    PTb = [a.alloc(512, f"PTb{i}") for i in range(2)]
    umask = st.af32.alloc(512, "umask")
    thet = st.asm.alloc(1, "thet")
    wvout = a.alloc(V1, "wvout")
    runm = st.asm.alloc(1, "runm")
    negm = st.asm.alloc(1, "negm")
    cmax = st.asm.alloc(1, "cmax")
    lsum = st.asm.alloc(1, "lsum")
    lpart = st.asm.alloc(1, "lpart")
    rl = st.asm.alloc(1, "rl")
    sdb = [st.db["strip0"], st.db["strip1"]]

    def build_strip(r):
        rp = r % 8
        pitch = STRIP_PITCH[rp]
        slot = r % 2
        nmc = pitch // 512
        q = qTr[r % 2]
        fetch_q(st, r, q)
        ts(st, qTv, q, st.sb["l1a_xlv"], ALU.add)
        S.op("gpsimd", lambda e, d=xlrsl.ap[:, 0:pitch], r_=r:
             _reg_dma(e, st, d, st.xlrT, "offxlr", r_,
                      [[XLCOLS, 128], [1, pitch]]),
             reads=[st.db["xlrT"], st.sb["offxlr"].buf],
             writes=[xlrsl.buf], dma=True, dsem="xl")
        for pc0 in range(0, nmc, PIECE):
            pc1 = min(nmc, pc0 + PIECE)
            sbh = shh[(pc0 // PIECE) % 2]
            for mc in range(pc0, pc1):
                p5 = pbs(st, 5, 512)
                mm(st, p5, qTv,
                   TH(xlrsl.ap[:, mc * 512:(mc + 1) * 512], xlrsl.buf),
                   True, True)
                copy(st, TH(sbh.ap[:, (mc - pc0) * 512:(mc - pc0 + 1) * 512],
                            sbh.buf), p5)
            S.op("gpsimd", lambda e, s=sbh.ap[:, 0:(pc1 - pc0) * 512],
                 o=bass.AP(st.stripd, slot * 128 * STRIP_MAX + pc0 * 512,
                           [[pitch, 128], [1, (pc1 - pc0) * 512]]):
                 e.dma_start(out=o, in_=s),
                 reads=[sbh.buf], writes=[sdb[slot]], dma=True, dsem="sw")

    def v_load(r, kh):
        b = 0 if r < 8 else 1
        g0 = kh * 256
        rank = (g0 // 1024) if b == 0 else (7 - g0 // 1024)
        vin = (rank * KVL + KV_VOFF + b * (1024 * V1) + (g0 % 1024) * V1)
        vb = Vb[kh % NVB]
        S.op("gpsimd",
             lambda e, d=vb.ap.rearrange("p (s w) -> p s w", w=V1),
             o=vin:
             e.dma_start(out=d, in_=bass.AP(
                 st.kv_out, o, [[V1, 128], [128 * V1, 2], [1, V1]])),
             reads=[st.db["kv_out"]], writes=[vb.buf], dma=True)

    def k_load(r, kc):
        b = 0 if r < 8 else 1
        g0 = kc * 512
        rank = (g0 // 1024) if b == 0 else (7 - g0 // 1024)
        kin = rank * KVL + b * 131072 + (g0 % 1024)
        kb = kTb[kc % 2]
        S.op(os.environ.get("MK_KTQ", "sync"), lambda e, d=kb.ap, o=kin:
             e.dma_start(out=d, in_=bass.AP(
                 st.kv_out, o, [[1024, 128], [1, 512]])),
             reads=[st.db["kv_out"]], writes=[kb.buf], dma=True, dsem="kt")

    def strip_read(slot, pitch, kc):
        sc = scb[kc % 2]
        S.op("scalar", lambda e, d=sc.ap, o=(
            slot * 128 * STRIP_MAX + kc * 512 + 127), pp=pitch:
            e.dma_start(out=d, in_=bass.AP(
                st.stripd, o, [[pp - 1, 128], [1, 512]])),
            reads=[sdb[slot]], writes=[sc.buf], dma=True, dsem="sr")

    build_strip(0)
    for r in range(16):
        rp = r % 8
        b = 0 if r < 8 else 1
        pitch = STRIP_PITCH[rp]
        slot = r % 2
        W512 = 2 * rp + 2
        KH = 2 * W512
        VDEPTH = int(os.environ.get("MK_VDEPTH", "4"))
        KTPRE = os.environ.get("MK_KTPRE", "1") == "1"
        for kh in range(min(VDEPTH, KH)):
            v_load(r, kh)
        ts(st, qTu, qTr[r % 2], st.sb["l1a_xlu"], ALU.add)
        memset(st, runm, -3.0e38)
        SCBPRE = os.environ.get("MK_SCBPRE", "1") == "1"
        if KTPRE:
            k_load(r, 0)
        if SCBPRE:
            strip_read(slot, pitch, 0)
        for kc in range(W512):
            if KTPRE:
                if kc + 1 < W512:
                    k_load(r, kc + 1)
            else:
                k_load(r, kc)
            if SCBPRE and kc + 1 < W512:
                strip_read(slot, pitch, kc + 1)
            if not SCBPRE:
                strip_read(slot, pitch, kc)
            p4 = pbs(st, 4, 512)
            mm(st, p4, qTu, kTb[kc % 2], True, True)
            sdst = TH(sarena.ap[:, kc * 512:(kc + 1) * 512], sarena.buf)
            stt(st, sdst, p4, np.float32(1.0), ALU.mult, ALU.add, scb[kc % 2])
            if kc >= W512 - 2:
                ts(st, thet, TH(st.sb["gi"].ap[:, r:r + 1], st.sb["gi"].buf),
                   float(-kc * 512), ALU.add)
                ts(st, umask, st.sb["iota512"], thet, ALU.subtract,
                   np.float32(0.0), ALU.max)
                stt(st, sdst, umask, NEG30, ALU.mult, ALU.add, sdst)
            red(st, cmax, sdst, ALU.max)
            tt(st, runm, runm, cmax, ALU.max)
        if r < 15:
            build_strip(r + 1)
        ts(st, negm, runm, np.float32(-1.0), ALU.mult)
        for kh in range(KH):
            vb = Vb[kh % NVB]
            if VDEPTH == 0:
                v_load(r, kh)
            PEPAR = 1 if os.environ.get("MK_PEPAR", "1") == "1" else 2
            if kh % 2 == 0:
                kc = kh // 2
                pe = Pe[(kc % 2) // PEPAR]
                ssrc = TH(sarena.ap[:, kc * 512:(kc + 1) * 512], sarena.buf)
                act(st, pe, ssrc, AF.Exp, bias=negm, accum=lpart)
                if kc == 0:
                    copy(st, lsum, lpart)
                else:
                    tt(st, lsum, lsum, lpart, ALU.add)
                for kt in range(4):
                    tpose(st, pbts(st, (kc % 2) // PEPAR, 128, kt * 128),
                          TH(pe.ap[:, kt * 128:(kt + 1) * 128], pe.buf))
                pp_ = (kc % 2) // PEPAR
                copy(st, PTb[pp_],
                     TH(PBT[pp_].ap[:, 0:512], PBT[pp_].buf))
            ptb = PTb[((kh // 2) % 2) // PEPAR]
            for kt2 in range(2):
                kt = (kh % 2) * 2 + kt2
                lt = TH(ptb.ap[:, kt * 128:(kt + 1) * 128], ptb.buf)
                for vc in range(4):
                    mm(st, pbs(st, vc, 512), lt,
                       TH(vb.ap[:, kt2 * V1 + vc * 512:
                                kt2 * V1 + (vc + 1) * 512], vb.buf),
                       start=(kh == 0 and kt2 == 0),
                       stop=(kh == KH - 1 and kt2 == 1))
            if VDEPTH > 0 and kh + VDEPTH < KH:
                v_load(r, kh + VDEPTH)
        recip(st, rl, lsum)
        for vc in range(4):
            ts(st, TH(wvout.ap[:, vc * 512:(vc + 1) * 512], wvout.buf),
               pbs(st, vc, 512), rl, ALU.mult)
        st_dram(st, "gpsimd", st.wv_in[r, :, :], wvout, st.db["wv_in"],
                dsem="wv")
    S.op("gpsimd", lambda e: e.collective_compute(
        "AllGather", ALU.bypass, replica_groups=[list(range(NCORE))],
        ins=[st.wv_in[:]], outs=[st.wv_out[:]]),
        reads=[st.db["wv_in"]], writes=[st.db["wv_out"]], always_inc=True)
    st.asm.pos -= 7
    st.af32.pos -= 512
    a.pop(); f.pop()


def att1_return(st):
    S, a, f = st.S, st.abf, st.af32
    di = st.di
    PB, PBT = st.PB, st.PBT
    a.push(); f.push()
    xfm = a.alloc(8 * TPC, "xfm_r")
    ld_dram(st, "sync", xfm, st.xfm_sp[:, :], extra_reads=[st.db["xfm_sp"]])
    wgr = a.alloc(8 * V1, "wgr")
    wf = f.alloc(V1, "wgf")
    wgv = di["l1a_Wg"].rearrange("(dt p) w -> p dt w", p=128)
    for dt in range(8):
        ld_dram(st, "sync", wf, wgv[:, dt, :])
        copy(st, TH(wgr.ap[:, dt * V1:(dt + 1) * V1], wgr.buf), wf)
    f.pos -= V1
    wvt = a.alloc(V1, "wvt")
    sgt = a.alloc(V1, "sgt")
    gT = a.alloc(1024, "gT")
    for tt_ in range(NTILE):
        S.op("scalar", lambda e, d=wvt.ap, t=tt_:
             _reg_dma(e, st, d, st.wv_out, "offwv", t, [[V1, 128], [1, V1]]),
             reads=[st.db["wv_out"], st.sb["offwv"].buf], writes=[wvt.buf],
             dma=True)
        xsl = TH(xfm.ap.rearrange("p (dt q) -> p dt q", q=TPC)
                 [:, :, tt_ * 128:(tt_ + 1) * 128], xfm.buf)
        for vc in range(4):
            pbk = pbs(st, 4 + (vc % 2), 512)
            for dt in range(8):
                mm(st, pbk, TH(xsl.ap[:, dt, :], xsl.buf),
                   TH(wgr.ap[:, dt * V1 + vc * 512:dt * V1 + (vc + 1) * 512],
                      wgr.buf),
                   start=(dt == 0), stop=(dt == 7))
            act(st, TH(sgt.ap[:, vc * 512:(vc + 1) * 512], sgt.buf), pbk,
                AF.Silu)
        tt(st, sgt, wvt, sgt, ALU.mult)
        for half in range(2):
            for vh in range(8):
                vt = half * 8 + vh
                tpose(st, pbts(st, 0, 128, vh * 128),
                      TH(sgt.ap[:, vt * 128:(vt + 1) * 128], sgt.buf))
            copy(st, gT, TH(PBT[0].ap[:, 0:1024], PBT[0].buf))
            st_dram(st, "gpsimd",
                    st.wv_in[tt_, :, half * 1024:(half + 1) * 1024], gT,
                    st.db["wv_in"])
    S.barrier()
    a.pop(); f.pop()
    a.push(); f.push()
    wor = a.alloc(16 * D, "wor")
    wf = f.alloc(D, "wof")
    wov = di["l1a_Wo"].rearrange("(vt p) w -> p vt w", p=128)
    for vt in range(16):
        ld_dram(st, "sync", wf, wov[:, vt, :])
        copy(st, TH(wor.ap[:, vt * D:(vt + 1) * D], wor.buf), wf)
    f.pos -= D
    gtile = a.alloc(V1, "gtile")
    for tt_ in range(NTILE):
        ld_dram(st, "sync", gtile, st.wv_in[tt_, :, :],
                extra_reads=[st.db["wv_in"]])
        for dc in range(2):
            pbk = pbs(st, dc, 512)
            for vt in range(16):
                mm(st, pbk,
                   TH(gtile.ap[:, vt * 128:(vt + 1) * 128], gtile.buf),
                   TH(wor.ap[:, vt * D + dc * 512:vt * D + (dc + 1) * 512],
                      wor.buf),
                   start=(vt == 0), stop=(vt == 15))
            xd = st.xt[tt_][:, dc * 512:(dc + 1) * 512]
            tt(st, xd, pbk, xd, ALU.add)
    a.pop(); f.pop()



def build_nc():
    nc = bass.Bass("TRN2", target_bir_lowering=False,
                   detect_race_conditions=False)
    st = declare_io(nc)
    S = Sched()
    st.S = S
    ctx = contextlib.ExitStack()
    with ctx:
        _build_body(nc, st, ctx)
        S.emit(nc, ctx)
    return nc


def _build_body(nc, st, ctx):
    di = st.di
    af32 = Arena(ctx.enter_context(
        nc.sbuf_tensor("af32", [128, 25856], F32)), 25856, "af32")
    abf = Arena(ctx.enter_context(
        nc.sbuf_tensor("abf", [128, 52480], BF16)), 52480, "abf")
    asm = Arena(ctx.enter_context(
        nc.sbuf_tensor("asm", [128, 512], F32)), 512, "asm")
    ai = Arena(ctx.enter_context(
        nc.sbuf_tensor("ai32", [1, 512], I32)), 512, "ai")
    st.af32, st.abf, st.asm, st.ai = af32, abf, asm, ai

    X = af32.alloc(NTILE * D, "X")
    st.X = X
    st.xt = [X[:, tt_ * D:(tt_ + 1) * D] for tt_ in range(NTILE)]

    pb = [ctx.enter_context(nc.psum_tensor(f"pb{i}", [128, 512], F32))
          for i in range(6)]
    pbt = [ctx.enter_context(nc.psum_tensor(f"pbt{i}", [128, 1024], BF16))
           for i in range(2)]
    st.PB = [TH(p[:, :], Buf(f"pb{i}")) for i, p in enumerate(pb)]
    st.PBT = [TH(p[:, :], Buf(f"pbt{i}")) for i, p in enumerate(pbt)]

    sb = {}
    sb["ident_bf"] = abf.alloc(128, "ident_bf")
    sb["utri"] = af32.alloc(128, "utri")
    sb["neg_utri"] = af32.alloc(128, "neg_utri")
    sb["stril"] = af32.alloc(128, "stril")
    sb["sel"] = af32.alloc(8, "sel")
    sb["ones_row"] = af32.row(512, "ones_row")
    for nm, wdt in [("l0a_bq", C), ("l0a_bk", C), ("l0a_bv", C),
                    ("l0a_bg", C), ("l0a_bab", 2), ("l0a_bo", D)]:
        sb[nm] = af32.row(wdt, nm)
    sb["l1a_xlu"] = af32.alloc(1, "xlu")
    sb["l1a_xlv"] = af32.alloc(1, "xlv")
    for nm, nn in [("offq", 16), ("offwv", 16), ("offxlr", 16),
                   ("offpad", 16)]:
        sb[nm] = TH(ai.t[0:1, ai.pos:ai.pos + nn], Buf(nm))
        ai.pos += nn
    sb["gi"] = af32.alloc(16, "gi")
    sb["iota512"] = af32.alloc(512, "iota512")
    st.sb = sb
    st.junk = abf.alloc(1024, "junk")
    st.rowstage = af32.row(512, "rowstage")
    st.wsc = af32.alloc(12, "wsc")
    st.magic_col = af32.alloc(1, "magic_col")
    memset(st, st.magic_col, MAGIC)

    S = st.S
    for name in ["ident_bf", "utri", "neg_utri", "stril", "sel",
                 "l1a_xlu", "l1a_xlv", "gi", "iota512"]:
        src = di[name]
        dst = sb[name]
        S.op("sync", lambda e, d=dst.ap, s=src[:]: e.dma_start(out=d, in_=s),
             writes=[dst.buf], dma=True)
    for name in ["ones_row", "l0a_bq", "l0a_bk", "l0a_bv", "l0a_bg",
                 "l0a_bab", "l0a_bo"]:
        src = di[name]
        dst = sb[name]
        S.op("sync", lambda e, d=dst.ap, s=src[0:1, :]:
             e.dma_start(out=d, in_=s), writes=[dst.buf], dma=True)
    for name in ["offq", "offwv", "offxlr", "offpad"]:
        src = di[name]
        dst = sb[name]
        S.op("sync", lambda e, d=dst.ap, s=src[0:1, :]:
             e.dma_start(out=d, in_=s), writes=[dst.buf], dma=True)

    xv = di["x_shard"].rearrange("(t p) d -> p t d", p=128)
    S.op("sync", lambda e: e.dma_start(
        out=af32.t[:, 0:NTILE * D].rearrange("p (t d) -> p t d", d=D),
        in_=xv[:, :, :]), writes=[X.buf], dma=True)

    S.barrier()
    if "ffn0a" in PHASES:
        ffn0_phase(st, first=True)
    if "att0" in PHASES:
        att0_phase(st)
    if "ffn0b" in PHASES:
        ffn0_phase(st, first=False)
    if "ffn1a" in PHASES:
        ffn1_phase(st, first=True)
    if "att1" in PHASES:
        att1_phase(st)
    if "ffn1b" in PHASES:
        ffn1_phase(st, first=False)

    yv = st.y.rearrange("(t p) d -> p t d", p=128)
    S.op("gpsimd", lambda e: e.dma_start(
        out=yv[:, :, :],
        in_=af32.t[:, 0:NTILE * D].rearrange("p (t d) -> p t d", d=D)),
        reads=[X.buf], writes=[st.db["y"]], dma=True)
    S.barrier()


def kernel(**inputs):
    in_maps = _shard_inputs(inputs)
    if "prog" not in _NC_CACHE:
        _NC_CACHE["prog"] = build_nc()
    nc = _NC_CACHE["prog"]
    res = run_bass_kernel_spmd(nc, in_maps, core_ids=list(range(NCORE)))
    return _unshard_output(res.results)


if __name__ == "__main__":
    nc = build_nc()
    print("build ok")


# version 07
# speedup vs baseline: 1.0727x; 1.0169x over previous
import os
import sys
import math
import contextlib

sys.path.insert(0, "/opt/trn_rl_repo")

import numpy as np
import ml_dtypes

import concourse.bass as bass
import concourse.mybir as mybir
from concourse.bass_utils import run_bass_kernel_spmd

F32 = mybir.dt.float32
BF16 = mybir.dt.bfloat16
I32 = mybir.dt.int32
AF = mybir.ActivationFunctionType
ALU = mybir.AluOpType
BF = ml_dtypes.bfloat16

B, T, D = 2, 8192, 1024
H = 4096
C = 128
V1 = 2048
NCORE = 8
TPC = 2048
NTILE = 16
EPS5 = np.float32(1e-5)
MAGIC = np.float32(1.5 * 2.0 ** 23)
NEG30 = np.float32(-1.0e30)
INV_TAU = np.float32(1.0 / math.sqrt(C))

XL_PAD = 2560
XLCOLS = T + XL_PAD
STRIP_PITCH = [(2 * rp + 2 + 3) * 512 for rp in range(8)]
STRIP_MAX = max(STRIP_PITCH)
KVL = 2 * 128 * 1024 + 2 * 1024 * V1
KV_VOFF = 2 * 128 * 1024

PHASES = os.environ.get(
    "MK_PHASES", "ffn0a,att0,ffn0b,ffn1a,att1,ffn1b").split(",")

_NC_CACHE = {}

ENGS = ("sync", "tensor", "vector", "scalar", "gpsimd")


class Buf:
    __slots__ = ("w", "rs", "name")

    def __init__(self, name=""):
        self.w = None
        self.rs = []
        self.name = name


class Rec:
    __slots__ = ("fn", "waits", "inc", "dma", "tok")

    def __init__(self, fn, waits, dma, tok):
        self.fn = fn
        self.waits = waits
        self.inc = False
        self.dma = dma
        self.tok = tok


class Sched:
    def __init__(self):
        self.recs = {e: [] for e in ENGS}
        self.ndma = {e: 0 for e in ENGS}
        self.nx = {}
        self.all = {}
        self.fencep = {e: None for e in ENGS}
        self.lastc = {e: None for e in ENGS}
        self.lastd = {e: None for e in ENGS}
        self.barp = {e: [] for e in ENGS}

    def op(self, eng, fn, reads=(), writes=(), dma=False, fence=False,
           always_inc=False, dsem=None):
        if os.environ.get("MK_FENCE_ALL", "0") == "1" and not dma:
            fence = True
        if dma and dsem is not None:
            tok = ("x", dsem, self.nx.get(dsem, 0))
            self.nx[dsem] = self.nx.get(dsem, 0) + 1
        elif dma:
            tok = ("d", eng, self.ndma[eng])
            self.ndma[eng] += 1
        else:
            tok = ("c", eng, len(self.recs[eng]))
        deps = set()
        for b in reads:
            if b.w is not None:
                deps.add(b.w)
        for b in writes:
            if b.w is not None:
                deps.add(b.w)
            for t in b.rs:
                deps.add(t)
        forced = []
        if self.barp[eng]:
            forced.extend(self.barp[eng])
            self.barp[eng] = []
        if self.fencep[eng] is not None:
            forced.append(self.fencep[eng])
            self.fencep[eng] = None
        if fence and self.lastc[eng] is not None:
            forced.append(self.lastc[eng])
        waits = []
        for t in deps:
            kind, e, i = t
            if kind == "c" and e == eng:
                continue
            waits.append(t)
            self.all[t].inc = True
        for t in forced:
            if t in waits:
                continue
            waits.append(t)
            self.all[t].inc = True
        rec = Rec(fn, waits, dma, tok)
        if always_inc:
            rec.inc = True
        self.recs[eng].append(rec)
        self.all[tok] = rec
        if dma:
            self.lastd[eng] = tok
        else:
            self.lastc[eng] = tok
        if fence:
            self.fencep[eng] = tok
        for b in reads:
            b.rs.append(tok)
        for b in writes:
            b.w = tok
            b.rs = []
        return tok

    def barrier(self):
        toks = []
        for e in ENGS:
            if self.lastc[e] is not None:
                toks.append(self.lastc[e])
            if self.lastd[e] is not None:
                toks.append(self.lastd[e])
        for e in ENGS:
            self.barp[e] = list(set(self.barp[e]) | set(toks))

    def emit(self, nc, ctx):
        sems = {}
        for e in ENGS:
            sems[("c", e)] = ctx.enter_context(nc.semaphore(f"sem_c_{e}"))
            sems[("d", e)] = ctx.enter_context(nc.semaphore(f"sem_d_{e}"))
        for nm in self.nx:
            sems[("x", nm)] = ctx.enter_context(nc.semaphore(f"sem_x_{nm}"))
        incidx = {}
        for e in ENGS:
            n = 0
            for r in self.recs[e]:
                if not r.dma:
                    if r.inc:
                        n += 1
                    incidx[r.tok] = n

        def target(t):
            kind, e, i = t
            if kind == "d":
                return sems[("d", e)], (i + 1) * 16
            if kind == "x":
                return sems[("x", e)], (i + 1) * 16
            return sems[("c", e)], incidx[t]

        with nc.Block() as block:
            for e in ENGS:
                def make(e):
                    def body(eng):
                        hw = {}
                        for r in self.recs[e]:
                            for t in sorted(r.waits):
                                sem, val = target(t)
                                kk = id(sem)
                                if hw.get(kk, -1) >= val:
                                    continue
                                hw[kk] = val
                                eng.wait_ge(sem, val)
                            if r.fn is None:
                                if r.inc:
                                    eng.nop().then_inc(sems[("c", e)], 1)
                                continue
                            h = r.fn(eng)
                            if r.dma:
                                if r.tok[0] == "x":
                                    h.then_inc(sems[("x", r.tok[1])], 16)
                                else:
                                    h.then_inc(sems[("d", e)], 16)
                            elif r.inc:
                                h.then_inc(sems[("c", e)], 1)
                    return body
                getattr(block, e)(make(e))


class TH:
    __slots__ = ("ap", "buf")

    def __init__(self, ap, buf):
        self.ap = ap
        self.buf = buf

    def __getitem__(self, sl):
        return TH(self.ap[sl], self.buf)


class Arena:
    def __init__(self, tensor, cols, name):
        self.t = tensor
        self.cols = cols
        self.pos = 0
        self.marks = []
        self.name = name
        self.recs = []

    def _mkbuf(self, n, name):
        start, end = self.pos, self.pos + n
        b = Buf(name)
        keep = []
        for (s0, e0, ob) in self.recs:
            if s0 < end and start < e0:
                if ob.w is not None:
                    b.rs.append(ob.w)
                b.rs.extend(ob.rs)
            else:
                keep.append((s0, e0, ob))
        keep.append((start, end, b))
        self.recs = keep
        self.pos = end
        return b

    def alloc(self, n, name=""):
        assert self.pos + n <= self.cols, \
            f"{self.name} overflow at {name}: {self.pos}+{n}>{self.cols}"
        start = self.pos
        b = self._mkbuf(n, name)
        return TH(self.t[:, start:start + n], b)

    def row(self, n, name=""):
        assert self.pos + n <= self.cols
        start = self.pos
        b = self._mkbuf(n, name)
        return TH(self.t[0:1, start:start + n], b)

    def push(self):
        self.marks.append(self.pos)

    def pop(self):
        self.pos = self.marks.pop()



def _host_tables():
    t = {}
    t["ident_bf"] = np.eye(128, dtype=np.float32).astype(BF)
    r = np.arange(128)
    t["utri"] = (r[:, None] <= r[None, :]).astype(np.float32)
    t["neg_utri"] = -t["utri"]
    t["stril"] = (r[:, None] > r[None, :]).astype(np.float32)
    t["ones_row"] = np.ones((1, 512), np.float32)
    t["neg30"] = np.full((128, 1152), NEG30, np.float32).astype(BF)
    t["iota512"] = np.broadcast_to(np.arange(512, dtype=np.float32),
                                   (128, 512)).copy()
    pos = np.arange(T, dtype=np.float32)
    inv_lams = 1.0 / 10000.0 ** (np.arange(0, D, 2, dtype=np.float32) / D)
    pre = pos[:, None] * inv_lams[None, :]
    emb = np.concatenate([np.sin(pre), np.cos(pre)], axis=-1)
    emb = np.flip(emb, axis=0)
    embT = np.zeros((D, XLCOLS), np.float32)
    embT[:, :T] = emb.T
    t["embT"] = embT.astype(BF)
    return t


def _host_percore(c):
    d = {}
    onehot = np.zeros((128, 8), np.float32)
    onehot[:, c] = 1.0
    d["sel"] = onehot
    offq, offwv, offxlr, offpad = [], [], [], []
    CONTRIBQ, TILEQ = 16 * 128 * 128, 128 * 128
    CONTRIBW, TILEW = 16 * 128 * V1, 128 * V1
    for r in range(16):
        rp = r % 8
        if r < 8:
            offq.append(r * CONTRIBQ + c * TILEQ)
        else:
            u = r - 8
            offq.append((7 - u) * CONTRIBQ + (8 + c) * TILEQ)
        I0 = (8 * rp + c) * 128
        offxlr.append((T - 1) - (I0 + 127))
        offpad.append(0)
    for j in range(8):
        offwv.append(j * CONTRIBW + c * TILEW)
    for j in range(8):
        offwv.append(j * CONTRIBW + (8 + (7 - c)) * TILEW)
    d["offq"] = np.array(offq, np.int32).reshape(1, 16)
    d["offwv"] = np.array(offwv, np.int32).reshape(1, 16)
    d["offxlr"] = np.array(offxlr, np.int32).reshape(1, 16)
    d["offpad"] = np.array(offpad, np.int32).reshape(1, 16)
    gi = np.zeros((128, 16), np.float32)
    for r in range(16):
        I0 = (8 * (r % 8) + c) * 128
        gi[:, r] = I0 + np.arange(128)
    d["gi"] = gi
    return d


def _shard_inputs(inp):
    tabs = _host_tables()
    x = np.ascontiguousarray(inp["L0_attn_x"].astype(np.float32))
    w = {}
    w["l0f_w1T"] = inp["L0_ffn_w1"].T
    w["l0f_w2T"] = inp["L0_ffn_w2"].T
    w["l0f_b1"] = inp["L0_ffn_b1"].reshape(1, -1)
    w["l0f_b2"] = inp["L0_ffn_b2"].reshape(1, -1)
    w["l0f_ln1w"] = inp["L0_ffn_ln1_w"].reshape(1, -1)
    w["l0f_ln1b"] = inp["L0_ffn_ln1_b"].reshape(1, -1)
    w["l0f_ln2w"] = inp["L0_ffn_ln2_w"].reshape(1, -1)
    w["l0f_ln2b"] = inp["L0_ffn_ln2_b"].reshape(1, -1)
    w["l0a_WqT"] = inp["L0_attn_Wq"].T
    w["l0a_WkT"] = inp["L0_attn_Wk"].T
    w["l0a_WvT"] = inp["L0_attn_Wv"].T
    w["l0a_WgT"] = inp["L0_attn_Wg"].T
    w["l0a_WabT"] = np.concatenate([inp["L0_attn_Wa"], inp["L0_attn_Wb"]], 0).T
    w["l0a_WoT"] = inp["L0_attn_Wo"].T
    w["l0a_bq"] = inp["L0_attn_bq"].reshape(1, -1)
    w["l0a_bk"] = inp["L0_attn_bk"].reshape(1, -1)
    w["l0a_bv"] = inp["L0_attn_bv"].reshape(1, -1)
    w["l0a_bg"] = inp["L0_attn_bg"].reshape(1, -1)
    w["l0a_bab"] = np.concatenate(
        [inp["L0_attn_ba"], inp["L0_attn_bb"]]).reshape(1, 2)
    w["l0a_bo"] = inp["L0_attn_bo"].reshape(1, -1)
    w["l1f_wgT"] = inp["L1_ffn_w_gate"].T
    w["l1f_wuT"] = inp["L1_ffn_w_up"].T
    w["l1f_wdT"] = inp["L1_ffn_w_down"].T
    w["l1a_gln"] = inp["L1_attn_g_ln"].reshape(1, -1)
    w["l1a_Wq"] = inp["L1_attn_W_q"]
    w["l1a_Wk"] = inp["L1_attn_W_k"]
    w["l1a_Wv"] = inp["L1_attn_W_v"]
    w["l1a_Wg"] = inp["L1_attn_W_g"]
    w["l1a_Wr"] = inp["L1_attn_W_r"]
    w["l1a_xlu"] = inp["L1_attn_xl_u"].reshape(-1, 1)
    w["l1a_xlv"] = inp["L1_attn_xl_v"].reshape(-1, 1)
    w["l1a_Wo"] = inp["L1_attn_W_o"]
    in_maps = []
    for c in range(NCORE):
        m = {}
        b0 = x[0, 1024 * c:1024 * (c + 1)]
        b1 = x[1, 1024 * (7 - c):1024 * (8 - c)]
        m["x_shard"] = np.ascontiguousarray(np.concatenate([b0, b1], 0))
        for k2, v in w.items():
            m[k2] = np.ascontiguousarray(v)
        for k2, v in tabs.items():
            m[k2] = np.ascontiguousarray(v)
        m.update(_host_percore(c))
        in_maps.append(m)
    return in_maps


def _unshard_output(results):
    y = np.zeros((B, T, D), np.float32)
    for c in range(NCORE):
        ys = results[c]["y_shard"]
        y[0, 1024 * c:1024 * (c + 1)] = ys[:1024]
        y[1, 1024 * (7 - c):1024 * (8 - c)] = ys[1024:]
    return y



class St:
    pass


def declare_io(nc):
    st = St()
    st.nc = nc
    di = {}

    def inp(name, shape, dt=F32):
        di[name] = nc.dram_tensor(name, list(shape), dt, kind="ExternalInput")

    inp("x_shard", (TPC, D))
    inp("l0f_w1T", (D, H)); inp("l0f_w2T", (H, D))
    inp("l0f_b1", (1, H)); inp("l0f_b2", (1, D))
    inp("l0f_ln1w", (1, D)); inp("l0f_ln1b", (1, D))
    inp("l0f_ln2w", (1, H)); inp("l0f_ln2b", (1, H))
    inp("l0a_WqT", (D, C)); inp("l0a_WkT", (D, C)); inp("l0a_WvT", (D, C))
    inp("l0a_WgT", (D, C)); inp("l0a_WabT", (D, 2)); inp("l0a_WoT", (C, D))
    inp("l0a_bq", (1, C)); inp("l0a_bk", (1, C)); inp("l0a_bv", (1, C))
    inp("l0a_bg", (1, C)); inp("l0a_bab", (1, 2)); inp("l0a_bo", (1, D))
    inp("l1f_wgT", (D, H)); inp("l1f_wuT", (D, H)); inp("l1f_wdT", (H, D))
    inp("l1a_gln", (1, D))
    inp("l1a_Wq", (D, C)); inp("l1a_Wk", (D, C))
    inp("l1a_Wv", (D, V1)); inp("l1a_Wg", (D, V1)); inp("l1a_Wr", (D, C))
    inp("l1a_xlu", (C, 1)); inp("l1a_xlv", (C, 1)); inp("l1a_Wo", (V1, D))
    inp("ident_bf", (128, 128), BF16)
    inp("utri", (128, 128)); inp("neg_utri", (128, 128))
    inp("stril", (128, 128))
    inp("ones_row", (1, 512)); inp("neg30", (128, 1152), BF16)
    inp("embT", (D, XLCOLS), BF16)
    inp("sel", (128, 8))
    inp("offq", (1, 16), I32); inp("offwv", (1, 16), I32)
    inp("offxlr", (1, 16), I32); inp("offpad", (1, 16), I32)
    inp("gi", (128, 16)); inp("iota512", (128, 512))
    st.di = di
    st.y = nc.dram_tensor("y_shard", [TPC, D], F32, kind="ExternalOutput")
    if os.environ.get("MK_DBG", "0") == "1":
        st.dbg1 = nc.dram_tensor("dbg1", [128, 4096], F32,
                                 kind="ExternalOutput")
        st.dbg2 = nc.dram_tensor("dbg2", [128, 4096], F32,
                                 kind="ExternalOutput")
        st.dbg3 = nc.dram_tensor("dbg3", [128, 64], F32,
                                 kind="ExternalOutput")

    st.wq1 = nc.dram_tensor("wq1", [D, H], BF16)
    st.wq2 = nc.dram_tensor("wq2", [H, D], BF16)
    st.wqg = nc.dram_tensor("wqg", [D, H], BF16)
    st.wqu = nc.dram_tensor("wqu", [D, H], BF16)
    st.wqd = nc.dram_tensor("wqd", [H, D], BF16)
    st.xlrT = nc.dram_tensor("xlrT", [C * XLCOLS], BF16)
    st.stripd = nc.dram_tensor("stripd", [2 * 128 * STRIP_MAX], BF16)
    st.xfm_sp = nc.dram_tensor("xfm_sp", [128, 8 * TPC], BF16)
    st.st_in = nc.dram_tensor("st_in", [4, 128, 128], F32)
    st.st_out = nc.dram_tensor("st_out", [NCORE, 4, 128, 128], F32)
    st.q_in = nc.dram_tensor("q_in", [16, 128, 128], BF16)
    st.q_out = nc.dram_tensor("q_out", [NCORE * 16 * 128 * 128], BF16)
    st.kv_in = nc.dram_tensor("kv_in", [KVL], BF16)
    st.kv_out = nc.dram_tensor("kv_out", [NCORE * KVL], BF16)
    st.wv_in = nc.dram_tensor("wv_in", [16, 128, V1], BF16)
    st.wv_out = nc.dram_tensor("wv_out", [NCORE * 16 * 128 * V1], BF16)
    st.db = {nm: Buf(nm) for nm in
             ["wq1", "wq2", "wqg", "wqu", "wqd", "xlrT", "strip0", "strip1",
              "st_in", "st_out", "q_in", "q_out", "kv_in", "kv_out", "wv_in",
              "wv_out", "y", "xfm_sp"]}
    return st



def _fd(ap):
    try:
        dims = list(ap.ap)
    except Exception:
        return 10 ** 9
    n = 1
    for st_, cnt in dims[1:]:
        n *= cnt
    if len(dims) == 1:
        n = dims[0][1]
    return n


def _small(out_ap):
    return _fd(out_ap) <= 32


def ld_dram(st, q, dst_th, src_ap, extra_reads=()):
    st.S.op(q, lambda e, d=dst_th.ap, s=src_ap: e.dma_start(out=d, in_=s),
            reads=list(extra_reads), writes=[dst_th.buf], dma=True)


def st_dram(st, q, dst_ap, src_th, dbuf, extra_reads=(), dsem=None):
    st.S.op(q, lambda e, d=dst_ap, s=src_th.ap: e.dma_start(out=d, in_=s),
            reads=[src_th.buf] + list(extra_reads), writes=[dbuf], dma=True,
            dsem=dsem)


def mm(st, out_th, lhsT_th, rhs_th, start, stop):
    st.S.op("tensor",
            lambda e, o=out_th.ap, l=lhsT_th.ap, r=rhs_th.ap:
            e.matmul(o, l, r, start=start, stop=stop),
            reads=[lhsT_th.buf, rhs_th.buf], writes=[out_th.buf])


def tpose(st, out_th, in_th):
    idb = st.sb["ident_bf"]
    st.S.op("tensor",
            lambda e, o=out_th.ap, i=in_th.ap, d=idb.ap:
            e.matmul(o, i, d, start=True, stop=True, is_transpose=True),
            reads=[in_th.buf, idb.buf], writes=[out_th.buf])


def ts(st, out_th, in_th, s1, op0, s2=None, op1=ALU.bypass, accum=None):
    r = [in_th.buf]
    a1 = s1.ap if isinstance(s1, TH) else (None if s1 is None else float(s1))
    if isinstance(s1, TH):
        r.append(s1.buf)
    a2 = s2.ap if isinstance(s2, TH) else (None if s2 is None else float(s2))
    if isinstance(s2, TH):
        r.append(s2.buf)
    wr = [out_th.buf]
    aac = None
    if accum is not None:
        aac = accum.ap
        wr.append(accum.buf)
    st.S.op("vector",
            lambda e, o=out_th.ap, i=in_th.ap, x1=a1, x2=a2, ac=aac:
            e.tensor_scalar(o, i, x1, x2, op0, op1, accum_out=ac),
            reads=r, writes=wr, fence=_small(out_th.ap) or aac is not None)


def tt(st, out_th, a_th, b_th, op):
    st.S.op("vector",
            lambda e, o=out_th.ap, a=a_th.ap, b=b_th.ap:
            e.tensor_tensor(o, a, b, op),
            reads=[a_th.buf, b_th.buf], writes=[out_th.buf],
            fence=_small(out_th.ap))


def stt(st, out_th, in0_th, s, op0, op1, in1_th):
    r = [in0_th.buf, in1_th.buf]
    a = s.ap if isinstance(s, TH) else float(s)
    if isinstance(s, TH):
        r.append(s.buf)
    st.S.op("vector",
            lambda e, o=out_th.ap, i0=in0_th.ap, x=a, i1=in1_th.ap:
            e.scalar_tensor_tensor(o, i0, x, i1, op0, op1),
            reads=r, writes=[out_th.buf])


def act(st, out_th, in_th, func, bias=0.0, scale=1.0, accum=None):
    r = [in_th.buf]
    ab = bias.ap if isinstance(bias, TH) else float(bias)
    if isinstance(bias, TH):
        r.append(bias.buf)
    asc = scale.ap if isinstance(scale, TH) else float(scale)
    if isinstance(scale, TH):
        r.append(scale.buf)
    wr = [out_th.buf]
    aac = None
    if accum is not None:
        aac = accum.ap
        wr.append(accum.buf)
    st.S.op("scalar",
            lambda e, o=out_th.ap, i=in_th.ap, bb=ab, ss=asc, ac=aac:
            e.activation(o, i, func, bias=bb, scale=ss, accum_out=ac),
            reads=r, writes=wr,
            fence=_small(out_th.ap) or aac is not None)
    if accum is not None:
        st.S.op("scalar",
                lambda e, o=accum.ap: e.mul(o, o, 1.0),
                reads=[accum.buf], writes=[accum.buf], fence=True)


def copy(st, out_th, in_th):
    st.S.op("vector", lambda e, o=out_th.ap, i=in_th.ap: e.tensor_copy(o, i),
            reads=[in_th.buf], writes=[out_th.buf],
            fence=_small(out_th.ap))


def red(st, out_th, in_th, op, absval=None):
    st.S.op("vector",
            lambda e, o=out_th.ap, i=in_th.ap:
            e.tensor_reduce(o, i, mybir.AxisListType.X, op,
                            apply_absolute_value=absval),
            reads=[in_th.buf], writes=[out_th.buf], fence=True)


def recip(st, out_th, in_th):
    act(st, out_th, in_th, AF.Ln)
    act(st, out_th, out_th, AF.Exp, scale=-1.0)


def memset(st, th, val):
    st.S.op("vector", lambda e, o=th.ap, v=float(val): e.memset(o, v),
            reads=[], writes=[th.buf], fence=_small(th.ap))


def pbs(st, i, n, off=0):
    p = st.PB[i]
    return TH(p.ap[:, off:off + n], p.buf)


def pbts(st, i, n, off=0):
    p = st.PBT[i]
    return TH(p.ap[:, off:off + n], p.buf)


def bcast_dram_row(st, name, width, out_th):
    lvl = int(os.environ.get("MK_BCAST_LVL", "3"))
    ones = st.sb["ones_row"]
    stage = st.rowstage
    src = st.di[name]
    for i in range((width + 511) // 512):
        w = min(512, width - i * 512)
        if lvl >= 1:
            ld_dram(st, "sync", TH(stage.ap[0:1, 0:w], stage.buf),
                    src[0:1, i * 512:i * 512 + w])
        p = pbs(st, 5, w)
        if lvl >= 2:
            st.S.op("tensor",
                    lambda e, o=p.ap, l=ones.ap[0:1, 0:128],
                    r=stage.ap[0:1, 0:w]:
                    e.matmul(o, l, r, start=True, stop=True),
                    reads=[ones.buf, stage.buf], writes=[p.buf])
        if lvl >= 3:
            st.S.op("scalar",
                    lambda e, o=out_th.ap[:, i * 512:i * 512 + w], ip=p.ap:
                    e.copy(o, ip),
                    reads=[p.buf], writes=[out_th.buf])


def bias_mm(st, p_th, row_th, n, row_off=0, transpose_roles=False):
    ones = st.sb["ones_row"]
    if transpose_roles:
        st.S.op("tensor",
                lambda e, o=p_th.ap, l=row_th.ap[0:1, row_off:row_off + 128],
                r=ones.ap[0:1, 0:n]:
                e.matmul(o, l, r, start=False, stop=True),
                reads=[row_th.buf, ones.buf], writes=[p_th.buf])
    else:
        st.S.op("tensor",
                lambda e, o=p_th.ap, l=ones.ap[0:1, 0:128],
                r=row_th.ap[0:1, row_off:row_off + n]:
                e.matmul(o, l, r, start=False, stop=True),
                reads=[row_th.buf, ones.buf], writes=[p_th.buf])


def batch_rstd(st, var_th, n, eps, out_th, scale_pre=1.0):
    tmp = st.asm.alloc(n, "rstd_tmp")
    ts(st, tmp, var_th, np.float32(scale_pre), ALU.mult, np.float32(eps),
       ALU.add)
    act(st, tmp, tmp, AF.Ln)
    act(st, out_th, tmp, AF.Exp, scale=-0.5)


def stats_all(st, width, tiles, mean_th, rstd_th, eps):
    n = len(tiles)
    sx = st.asm.alloc(n, "sx")
    sxx = st.asm.alloc(n, "sxx")
    junk = TH(st.junk.ap[:, 0:width], st.junk.buf)
    for i, t in enumerate(tiles):
        red(st, sx[:, i:i + 1], t, ALU.add)
        act(st, junk, t, AF.Square, accum=sxx[:, i:i + 1])
    ts(st, mean_th, sx, np.float32(1.0 / width), ALU.mult)
    ex2 = st.asm.alloc(n, "ex2")
    ts(st, ex2, sxx, np.float32(1.0 / width), ALU.mult)
    m2 = st.asm.alloc(n, "m2")
    tt(st, m2, mean_th, mean_th, ALU.mult)
    tt(st, ex2, ex2, m2, ALU.subtract)
    batch_rstd(st, ex2, n, eps, rstd_th)
    st.asm.pos -= 4 * n


def ffn_common_start(st):
    st.mean = st.asm.alloc(16, "mean")
    st.rstd = st.asm.alloc(16, "rstd")
    if os.environ.get("MK_STATS", "1") == "1":
        stats_all(st, D, st.xt, st.mean, st.rstd, 1e-5)
    else:
        memset(st, st.mean, 0.0)
        memset(st, st.rstd, 1.0)


def quant_chain(st, src_th, width, xq_th, isx_col, tscr):
    mx = st.asm.alloc(1, "mx")
    red(st, mx, src_th, ALU.max, absval=True)
    ts(st, mx, mx, np.float32(EPS5), ALU.max)
    sxc = st.asm.alloc(1, "sxc")
    recip(st, sxc, mx)
    ts(st, sxc, sxc, np.float32(127.0), ALU.mult)
    ts(st, isx_col, mx, np.float32(1.0 / 127.0), ALU.mult)
    t = TH(tscr.ap[:, 0:width], tscr.buf)
    ts(st, t, src_th, sxc, ALU.mult, MAGIC, ALU.add)
    ts(st, t, t, MAGIC, ALU.subtract, np.float32(127.0), ALU.min)
    ts(st, xq_th, t, np.float32(-128.0), ALU.max)
    st.asm.pos -= 2


def quant_weight(st, src, dstd, dbuf, rows, cols, sw_col, swr_col):
    a, f = st.abf, st.af32
    a.push(); f.push()
    nt = rows // 128
    csz = min(1024, cols)
    nch = cols // csz
    wtile = [f.alloc(csz, f"qw_w{j}") for j in range(2)]
    tq = f.alloc(csz, "qw_t")
    qb = [a.alloc(csz, f"qw_q{j}") for j in range(2)]
    absac = st.asm.alloc(nt, "absac")
    junk = TH(st.junk.ap[:, 0:1024], st.junk.buf)
    acc1 = st.asm.alloc(1, "acc1")
    ld_dram(st, "sync", wtile[0], src[0:128, 0:csz])
    for i in range(nt):
        for ch in range(nch):
            k = i * nch + ch
            if k + 1 < nt * nch:
                i2, c2 = divmod(k + 1, nch)
                ld_dram(st, "sync", wtile[(k + 1) % 2],
                        src[i2 * 128:(i2 + 1) * 128, c2 * csz:(c2 + 1) * csz])
            wt = wtile[k % 2]
            for off in range(0, csz, 1024):
                for sgn in (1.0, -1.0):
                    act(st, junk, TH(wt.ap[:, off:off + 1024], wt.buf),
                        AF.Relu, scale=sgn, accum=acc1)
                    if ch == 0 and off == 0 and sgn == 1.0:
                        copy(st, TH(absac.ap[:, i:i + 1], absac.buf), acc1)
                    else:
                        tt(st, TH(absac.ap[:, i:i + 1], absac.buf),
                           TH(absac.ap[:, i:i + 1], absac.buf), acc1, ALU.add)
    onescol = TH(st.sb["utri"].ap[:, 127:128], st.sb["utri"].buf)
    pt = TH(st.PB[5].ap[0:1, 0:nt], st.PB[5].buf)
    st.S.op("tensor", lambda e, o=pt.ap, l=onescol.ap, r=absac.ap:
            e.matmul(o, l, r, start=True, stop=True),
            reads=[onescol.buf, absac.buf], writes=[st.PB[5].buf])
    tot = st.asm.alloc(1, "tot")
    tot1 = TH(tot.ap[0:1, 0:1], tot.buf)
    st.S.op("vector", lambda e, o=tot1.ap, i=pt.ap:
            e.tensor_reduce(o, i, mybir.AxisListType.X, ALU.add),
            reads=[st.PB[5].buf], writes=[tot.buf], fence=True)
    ts(st, tot1, tot1, np.float32(1.0 / (rows * cols)), ALU.mult,
       np.float32(EPS5), ALU.max)
    swr1 = st.asm.alloc(1, "swr1")
    sw1 = st.asm.alloc(1, "sw1")
    copy(st, TH(swr1.ap[0:1, 0:1], swr1.buf), tot1)
    recip(st, TH(sw1.ap[0:1, 0:1], sw1.buf), tot1)
    ones = st.sb["ones_row"]
    for src1, dstc in ((sw1, sw_col), (swr1, swr_col)):
        p = pbs(st, 5, 1)
        st.S.op("tensor",
                lambda e, o=p.ap, l=ones.ap[0:1, 0:128],
                r=src1.ap[0:1, 0:1]:
                e.matmul(o, l, r, start=True, stop=True),
                reads=[ones.buf, src1.buf], writes=[p.buf])
        copy(st, dstc, p)
    ld_dram(st, "sync", wtile[0], src[0:128, 0:csz])
    for i in range(nt):
        for ch in range(nch):
            k = i * nch + ch
            if k + 1 < nt * nch:
                i2, c2 = divmod(k + 1, nch)
                ld_dram(st, "sync", wtile[(k + 1) % 2],
                        src[i2 * 128:(i2 + 1) * 128, c2 * csz:(c2 + 1) * csz])
            ts(st, tq, wtile[k % 2], sw_col, ALU.mult, MAGIC, ALU.add)
            ts(st, tq, tq, MAGIC, ALU.subtract, np.float32(1.0), ALU.min)
            ts(st, qb[k % 2], tq, np.float32(-1.0), ALU.max)
            st_dram(st, "gpsimd",
                    dstd[i * 128:(i + 1) * 128, ch * csz:(ch + 1) * csz],
                    qb[k % 2], dbuf)
    st.asm.pos -= nt + 4
    a.pop(); f.pop()



def ffn0_phase(st, first):
    S, a, f = st.S, st.abf, st.af32
    a.push(); f.push(); st.asm.push()
    di = st.di
    if first and os.environ.get("MK_FFN0_QW", "1") == "1":
        quant_weight(st, di["l0f_w1T"], st.wq1, st.db["wq1"], D, H,
                     st.wsc[:, 0:1], st.wsc[:, 1:2])
        quant_weight(st, di["l0f_w2T"], st.wq2, st.db["wq2"], H, D,
                     st.wsc[:, 2:3], st.wsc[:, 3:4])
    swr1 = TH(st.wsc.ap[:, 1:2], st.wsc.buf)
    swr2 = TH(st.wsc.ap[:, 3:4], st.wsc.buf)
    ffn_common_start(st)
    bln1w = a.alloc(D, "bln1w"); bln1b = a.alloc(D, "bln1b")
    bln2w = a.alloc(H, "bln2w"); bln2b = a.alloc(H, "bln2b")
    if os.environ.get("MK_BCAST", "1") == "1":
        bcast_dram_row(st, "l0f_ln1w", D, bln1w)
        bcast_dram_row(st, "l0f_ln1b", D, bln1b)
        bcast_dram_row(st, "l0f_ln2w", H, bln2w)
        bcast_dram_row(st, "l0f_ln2b", H, bln2b)
    zb = [a.alloc(D, f"z{i}") for i in range(2)]
    hsup = a.alloc(4 * H, "hsup")
    xqfm = a.alloc(8 * 512, "xqfm")
    ghqT = xqfm
    wbuf = [a.alloc(8 * 1024, f"wb{i}") for i in range(2)]
    tscr = f.alloc(H, "tscr")
    uscr = f.alloc(512, "uscr")
    uscr2 = f.alloc(512, "uscr2")
    PB, PBT = st.PB, st.PBT

    wq1v = st.wq1.rearrange("(dt p) h -> p dt h", p=128)
    wq2v = st.wq2.rearrange("(ht p) dd -> p ht dd", p=128)

    for sbk in range(int(os.environ.get("MK_FFN0_NSB", "4"))):
        st.asm.push()
        tl = [sbk * 4 + i for i in range(4)]
        m2c = st.asm.alloc(4, "m2c")
        v2c = st.asm.alloc(4, "v2c")
        sxi = st.asm.alloc(4, "sxi")
        sc1 = st.asm.alloc(4, "sc1")
        sc2 = st.asm.alloc(4, "sc2")
        shs = st.asm.alloc(4, "shs")
        sqs = st.asm.alloc(4, "sqs")
        junkD = TH(st.junk.ap[:, 0:D], st.junk.buf)
        for i, tt_ in enumerate(tl):
            z = zb[i % 2]
            ts(st, z, st.xt[tt_], st.mean[:, tt_:tt_ + 1], ALU.subtract,
               st.rstd[:, tt_:tt_ + 1], ALU.mult)
            red(st, TH(m2c.ap[:, i:i + 1], m2c.buf), z, ALU.add)
            act(st, junkD, z, AF.Square, accum=v2c[:, i:i + 1])
        ts(st, m2c, m2c, np.float32(1.0 / D), ALU.mult)
        ts(st, v2c, v2c, np.float32(1.0 / D), ALU.mult)
        mm2_ = st.asm.alloc(4, "mm2_")
        tt(st, mm2_, m2c, m2c, ALU.mult)
        tt(st, v2c, v2c, mm2_, ALU.subtract)
        batch_rstd(st, v2c, 4, 1e-5, v2c)
        for i, tt_ in enumerate(tl):
            z = zb[i % 2]
            ts(st, z, st.xt[tt_], st.mean[:, tt_:tt_ + 1], ALU.subtract,
               st.rstd[:, tt_:tt_ + 1], ALU.mult)
            ts(st, z, z, m2c[:, i:i + 1], ALU.subtract, v2c[:, i:i + 1],
               ALU.mult)
            tt(st, z, z, bln1w, ALU.mult)
            tt(st, z, z, bln1b, ALU.add)
            quant_chain(st, z, D, z, sxi[:, i:i + 1], tscr)
            for dt in range(8):
                tpose(st, pbts(st, 0, 128, dt * 128),
                      z[:, dt * 128:(dt + 1) * 128])
            dst = TH(xqfm.ap.rearrange("p (dt q) -> p dt q", q=512)
                     [:, :, i * 128:(i + 1) * 128], xqfm.buf)
            copy(st, dst, TH(PBT[0].ap.rearrange("p (dt q) -> p dt q", q=128),
                             PBT[0].buf))
            tt(st, TH(sc1.ap[:, i:i + 1], sc1.buf),
               TH(sxi.ap[:, i:i + 1], sxi.buf), swr1, ALU.mult)
        chs = st.asm.alloc(1, "chs")
        junk5 = TH(st.junk.ap[:, 0:512], st.junk.buf)
        if sbk == 0:
            st.bb1w = a.alloc(1024, "bb1w")
        bb1w = st.bb1w
        bb2 = TH(bb1w.ap[:, 0:512], bb1w.buf)
        for hcp in range(4):
            wb = wbuf[hcp % 2]
            ld_dram(st, "sync", wb,
                    wq1v[:, :, hcp * 1024:(hcp + 1) * 1024])
            ld_dram(st, "sync", TH(st.rowstage.ap[0:1, 0:512],
                                   st.rowstage.buf),
                    di["l0f_b1"][0:1, hcp * 1024:hcp * 1024 + 512])
            p5 = pbs(st, 5, 512)
            st.S.op("tensor", lambda e, o=p5.ap,
                    l=st.sb["ones_row"].ap[0:1, 0:128],
                    r=st.rowstage.ap[0:1, 0:512]:
                    e.matmul(o, l, r, start=True, stop=True),
                    reads=[st.sb["ones_row"].buf, st.rowstage.buf],
                    writes=[p5.buf])
            copy(st, TH(bb1w.ap[:, 0:512], bb1w.buf), p5)
            ld_dram(st, "sync", TH(st.rowstage.ap[0:1, 0:512],
                                   st.rowstage.buf),
                    di["l0f_b1"][0:1, hcp * 1024 + 512:hcp * 1024 + 1024])
            p5 = pbs(st, 5, 512)
            st.S.op("tensor", lambda e, o=p5.ap,
                    l=st.sb["ones_row"].ap[0:1, 0:128],
                    r=st.rowstage.ap[0:1, 0:512]:
                    e.matmul(o, l, r, start=True, stop=True),
                    reads=[st.sb["ones_row"].buf, st.rowstage.buf],
                    writes=[p5.buf])
            copy(st, TH(bb1w.ap[:, 512:1024], bb1w.buf), p5)
            for i in range(4):
                pb0 = pbs(st, 0 if i % 2 == 0 else 2, 512)
                pb1 = pbs(st, 1 if i % 2 == 0 else 3, 512)
                for dt in range(8):
                    l = TH(xqfm.ap[:, dt * 512 + i * 128:
                                   dt * 512 + (i + 1) * 128], xqfm.buf)
                    mm(st, pb0, l,
                       TH(wb.ap[:, dt * 1024:dt * 1024 + 512], wb.buf),
                       start=(dt == 0), stop=(dt == 7))
                    mm(st, pb1, l,
                       TH(wb.ap[:, dt * 1024 + 512:(dt + 1) * 1024], wb.buf),
                       start=(dt == 0), stop=(dt == 7))
                for sub, pbk, usc in ((0, pb0, uscr), (1, pb1, uscr2)):
                    hc = hcp * 2 + sub
                    usct = TH(usc.ap[:, 0:512], usc.buf)
                    stt(st, usct, pbk, TH(sc1.ap[:, i:i + 1], sc1.buf),
                        ALU.mult, ALU.add,
                        TH(bb1w.ap[:, sub * 512:(sub + 1) * 512], bb1w.buf))
                    hdst = TH(hsup.ap[:, i * H + hc * 512:
                                      i * H + (hc + 1) * 512], hsup.buf)
                    act(st, hdst, usct, AF.Gelu, accum=chs)
                    if hc == 0:
                        copy(st, TH(shs.ap[:, i:i + 1], shs.buf), chs)
                    else:
                        tt(st, TH(shs.ap[:, i:i + 1], shs.buf),
                           TH(shs.ap[:, i:i + 1], shs.buf), chs, ALU.add)
                    act(st, junk5, hdst, AF.Square, accum=chs)
                    if hc == 0:
                        copy(st, TH(sqs.ap[:, i:i + 1], sqs.buf), chs)
                    else:
                        tt(st, TH(sqs.ap[:, i:i + 1], sqs.buf),
                           TH(sqs.ap[:, i:i + 1], sqs.buf), chs, ALU.add)
        if sbk == 0 and os.environ.get("MK_DBG", "0") == "1":
            st_dram(st, "gpsimd", st.dbg1[:, :], xqfm, Buf("dbg1"))
            st_dram(st, "gpsimd", st.dbg2[:, :],
                    TH(hsup.ap[:, 0:H], hsup.buf), Buf("dbg2"))
            d3 = f.alloc(64, "d3scr")
            copy(st, TH(d3.ap[:, 0:12], d3.buf), st.wsc)
            copy(st, TH(d3.ap[:, 12:28], d3.buf), st.mean)
            copy(st, TH(d3.ap[:, 28:44], d3.buf), st.rstd)
            copy(st, TH(d3.ap[:, 44:48], d3.buf), m2c)
            copy(st, TH(d3.ap[:, 48:52], d3.buf), v2c)
            copy(st, TH(d3.ap[:, 52:56], d3.buf), sc1)
            st_dram(st, "gpsimd", st.dbg3[:, :], d3, Buf("dbg3"))
        ts(st, shs, shs, np.float32(1.0 / H), ALU.mult)
        ts(st, sqs, sqs, np.float32(1.0 / H), ALU.mult)
        mh2 = st.asm.alloc(4, "mh2")
        tt(st, mh2, shs, shs, ALU.mult)
        tt(st, sqs, sqs, mh2, ALU.subtract)
        batch_rstd(st, sqs, 4, 1e-5, sqs)
        shi = st.asm.alloc(4, "shi")
        for i in range(4):
            hv = TH(hsup.ap[:, i * H:(i + 1) * H], hsup.buf)
            ts(st, hv, hv, shs[:, i:i + 1], ALU.subtract, sqs[:, i:i + 1],
               ALU.mult)
            tt(st, hv, hv, bln2w, ALU.mult)
            tt(st, hv, hv, bln2b, ALU.add)
            quant_chain(st, hv, H, hv, shi[:, i:i + 1], tscr)
            tt(st, TH(sc2.ap[:, i:i + 1], sc2.buf),
               TH(shi.ap[:, i:i + 1], shi.buf), swr2, ALU.mult)
        for dc in range(2):
            ld_dram(st, "sync", TH(st.rowstage.ap[0:1, 0:512],
                                   st.rowstage.buf),
                    di["l0f_b2"][0:1, dc * 512:(dc + 1) * 512])
            p5 = pbs(st, 5, 512)
            st.S.op("tensor", lambda e, o=p5.ap,
                    l=st.sb["ones_row"].ap[0:1, 0:128],
                    r=st.rowstage.ap[0:1, 0:512]:
                    e.matmul(o, l, r, start=True, stop=True),
                    reads=[st.sb["ones_row"].buf, st.rowstage.buf],
                    writes=[p5.buf])
            copy(st, bb2, p5)
            wb2 = TH(wbuf[dc % 2].ap[:, 0:4096], wbuf[dc % 2].buf)
            for htg in range(4):
                ld_dram(st, "sync", wb2,
                        wq2v[:, htg * 8:(htg + 1) * 8,
                             dc * 512:(dc + 1) * 512])
                for i in range(4):
                    for hh in range(8):
                        ht = htg * 8 + hh
                        tpose(st, pbts(st, 1, 128, hh * 128),
                              TH(hsup.ap[:, i * H + ht * 128:
                                         i * H + (ht + 1) * 128], hsup.buf))
                    dst = TH(ghqT.ap.rearrange("p (g q) -> p g q", q=512)
                             [:, :, i * 128:(i + 1) * 128], ghqT.buf)
                    copy(st, dst, TH(
                        PBT[1].ap.rearrange("p (g q) -> p g q", q=128),
                        PBT[1].buf))
                for i in range(4):
                    pbk = pbs(st, 2 + i, 512)
                    for hh in range(8):
                        mm(st, pbk,
                           TH(ghqT.ap[:, hh * 512 + i * 128:
                                      hh * 512 + (i + 1) * 128], ghqT.buf),
                           TH(wb2.ap[:, hh * 512:(hh + 1) * 512], wb2.buf),
                           start=(htg == 0 and hh == 0),
                           stop=(htg == 3 and hh == 7))
            for i, tt_ in enumerate(tl):
                pbk = pbs(st, 2 + i, 512)
                stt(st, uscr, pbk, TH(sc2.ap[:, i:i + 1], sc2.buf), ALU.mult,
                    ALU.add, bb2)
                xd = st.xt[tt_][:, dc * 512:(dc + 1) * 512]
                stt(st, xd, uscr, np.float32(0.5), ALU.mult, ALU.add, xd)
        st.asm.pop()
    st.asm.pop(); a.pop(); f.pop()



def att0_phase(st):
    S, a, f = st.S, st.abf, st.af32
    a.push(); f.push(); st.asm.push()
    di = st.di
    PB, PBT = st.PB, st.PBT
    ones = st.sb["ones_row"]
    utri = st.sb["utri"]
    stril = st.sb["stril"]
    neg_utri = st.sb["neg_utri"]
    ffn_common_start(st)

    def apc(name, arena=a):
        return [arena.alloc(128, f"{name}{i}") for i in range(NTILE)]
    qT, k_n, bK, bV, WtT, UT, DU, Rc, Ph, gate = (
        apc("qT"), apc("k_n"), apc("bK"), apc("bV"), apc("WtT"), apc("UT"),
        apc("DU"), apc("Rc"), apc("Ph"), apc("gate"))
    graw = apc("graw")
    GUt = apc("GUt")
    kraw = f.alloc(128, "kraw")
    zab = [f.alloc(2, f"zab{i}") for i in range(NTILE)]
    crow = [f.alloc(1, f"crow{i}") for i in range(NTILE)]
    crw = [f.alloc(128, f"crw{i}") for i in range(NTILE)]
    la = st.asm.alloc(16, "la")
    beta = st.asm.alloc(16, "beta")
    ssk = st.asm.alloc(16, "ssk")
    gam = [f.alloc(1, f"gam{i}") for i in range(NTILE)]
    gamL = [f.alloc(1, f"gamL{i}") for i in range(NTILE)]
    Dcol = [f.alloc(1, f"Dcol{i}") for i in range(NTILE)]
    woT = a.alloc(D, "woT")
    woT_f = f.alloc(D, "woTf")
    ld_dram(st, "sync", woT_f, di["l0a_WoT"][:, :])
    copy(st, woT, woT_f)
    f.pos -= D

    a.push(); f.push()
    xfm = a.alloc(8 * TPC, "xfm")
    zscr = a.alloc(D, "zscr")
    for tt_ in range(NTILE):
        ts(st, zscr, st.xt[tt_], st.mean[:, tt_:tt_ + 1], ALU.subtract,
           st.rstd[:, tt_:tt_ + 1], ALU.mult)
        for dt in range(8):
            tpose(st, pbts(st, 0, 128, dt * 128),
                  zscr[:, dt * 128:(dt + 1) * 128])
        dst = TH(xfm.ap.rearrange("p (dt q) -> p dt q", q=TPC)
                 [:, :, tt_ * 128:(tt_ + 1) * 128], xfm.buf)
        copy(st, dst, TH(PBT[0].ap.rearrange("p (dt q) -> p dt q", q=128),
                         PBT[0].buf))
    pw = {}
    for nm, wd in [("l0a_WqT", C), ("l0a_WkT", C), ("l0a_WvT", C),
                   ("l0a_WgT", C), ("l0a_WabT", 2)]:
        wf = f.alloc(wd * 8, "pwf")
        v = di[nm].rearrange("(dt p) w -> p dt w", p=128)
        ld_dram(st, "sync",
                TH(wf.ap.rearrange("p (dt w) -> p dt w", w=wd), wf.buf),
                v[:, :, :])
        wb_ = a.alloc(wd * 8, "pwb")
        copy(st, wb_, wf)
        pw[nm] = wb_
        f.pos -= wd * 8

    for ch in range(NTILE):
        xsl = TH(xfm.ap.rearrange("p (dt q) -> p dt q", q=TPC)
                 [:, :, ch * 128:(ch + 1) * 128], xfm.buf)
        p0 = pbs(st, 0, 128)
        for dt in range(8):
            mm(st, p0, TH(pw["l0a_WqT"].ap[:, dt * C:(dt + 1) * C],
                          pw["l0a_WqT"].buf),
               TH(xsl.ap[:, dt, :], xsl.buf), start=(dt == 0), stop=False)
        bias_mm(st, p0, st.sb["l0a_bq"], 128, transpose_roles=True)
        copy(st, qT[ch], p0)
        p1 = pbs(st, 1, 128)
        for dt in range(8):
            mm(st, p1, TH(xsl.ap[:, dt, :], xsl.buf),
               TH(pw["l0a_WkT"].ap[:, dt * C:(dt + 1) * C],
                  pw["l0a_WkT"].buf), start=(dt == 0), stop=False)
        bias_mm(st, p1, st.sb["l0a_bk"], 128)
        copy(st, kraw, p1)
        act(st, TH(st.junk.ap[:, 0:128], st.junk.buf), kraw, AF.Square,
            accum=ssk[:, ch:ch + 1])
        copy(st, k_n[ch], kraw)
        p2 = pbs(st, 2, 128)
        for dt in range(8):
            mm(st, p2, TH(xsl.ap[:, dt, :], xsl.buf),
               TH(pw["l0a_WvT"].ap[:, dt * C:(dt + 1) * C],
                  pw["l0a_WvT"].buf), start=(dt == 0), stop=False)
        bias_mm(st, p2, st.sb["l0a_bv"], 128)
        copy(st, bV[ch], p2)
        p3 = pbs(st, 3, 128)
        for dt in range(8):
            mm(st, p3, TH(xsl.ap[:, dt, :], xsl.buf),
               TH(pw["l0a_WgT"].ap[:, dt * C:(dt + 1) * C],
                  pw["l0a_WgT"].buf), start=(dt == 0), stop=False)
        bias_mm(st, p3, st.sb["l0a_bg"], 128)
        copy(st, graw[ch], p3)
        p4 = pbs(st, 4, 2)
        for dt in range(8):
            mm(st, p4, TH(xsl.ap[:, dt, :], xsl.buf),
               TH(pw["l0a_WabT"].ap[:, dt * 2:(dt + 1) * 2],
                  pw["l0a_WabT"].buf), start=(dt == 0), stop=False)
        bias_mm(st, p4, st.sb["l0a_bab"], 2)
        copy(st, zab[ch], p4)
    st.S.barrier()
    a.pop(); f.pop()
    for ch in range(NTILE):
        act(st, gate[ch], graw[ch], AF.Sigmoid)
        act(st, TH(beta.ap[:, ch:ch + 1], beta.buf),
            TH(zab[ch].ap[:, 1:2], zab[ch].buf), AF.Sigmoid)
    for ch in range(NTILE):
        act(st, TH(la.ap[:, ch:ch + 1], la.buf),
            TH(zab[ch].ap[:, 0:1], zab[ch].buf), AF.Sigmoid)
    act(st, la, la, AF.Ln)
    kn_s = st.asm.alloc(16, "kn_s")
    kn_t = st.asm.alloc(16, "kn_t")
    ts(st, kn_t, ssk, np.float32(1.0), ALU.mult, np.float32(1e-30), ALU.max)
    act(st, kn_t, kn_t, AF.Ln)
    act(st, kn_t, kn_t, AF.Exp, scale=-0.5)
    ts(st, kn_s, kn_t, np.float32(1e12), ALU.min)
    for ch in range(NTILE):
        ts(st, k_n[ch], k_n[ch], kn_s[:, ch:ch + 1], ALU.mult)
        ts(st, bK[ch], k_n[ch], TH(beta.ap[:, ch:ch + 1], beta.buf), ALU.mult)
        ts(st, bV[ch], bV[ch], TH(beta.ap[:, ch:ch + 1], beta.buf), ALU.mult)
    Gt_args = []
    for ch in range(NTILE):
        lac = TH(la.ap[:, ch:ch + 1], la.buf)
        p0 = pbs(st, 0, 1)
        st.S.op("tensor", lambda e, o=p0.ap, l=utri.ap, r=lac.ap:
                e.matmul(o, l, r, start=True, stop=True),
                reads=[utri.buf, la.buf], writes=[p0.buf])
        copy(st, crow[ch], p0)
        p1 = TH(PB[1].ap[0:1, 0:128], PB[1].buf)
        st.S.op("tensor", lambda e, o=p1.ap, l=lac.ap, r=utri.ap:
                e.matmul(o, l, r, start=True, stop=True),
                reads=[utri.buf, la.buf], writes=[PB[1].buf])
        cr1 = TH(crw[ch].ap[0:1, 0:128], crw[ch].buf)
        copy(st, cr1, p1)
        p2 = pbs(st, 2, 128)
        st.S.op("tensor", lambda e, o=p2.ap, l=ones.ap[0:1, 0:128], r=cr1.ap:
                e.matmul(o, l, r, start=True, stop=True),
                reads=[ones.buf, crw[ch].buf], writes=[p2.buf])
        garg = f.alloc(128, f"garg{ch}")
        ts(st, garg, p2, crow[ch], ALU.subtract, np.float32(0.0), ALU.min)
        Gt_args.append(garg)
    for ch in range(NTILE):
        act(st, GUt[ch], Gt_args[ch], AF.Exp)
        tt(st, GUt[ch], GUt[ch], utri, ALU.mult)
    f.pos -= NTILE * 128
    for ch in range(NTILE):
        act(st, gam[ch], crow[ch], AF.Exp)
        cr1 = TH(crw[ch].ap[0:1, 0:128], crw[ch].buf)
        p0 = pbs(st, 0, 1)
        st.S.op("tensor", lambda e, o=p0.ap, l=ones.ap[0:1, 0:128],
                r=cr1.ap[0:1, 127:128]:
                e.matmul(o, l, r, start=True, stop=True),
                reads=[ones.buf, crw[ch].buf], writes=[p0.buf])
        cl = f.alloc(1, "cl")
        copy(st, cl, p0)
        dar = f.alloc(1, "dar")
        tt(st, dar, cl, crow[ch], ALU.subtract)
        act(st, Dcol[ch], dar, AF.Exp)
        act(st, gamL[ch], cl, AF.Exp)
        f.pos -= 2
    kTn = a.alloc(128, "kTn")
    Sm = a.alloc(128, "Sm"); SmT = a.alloc(128, "SmT")
    accm = a.alloc(128, "accm"); accT = a.alloc(128, "accT")
    Mt = a.alloc(128, "Mt")
    M2t = a.alloc(128, "M2t")
    WtS = a.alloc(128, "WtS")
    Us = a.alloc(128, "Us")
    gI = f.alloc(128, "gI")
    ngl = f.alloc(1, "ngl")
    for ch in range(NTILE):
        tpose(st, pbts(st, 0, 128), k_n[ch])
        copy(st, kTn, pbts(st, 0, 128))
        p0 = pbs(st, 0, 128)
        mm(st, p0, kTn, kTn, True, True)
        stt(st, Sm, p0, TH(beta.ap[:, ch:ch + 1], beta.buf), ALU.mult,
            ALU.mult, stril)
        ts(st, Sm, Sm, np.float32(-1.0), ALU.mult)
        tt(st, accm, Sm, st.sb["ident_bf"], ALU.add)
        for it in range(6):
            tpose(st, pbts(st, 0, 128), Sm)
            copy(st, SmT, pbts(st, 0, 128))
            p1 = pbs(st, 1, 128)
            mm(st, p1, SmT, Sm, True, True)
            copy(st, Sm, p1)
            tpose(st, pbts(st, 0, 128), accm)
            copy(st, accT, pbts(st, 0, 128))
            p1 = pbs(st, 1, 128)
            mm(st, p1, accT, Sm, True, True)
            tt(st, accm, accm, p1, ALU.add)
        tpose(st, pbts(st, 0, 128), accm)
        copy(st, Mt, pbts(st, 0, 128))
        p1 = pbs(st, 1, 128)
        mm(st, p1, bK[ch], Mt, True, True)
        copy(st, WtT[ch], p1)
        tt(st, M2t, Mt, GUt[ch], ALU.mult)
        p2 = pbs(st, 2, 128)
        mm(st, p2, bV[ch], M2t, True, True)
        copy(st, UT[ch], p2)
        tpose(st, pbts(st, 0, 128), UT[ch])
        copy(st, Us, pbts(st, 0, 128))
        ts(st, DU[ch], Us, Dcol[ch], ALU.mult)
        p3 = pbs(st, 3, 128)
        mm(st, p3, k_n[ch], DU[ch], True, True)
        copy(st, Rc[ch], p3)
        tpose(st, pbts(st, 0, 128), WtT[ch])
        copy(st, WtS, pbts(st, 0, 128))
        p4 = pbs(st, 4, 128)
        mm(st, p4, k_n[ch], WtS, True, True)
        ts(st, gI, st.sb["ident_bf"], gamL[ch], ALU.mult)
        ts(st, ngl, gamL[ch], np.float32(-1.0), ALU.mult)
        stt(st, Ph[ch], p4, ngl, ALU.mult, ALU.add, gI)
    PhC = [f.alloc(128, f"PhC{b}") for b in range(2)]
    RcC = [f.alloc(128, f"RcC{b}") for b in range(2)]
    tmpb = a.alloc(128, "tmpb")
    tmpb2 = a.alloc(128, "tmpb2")
    for b in range(2):
        chs_ = range(8) if b == 0 else range(8, 16)
        for j, ch in enumerate(chs_):
            if j == 0:
                copy(st, PhC[b], Ph[ch])
                copy(st, RcC[b], Rc[ch])
                continue
            tpose(st, pbts(st, 0, 128), Ph[ch])
            copy(st, tmpb, pbts(st, 0, 128))
            copy(st, tmpb2, PhC[b])
            p0 = pbs(st, 0, 128)
            mm(st, p0, tmpb, tmpb2, True, True)
            copy(st, PhC[b], p0)
            copy(st, tmpb2, RcC[b])
            p1 = pbs(st, 1, 128)
            mm(st, p1, tmpb, tmpb2, True, True)
            stt(st, RcC[b], p1, np.float32(1.0), ALU.mult, ALU.add, Rc[ch])
    for b in range(2):
        st_dram(st, "gpsimd", st.st_in[2 * b + 0, :, :], PhC[b],
                st.db["st_in"])
        st_dram(st, "gpsimd", st.st_in[2 * b + 1, :, :], RcC[b],
                st.db["st_in"])
    S.op("gpsimd", lambda e: e.collective_compute(
        "AllGather", ALU.bypass, replica_groups=[list(range(NCORE))],
        ins=[st.st_in[:]], outs=[st.st_out[:]]),
        reads=[st.db["st_in"]], writes=[st.db["st_out"]], always_inc=True)
    gst = a.alloc(NCORE * 4 * 128, "gst")
    gstage = f.alloc(512, "gstage")
    for r_ in range(NCORE):
        S.op("sync", lambda e, d=gstage.ap.rearrange(
            "p (k q) -> p k q", q=128), r2=r_: e.dma_start(
            out=d, in_=bass.AP(st.st_out, r2 * 65536,
                               [[128, 128], [16384, 4], [1, 128]])),
            reads=[st.db["st_out"]], writes=[gstage.buf], dma=True)
        copy(st, TH(gst.ap[:, r_ * 512:(r_ + 1) * 512], gst.buf), gstage)
    f.pos -= 512

    def gPh(r, b):
        return TH(gst.ap[:, (r * 4 + 2 * b) * 128:(r * 4 + 2 * b + 1) * 128],
                  gst.buf)

    def gRc(r, b):
        return TH(gst.ap[:, (r * 4 + 2 * b + 1) * 128:
                         (r * 4 + 2 * b + 2) * 128], gst.buf)
    Astate = [f.alloc(128, f"Ast{b}") for b in range(2)]
    Tk = f.alloc(8 * 128, "Tk")
    sel = st.sb["sel"]
    for b in range(2):
        order = list(range(8)) if b == 0 else list(range(7, -1, -1))
        cur = TH(Tk.ap[:, 0:128], Tk.buf)
        memset(st, cur, 0.0)
        for posi in range(1, 8):
            prev_rank = order[posi - 1]
            prv = TH(Tk.ap[:, (posi - 1) * 128:posi * 128], Tk.buf)
            nxt = TH(Tk.ap[:, posi * 128:(posi + 1) * 128], Tk.buf)
            tpose(st, pbts(st, 0, 128), gPh(prev_rank, b))
            copy(st, tmpb, pbts(st, 0, 128))
            copy(st, tmpb2, prv)
            p0 = pbs(st, 0, 128)
            mm(st, p0, tmpb, tmpb2, True, True)
            stt(st, nxt, p0, np.float32(1.0), ALU.mult, ALU.add,
                gRc(prev_rank, b))
        firstk = True
        for kk in range(8):
            posi = kk if b == 0 else 7 - kk
            tkk = TH(Tk.ap[:, posi * 128:(posi + 1) * 128], Tk.buf)
            if firstk:
                ts(st, Astate[b], tkk, TH(sel.ap[:, kk:kk + 1], sel.buf),
                   ALU.mult)
                firstk = False
            else:
                stt(st, Astate[b], tkk, TH(sel.ap[:, kk:kk + 1], sel.buf),
                    ALU.mult, ALU.add, Astate[b])
    At = a.alloc(128, "At")
    Ab = a.alloc(128, "Ab")
    ZT = a.alloc(128, "ZT")
    Y1n = a.alloc(128, "Y1n")
    Y2w = a.alloc(128, "Y2w")
    Zn = a.alloc(128, "Zn")
    zg = a.alloc(128, "zg")
    og = a.alloc(128, "og")
    ogT = a.alloc(128, "ogT")
    WAb = a.alloc(128, "WAb")
    oA = f.alloc(128, "oA")
    ou = f.alloc(128, "ou")
    for ch in range(NTILE):
        b = 0 if ch < 8 else 1
        A = Astate[b]
        copy(st, Ab, A)
        tpose(st, pbts(st, 0, 128), Ab)
        copy(st, At, pbts(st, 0, 128))
        p0 = pbs(st, 0, 128)
        mm(st, p0, At, qT[ch], True, True)
        copy(st, ZT, p0)
        p1 = pbs(st, 1, 128)
        mm(st, p1, WtT[ch], ZT, True, True)
        stt(st, Y1n, p1, np.float32(1.0), ALU.mult, ALU.mult, neg_utri)
        p2 = pbs(st, 2, 128)
        mm(st, p2, UT[ch], qT[ch], True, True)
        stt(st, Y2w, p2, np.float32(1.0), ALU.mult, ALU.mult, GUt[ch])
        tpose(st, pbts(st, 0, 128), ZT)
        copy(st, Zn, pbts(st, 0, 128))
        ts(st, zg, Zn, gam[ch], ALU.mult)
        p3 = pbs(st, 3, 128)
        mm(st, p3, Y1n, k_n[ch], True, True)
        stt(st, oA, p3, gam[ch], ALU.mult, ALU.add, zg)
        p4 = pbs(st, 4, 128)
        mm(st, p4, Y2w, k_n[ch], True, True)
        stt(st, ou, p4, np.float32(1.0), ALU.mult, ALU.add, oA)
        tt(st, og, ou, gate[ch], ALU.mult)
        tpose(st, pbts(st, 0, 128), og)
        copy(st, ogT, pbts(st, 0, 128))
        for dc in range(2):
            pbk = pbs(st, dc, 512)
            mm(st, pbk, ogT, TH(woT.ap[:, dc * 512:(dc + 1) * 512], woT.buf),
               True, False)
            bias_mm(st, pbk, st.sb["l0a_bo"], 512, row_off=dc * 512)
            xd = st.xt[ch][:, dc * 512:(dc + 1) * 512]
            tt(st, xd, pbk, xd, ALU.add)
        p5 = pbs(st, 5, 128)
        mm(st, p5, WtT[ch], Ab, True, True)
        copy(st, WAb, p5)
        p5b = pbs(st, 5, 128)
        mm(st, p5b, k_n[ch], WAb, True, True)
        stt(st, oA, p5b, np.float32(-1.0), ALU.mult, ALU.add, A)
        stt(st, A, oA, gamL[ch], ALU.mult, ALU.add, Rc[ch])
    st.asm.pop(); a.pop(); f.pop()


def bias_mm_off(st, p_th, row_th, n, row_off):
    bias_mm(st, p_th, row_th, n, row_off=row_off)



def ffn1_phase(st, first):
    S, a, f = st.S, st.abf, st.af32
    a.push(); f.push(); st.asm.push()
    di = st.di
    PB, PBT = st.PB, st.PBT
    if first:
        quant_weight(st, di["l1f_wgT"], st.wqg, st.db["wqg"], D, H,
                     st.wsc[:, 4:5], st.wsc[:, 5:6])
        quant_weight(st, di["l1f_wuT"], st.wqu, st.db["wqu"], D, H,
                     st.wsc[:, 6:7], st.wsc[:, 7:8])
        quant_weight(st, di["l1f_wdT"], st.wqd, st.db["wqd"], H, D,
                     st.wsc[:, 8:9], st.wsc[:, 9:10])
    swgr = TH(st.wsc.ap[:, 5:6], st.wsc.buf)
    swur = TH(st.wsc.ap[:, 7:8], st.wsc.buf)
    swdr = TH(st.wsc.ap[:, 9:10], st.wsc.buf)
    ffn_common_start(st)
    zb = [a.alloc(D, f"z{i}") for i in range(2)]
    hsup = a.alloc(4 * H, "hsup")
    xqfm = a.alloc(8 * 512, "xqfm")
    ghqT = a.alloc(8 * 512, "ghqT")
    wbufg = [a.alloc(8 * 512, f"wbg{i}") for i in range(2)]
    wbufu = [a.alloc(8 * 512, f"wbu{i}") for i in range(2)]
    sgb = a.alloc(512, "sgb")
    tscr = f.alloc(H, "tscr")
    uscr = f.alloc(512, "uscr")
    wqgv = st.wqg.rearrange("(dt p) h -> p dt h", p=128)
    wquv = st.wqu.rearrange("(dt p) h -> p dt h", p=128)
    wqdv = st.wqd.rearrange("(ht p) dd -> p ht dd", p=128)

    for sbk in range(4):
        st.asm.push()
        tl = [sbk * 4 + i for i in range(4)]
        sxi = st.asm.alloc(4, "sxi")
        scg = st.asm.alloc(4, "scg")
        scu = st.asm.alloc(4, "scu")
        for i, tt_ in enumerate(tl):
            z = zb[i % 2]
            ts(st, z, st.xt[tt_], st.mean[:, tt_:tt_ + 1], ALU.subtract,
               st.rstd[:, tt_:tt_ + 1], ALU.mult)
            quant_chain(st, z, D, z, sxi[:, i:i + 1], tscr)
            for dt in range(8):
                tpose(st, pbts(st, 0, 128, dt * 128),
                      z[:, dt * 128:(dt + 1) * 128])
            dst = TH(xqfm.ap.rearrange("p (dt q) -> p dt q", q=512)
                     [:, :, i * 128:(i + 1) * 128], xqfm.buf)
            copy(st, dst, TH(PBT[0].ap.rearrange("p (dt q) -> p dt q", q=128),
                             PBT[0].buf))
            tt(st, TH(scg.ap[:, i:i + 1], scg.buf),
               TH(sxi.ap[:, i:i + 1], sxi.buf), swgr, ALU.mult)
            tt(st, TH(scu.ap[:, i:i + 1], scu.buf),
               TH(sxi.ap[:, i:i + 1], sxi.buf), swur, ALU.mult)
        for hc in range(8):
            wg = wbufg[hc % 2]
            wu = wbufu[hc % 2]
            ld_dram(st, "sync", wg, wqgv[:, :, hc * 512:(hc + 1) * 512])
            ld_dram(st, "sync", wu, wquv[:, :, hc * 512:(hc + 1) * 512])
            for i in range(4):
                pg = pbs(st, i % 2, 512)
                pu = pbs(st, 2 + i % 2, 512)
                for dt in range(8):
                    mm(st, pg,
                       TH(xqfm.ap[:, dt * 512 + i * 128:
                                  dt * 512 + (i + 1) * 128], xqfm.buf),
                       TH(wg.ap[:, dt * 512:(dt + 1) * 512], wg.buf),
                       start=(dt == 0), stop=(dt == 7))
                act(st, sgb, pg, AF.Silu,
                    scale=TH(scg.ap[:, i:i + 1], scg.buf))
                for dt in range(8):
                    mm(st, pu,
                       TH(xqfm.ap[:, dt * 512 + i * 128:
                                  dt * 512 + (i + 1) * 128], xqfm.buf),
                       TH(wu.ap[:, dt * 512:(dt + 1) * 512], wu.buf),
                       start=(dt == 0), stop=(dt == 7))
                ts(st, uscr, pu, TH(scu.ap[:, i:i + 1], scu.buf), ALU.mult)
                hdst = TH(hsup.ap[:, i * H + hc * 512:i * H + (hc + 1) * 512],
                          hsup.buf)
                tt(st, hdst, sgb, uscr, ALU.mult)
        shi = st.asm.alloc(4, "shi")
        sc2 = st.asm.alloc(4, "sc2")
        for i in range(4):
            hv = TH(hsup.ap[:, i * H:(i + 1) * H], hsup.buf)
            quant_chain(st, hv, H, hv, shi[:, i:i + 1], tscr)
            tt(st, TH(sc2.ap[:, i:i + 1], sc2.buf),
               TH(shi.ap[:, i:i + 1], shi.buf), swdr, ALU.mult)
        for dc in range(2):
            wb2 = wbufg[dc % 2]
            for htg in range(4):
                ld_dram(st, "sync", wb2,
                        wqdv[:, htg * 8:(htg + 1) * 8,
                             dc * 512:(dc + 1) * 512])
                for i in range(4):
                    for hh in range(8):
                        ht = htg * 8 + hh
                        tpose(st, pbts(st, 1, 128, hh * 128),
                              TH(hsup.ap[:, i * H + ht * 128:
                                         i * H + (ht + 1) * 128], hsup.buf))
                    dst = TH(ghqT.ap.rearrange("p (g q) -> p g q", q=512)
                             [:, :, i * 128:(i + 1) * 128], ghqT.buf)
                    copy(st, dst, TH(
                        PBT[1].ap.rearrange("p (g q) -> p g q", q=128),
                        PBT[1].buf))
                for i in range(4):
                    pbk = pbs(st, 2 + i, 512)
                    for hh in range(8):
                        mm(st, pbk,
                           TH(ghqT.ap[:, hh * 512 + i * 128:
                                      hh * 512 + (i + 1) * 128], ghqT.buf),
                           TH(wb2.ap[:, hh * 512:(hh + 1) * 512], wb2.buf),
                           start=(htg == 0 and hh == 0),
                           stop=(htg == 3 and hh == 7))
            for i, tt_ in enumerate(tl):
                pbk = pbs(st, 2 + i, 512)
                ts(st, uscr, pbk, TH(sc2.ap[:, i:i + 1], sc2.buf), ALU.mult)
                xd = st.xt[tt_][:, dc * 512:(dc + 1) * 512]
                stt(st, xd, uscr, np.float32(0.5), ALU.mult, ALU.add, xd)
        st.asm.pop()
    st.asm.pop(); a.pop(); f.pop()



_RREG = {}


def _reg_dma(e, st, sbuf_ap, dram, tbl, idx, pattern, extra=0, out_mode=False):
    assert extra == 0
    key = id(e)
    if key not in _RREG:
        _RREG[key] = e.register("roff_shared").__enter__()
    rg = _RREG[key]
    e.reg_load(rg, st.sb[tbl].ap[0:1, idx:idx + 1])
    if out_mode:
        return e.dma_start(out=bass.AP(dram, rg, pattern), in_=sbuf_ap)
    return e.dma_start(out=sbuf_ap, in_=bass.AP(dram, rg, pattern))


def fetch_q(st, r, dst_th):
    st.S.op("sync", lambda e, d=dst_th.ap, r_=r:
            _reg_dma(e, st, d, st.q_out, "offq", r_, [[128, 128], [1, 128]]),
            reads=[st.db["q_out"], st.sb["offq"].buf], writes=[dst_th.buf],
            dma=True, dsem="qf")


def att1_phase(st):
    S, a, f = st.S, st.abf, st.af32
    a.push(); f.push(); st.asm.push()
    di = st.di
    PB, PBT = st.PB, st.PBT
    ffn_common_start(st)
    a.push(); f.push()
    bgln = a.alloc(D, "bgln")
    bcast_dram_row(st, "l1a_gln", D, bgln)
    xfm = a.alloc(8 * TPC, "xfm")
    zscr = f.alloc(D, "zscr")
    zb2 = a.alloc(D, "zb2")
    rms = st.asm.alloc(16, "rms")
    for tt_ in range(NTILE):
        ts(st, zscr, st.xt[tt_], st.mean[:, tt_:tt_ + 1], ALU.subtract,
           st.rstd[:, tt_:tt_ + 1], ALU.mult)
        sszc = st.asm.alloc(1, "sszc")
        act(st, TH(st.junk.ap[:, 0:D], st.junk.buf), zscr, AF.Square,
            accum=sszc)
        batch_rstd(st, sszc, 1, 1e-6, TH(rms.ap[:, tt_:tt_ + 1], rms.buf),
                   scale_pre=1.0 / D)
        st.asm.pos -= 1
        stt(st, zb2, zscr, TH(rms.ap[:, tt_:tt_ + 1], rms.buf), ALU.mult,
            ALU.mult, bgln)
        for dt in range(8):
            tpose(st, pbts(st, 0, 128, dt * 128),
                  zb2[:, dt * 128:(dt + 1) * 128])
        dst = TH(xfm.ap.rearrange("p (dt q) -> p dt q", q=TPC)
                 [:, :, tt_ * 128:(tt_ + 1) * 128], xfm.buf)
        copy(st, dst, TH(PBT[0].ap.rearrange("p (dt q) -> p dt q", q=128),
                         PBT[0].buf))
    st_dram(st, "gpsimd", st.xfm_sp[:, :], xfm, st.db["xfm_sp"])
    wq_b = a.alloc(8 * C, "wq_b")
    wk_b = a.alloc(8 * C, "wk_b")
    wr_b = a.alloc(8 * C, "wr_b")
    for nm, dstw in (("l1a_Wq", wq_b), ("l1a_Wk", wk_b), ("l1a_Wr", wr_b)):
        wf = f.alloc(8 * C, "wf_")
        v = di[nm].rearrange("(dt p) w -> p dt w", p=128)
        ld_dram(st, "sync",
                TH(wf.ap.rearrange("p (dt w) -> p dt w", w=C), wf.buf),
                v[:, :, :])
        copy(st, dstw, wf)
        f.pos -= 8 * C
    wvr = a.alloc(8 * V1, "wvr")
    wv_f = [f.alloc(V1, f"wv_f{j}") for j in range(2)]
    wvv = di["l1a_Wv"].rearrange("(dt p) w -> p dt w", p=128)
    ld_dram(st, "sync", wv_f[0], wvv[:, 0, :])
    for dt in range(8):
        if dt + 1 < 8:
            ld_dram(st, "sync", wv_f[(dt + 1) % 2], wvv[:, dt + 1, :])
        copy(st, TH(wvr.ap[:, dt * V1:(dt + 1) * V1], wvr.buf),
             wv_f[dt % 2])
    f.pos -= 2 * V1
    qTt = a.alloc(128, "qTt")
    kTt = a.alloc(128, "kTt")
    vtile = a.alloc(V1, "vtile")
    for tt_ in range(NTILE):
        xsl = TH(xfm.ap.rearrange("p (dt q) -> p dt q", q=TPC)
                 [:, :, tt_ * 128:(tt_ + 1) * 128], xfm.buf)
        p0 = pbs(st, 0, 128)
        for dt in range(8):
            mm(st, p0, TH(wq_b.ap[:, dt * C:(dt + 1) * C], wq_b.buf),
               TH(xsl.ap[:, dt, :], xsl.buf), start=(dt == 0), stop=(dt == 7))
        ts(st, qTt, p0, INV_TAU, ALU.mult)
        st_dram(st, "gpsimd", st.q_in[tt_, :, :], qTt, st.db["q_in"])
        p1 = pbs(st, 1, 128)
        for dt in range(8):
            mm(st, p1, TH(wk_b.ap[:, dt * C:(dt + 1) * C], wk_b.buf),
               TH(xsl.ap[:, dt, :], xsl.buf), start=(dt == 0), stop=(dt == 7))
        copy(st, kTt, p1)
        b = 0 if tt_ < 8 else 1
        col = (tt_ % 8) * 128
        S.op("gpsimd", lambda e, s=kTt.ap,
             o=bass.AP(st.kv_in, b * 131072 + col, [[1024, 128], [1, 128]]):
             e.dma_start(out=o, in_=s),
             reads=[kTt.buf], writes=[st.db["kv_in"]], dma=True)
        for vc in range(4):
            pbk = pbs(st, 2 + (vc % 2), 512)
            for dt in range(8):
                mm(st, pbk, TH(xsl.ap[:, dt, :], xsl.buf),
                   TH(wvr.ap[:, dt * V1 + vc * 512:dt * V1 + (vc + 1) * 512],
                      wvr.buf),
                   start=(dt == 0), stop=(dt == 7))
            copy(st, TH(vtile.ap[:, vc * 512:(vc + 1) * 512], vtile.buf), pbk)
        S.op("gpsimd", lambda e, s=vtile.ap,
             o=bass.AP(st.kv_in, KV_VOFF + b * (1024 * V1) + col * V1,
                       [[V1, 128], [1, V1]]):
             e.dma_start(out=o, in_=s),
             reads=[vtile.buf], writes=[st.db["kv_in"]], dma=True)
    lvl0 = os.environ.get("MK_A1", "full")
    if lvl0 not in ("a", "ac2"):
        S.op("gpsimd", lambda e: e.collective_compute(
            "AllGather", ALU.bypass, replica_groups=[list(range(NCORE))],
            ins=[st.q_in[:]], outs=[st.q_out[:]]),
            reads=[st.db["q_in"]], writes=[st.db["q_out"]],
            always_inc=True)
    if lvl0 not in ("a", "ac1"):
        S.op("gpsimd", lambda e: e.collective_compute(
            "AllGather", ALU.bypass, replica_groups=[list(range(NCORE))],
            ins=[st.kv_in[:]], outs=[st.kv_out[:]]),
            reads=[st.db["kv_in"]], writes=[st.db["kv_out"]],
            always_inc=True)
    if os.environ.get("MK_NOXLR", "0") == "1":
        S.barrier()
        a.pop(); f.pop()
        lvl = os.environ.get("MK_A1", "full")
        if lvl in ("acf", "full"):
            flash(st)
            S.barrier()
        if lvl == "full":
            att1_return(st)
        st.asm.pop(); a.pop(); f.pop()
        st.S.barrier()
        return
    embv = di["embT"].rearrange("(dt p) m -> p dt m", p=128)
    emb_b = [a.alloc(512, f"embb{i}") for i in range(3)]
    xrt_s = [a.alloc(512, f"xrt_s{i}") for i in range(2)]
    ld_dram(st, "sync", emb_b[0], embv[:, 0, 0:512])
    for mc in range(XLCOLS // 512):
        p4 = pbs(st, 4, 512)
        for dt in range(8):
            i = mc * 8 + dt
            if i + 1 < 8 * (XLCOLS // 512):
                mc2, dt2 = divmod(i + 1, 8)
                ld_dram(st, "sync", emb_b[(i + 1) % 3],
                        embv[:, dt2, mc2 * 512:(mc2 + 1) * 512])
            mm(st, p4, TH(wr_b.ap[:, dt * C:(dt + 1) * C], wr_b.buf),
               emb_b[i % 3], start=(dt == 0), stop=(dt == 7))
        xs = xrt_s[mc % 2]
        copy(st, xs, p4)
        S.op("gpsimd", lambda e, s=xs.ap,
             o=bass.AP(st.xlrT, mc * 512, [[XLCOLS, 128], [1, 512]]):
             e.dma_start(out=o, in_=s),
             reads=[xs.buf], writes=[st.db["xlrT"]], dma=True)
    S.barrier()
    a.pop(); f.pop()
    lvl = os.environ.get("MK_A1", "full")
    if lvl in ("acf", "full"):
        flash(st)
        S.barrier()
    if lvl == "full":
        att1_return(st)
    st.asm.pop(); a.pop(); f.pop()


def flash(st):
    S, a, f = st.S, st.abf, st.af32
    a.push(); f.push()
    PB, PBT = st.PB, st.PBT
    qTu = a.alloc(128, "qTu")
    qTv = a.alloc(128, "qTv")
    qTr = [a.alloc(128, f"qTr{i}") for i in range(2)]
    xlrsl = a.alloc(STRIP_MAX, "xlrsl")
    PIECE = int(os.environ.get("MK_PIECE", "5"))
    shh = [a.alloc(512 * PIECE, f"striph{i}") for i in range(2)]
    sarena = a.alloc(16 * 512, "sarena")
    NVB = int(os.environ.get("MK_NVB", "4"))
    Vb = [a.alloc(2 * V1, f"Vb{i}") for i in range(NVB)]
    kTb = [a.alloc(512, f"kTb{i}") for i in range(2)]
    if os.environ.get("MK_ONEPCH", "0") == "1":
        pch = a.alloc(512, "Pch")
        scb = [pch, pch]
        Pe = [pch, pch]
    else:
        scb = [a.alloc(512, f"scb{i}") for i in range(2)]
        Pe = [a.alloc(512, f"Pe{i}") for i in range(2)]
    PTb = [a.alloc(512, f"PTb{i}") for i in range(2)]
    umask = st.af32.alloc(512, "umask")
    thet = st.asm.alloc(1, "thet")
    wvout = a.alloc(V1, "wvout")
    runm = st.asm.alloc(1, "runm")
    negm = st.asm.alloc(1, "negm")
    cmax = st.asm.alloc(1, "cmax")
    lsum = st.asm.alloc(1, "lsum")
    lpart = st.asm.alloc(1, "lpart")
    rl = st.asm.alloc(1, "rl")
    sdb = [st.db["strip0"], st.db["strip1"]]

    def build_strip(r):
        rp = r % 8
        pitch = STRIP_PITCH[rp]
        slot = r % 2
        nmc = pitch // 512
        q = qTr[r % 2]
        fetch_q(st, r, q)
        ts(st, qTv, q, st.sb["l1a_xlv"], ALU.add)
        S.op("gpsimd", lambda e, d=xlrsl.ap[:, 0:pitch], r_=r:
             _reg_dma(e, st, d, st.xlrT, "offxlr", r_,
                      [[XLCOLS, 128], [1, pitch]]),
             reads=[st.db["xlrT"], st.sb["offxlr"].buf],
             writes=[xlrsl.buf], dma=True, dsem="xl")
        for pc0 in range(0, nmc, PIECE):
            pc1 = min(nmc, pc0 + PIECE)
            sbh = shh[(pc0 // PIECE) % 2]
            for mc in range(pc0, pc1):
                p5 = pbs(st, 5, 512)
                mm(st, p5, qTv,
                   TH(xlrsl.ap[:, mc * 512:(mc + 1) * 512], xlrsl.buf),
                   True, True)
                copy(st, TH(sbh.ap[:, (mc - pc0) * 512:(mc - pc0 + 1) * 512],
                            sbh.buf), p5)
            S.op("gpsimd", lambda e, s=sbh.ap[:, 0:(pc1 - pc0) * 512],
                 o=bass.AP(st.stripd, slot * 128 * STRIP_MAX + pc0 * 512,
                           [[pitch, 128], [1, (pc1 - pc0) * 512]]):
                 e.dma_start(out=o, in_=s),
                 reads=[sbh.buf], writes=[sdb[slot]], dma=True, dsem="sw")

    def v_load(r, kh):
        b = 0 if r < 8 else 1
        g0 = kh * 256
        rank = (g0 // 1024) if b == 0 else (7 - g0 // 1024)
        vin = (rank * KVL + KV_VOFF + b * (1024 * V1) + (g0 % 1024) * V1)
        vb = Vb[kh % NVB]
        S.op("gpsimd",
             lambda e, d=vb.ap.rearrange("p (s w) -> p s w", w=V1),
             o=vin:
             e.dma_start(out=d, in_=bass.AP(
                 st.kv_out, o, [[V1, 128], [128 * V1, 2], [1, V1]])),
             reads=[st.db["kv_out"]], writes=[vb.buf], dma=True)

    def k_load(r, kc):
        b = 0 if r < 8 else 1
        g0 = kc * 512
        rank = (g0 // 1024) if b == 0 else (7 - g0 // 1024)
        kin = rank * KVL + b * 131072 + (g0 % 1024)
        kb = kTb[kc % 2]
        S.op(os.environ.get("MK_KTQ", "sync"), lambda e, d=kb.ap, o=kin:
             e.dma_start(out=d, in_=bass.AP(
                 st.kv_out, o, [[1024, 128], [1, 512]])),
             reads=[st.db["kv_out"]], writes=[kb.buf], dma=True, dsem="kt")

    def strip_read(slot, pitch, kc):
        sc = scb[kc % 2]
        S.op("scalar", lambda e, d=sc.ap, o=(
            slot * 128 * STRIP_MAX + kc * 512 + 127), pp=pitch:
            e.dma_start(out=d, in_=bass.AP(
                st.stripd, o, [[pp - 1, 128], [1, 512]])),
            reads=[sdb[slot]], writes=[sc.buf], dma=True, dsem="sr")

    build_strip(0)
    for r in range(16):
        rp = r % 8
        b = 0 if r < 8 else 1
        pitch = STRIP_PITCH[rp]
        slot = r % 2
        W512 = 2 * rp + 2
        KH = 2 * W512
        VDEPTH = int(os.environ.get("MK_VDEPTH", "4"))
        KTPRE = os.environ.get("MK_KTPRE", "1") == "1"
        for kh in range(min(VDEPTH, KH)):
            v_load(r, kh)
        ts(st, qTu, qTr[r % 2], st.sb["l1a_xlu"], ALU.add)
        memset(st, runm, -3.0e38)
        SCBPRE = os.environ.get("MK_SCBPRE", "1") == "1"
        if KTPRE:
            k_load(r, 0)
        if SCBPRE:
            strip_read(slot, pitch, 0)
        for kc in range(W512):
            if KTPRE:
                if kc + 1 < W512:
                    k_load(r, kc + 1)
            else:
                k_load(r, kc)
            if SCBPRE and kc + 1 < W512:
                strip_read(slot, pitch, kc + 1)
            if not SCBPRE:
                strip_read(slot, pitch, kc)
            p4 = pbs(st, 4, 512)
            mm(st, p4, qTu, kTb[kc % 2], True, True)
            sdst = TH(sarena.ap[:, kc * 512:(kc + 1) * 512], sarena.buf)
            stt(st, sdst, p4, np.float32(1.0), ALU.mult, ALU.add, scb[kc % 2])
            if kc >= W512 - 2:
                ts(st, thet, TH(st.sb["gi"].ap[:, r:r + 1], st.sb["gi"].buf),
                   float(-kc * 512), ALU.add)
                ts(st, umask, st.sb["iota512"], thet, ALU.subtract,
                   np.float32(0.0), ALU.max)
                stt(st, sdst, umask, NEG30, ALU.mult, ALU.add, sdst)
            red(st, cmax, sdst, ALU.max)
            tt(st, runm, runm, cmax, ALU.max)
        if r < 15:
            build_strip(r + 1)
        ts(st, negm, runm, np.float32(-1.0), ALU.mult)
        for kh in range(KH):
            vb = Vb[kh % NVB]
            if VDEPTH == 0:
                v_load(r, kh)
            PEPAR = 1 if os.environ.get("MK_PEPAR", "1") == "1" else 2
            if kh % 2 == 0:
                kc = kh // 2
                pe = Pe[(kc % 2) // PEPAR]
                ssrc = TH(sarena.ap[:, kc * 512:(kc + 1) * 512], sarena.buf)
                act(st, pe, ssrc, AF.Exp, bias=negm, accum=lpart)
                if kc == 0:
                    copy(st, lsum, lpart)
                else:
                    tt(st, lsum, lsum, lpart, ALU.add)
                for kt in range(4):
                    tpose(st, pbts(st, (kc % 2) // PEPAR, 128, kt * 128),
                          TH(pe.ap[:, kt * 128:(kt + 1) * 128], pe.buf))
                pp_ = (kc % 2) // PEPAR
                copy(st, PTb[pp_],
                     TH(PBT[pp_].ap[:, 0:512], PBT[pp_].buf))
            ptb = PTb[((kh // 2) % 2) // PEPAR]
            for kt2 in range(2):
                kt = (kh % 2) * 2 + kt2
                lt = TH(ptb.ap[:, kt * 128:(kt + 1) * 128], ptb.buf)
                for vc in range(4):
                    mm(st, pbs(st, vc, 512), lt,
                       TH(vb.ap[:, kt2 * V1 + vc * 512:
                                kt2 * V1 + (vc + 1) * 512], vb.buf),
                       start=(kh == 0 and kt2 == 0),
                       stop=(kh == KH - 1 and kt2 == 1))
            if VDEPTH > 0 and kh + VDEPTH < KH:
                v_load(r, kh + VDEPTH)
        recip(st, rl, lsum)
        for vc in range(4):
            ts(st, TH(wvout.ap[:, vc * 512:(vc + 1) * 512], wvout.buf),
               pbs(st, vc, 512), rl, ALU.mult)
        st_dram(st, "gpsimd", st.wv_in[r, :, :], wvout, st.db["wv_in"],
                dsem="wv")
    S.op("gpsimd", lambda e: e.collective_compute(
        "AllGather", ALU.bypass, replica_groups=[list(range(NCORE))],
        ins=[st.wv_in[:]], outs=[st.wv_out[:]]),
        reads=[st.db["wv_in"]], writes=[st.db["wv_out"]], always_inc=True)
    st.asm.pos -= 7
    st.af32.pos -= 512
    a.pop(); f.pop()


def att1_return(st):
    S, a, f = st.S, st.abf, st.af32
    di = st.di
    PB, PBT = st.PB, st.PBT
    a.push(); f.push()
    xfm = a.alloc(8 * TPC, "xfm_r")
    ld_dram(st, "sync", xfm, st.xfm_sp[:, :], extra_reads=[st.db["xfm_sp"]])
    wgr = a.alloc(8 * V1, "wgr")
    wf = f.alloc(V1, "wgf")
    wgv = di["l1a_Wg"].rearrange("(dt p) w -> p dt w", p=128)
    for dt in range(8):
        ld_dram(st, "sync", wf, wgv[:, dt, :])
        copy(st, TH(wgr.ap[:, dt * V1:(dt + 1) * V1], wgr.buf), wf)
    f.pos -= V1
    wvt = a.alloc(V1, "wvt")
    sgt = a.alloc(V1, "sgt")
    gT = a.alloc(1024, "gT")
    for tt_ in range(NTILE):
        S.op("scalar", lambda e, d=wvt.ap, t=tt_:
             _reg_dma(e, st, d, st.wv_out, "offwv", t, [[V1, 128], [1, V1]]),
             reads=[st.db["wv_out"], st.sb["offwv"].buf], writes=[wvt.buf],
             dma=True)
        xsl = TH(xfm.ap.rearrange("p (dt q) -> p dt q", q=TPC)
                 [:, :, tt_ * 128:(tt_ + 1) * 128], xfm.buf)
        for vc in range(4):
            pbk = pbs(st, 4 + (vc % 2), 512)
            for dt in range(8):
                mm(st, pbk, TH(xsl.ap[:, dt, :], xsl.buf),
                   TH(wgr.ap[:, dt * V1 + vc * 512:dt * V1 + (vc + 1) * 512],
                      wgr.buf),
                   start=(dt == 0), stop=(dt == 7))
            act(st, TH(sgt.ap[:, vc * 512:(vc + 1) * 512], sgt.buf), pbk,
                AF.Silu)
        tt(st, sgt, wvt, sgt, ALU.mult)
        for half in range(2):
            for vh in range(8):
                vt = half * 8 + vh
                tpose(st, pbts(st, 0, 128, vh * 128),
                      TH(sgt.ap[:, vt * 128:(vt + 1) * 128], sgt.buf))
            copy(st, gT, TH(PBT[0].ap[:, 0:1024], PBT[0].buf))
            st_dram(st, "gpsimd",
                    st.wv_in[tt_, :, half * 1024:(half + 1) * 1024], gT,
                    st.db["wv_in"])
    S.barrier()
    a.pop(); f.pop()
    a.push(); f.push()
    wor = a.alloc(16 * D, "wor")
    wf = [f.alloc(D, f"wof{j}") for j in range(2)]
    wov = di["l1a_Wo"].rearrange("(vt p) w -> p vt w", p=128)
    ld_dram(st, "sync", wf[0], wov[:, 0, :])
    for vt in range(16):
        if vt + 1 < 16:
            ld_dram(st, "sync", wf[(vt + 1) % 2], wov[:, vt + 1, :])
        copy(st, TH(wor.ap[:, vt * D:(vt + 1) * D], wor.buf), wf[vt % 2])
    f.pos -= 2 * D
    gtile = [a.alloc(V1, f"gtile{j}") for j in range(2)]
    ld_dram(st, "sync", gtile[0], st.wv_in[0, :, :],
            extra_reads=[st.db["wv_in"]])
    for tt_ in range(NTILE):
        if tt_ + 1 < NTILE:
            ld_dram(st, "sync", gtile[(tt_ + 1) % 2],
                    st.wv_in[tt_ + 1, :, :],
                    extra_reads=[st.db["wv_in"]])
        for dc in range(2):
            pbk = pbs(st, dc, 512)
            for vt in range(16):
                mm(st, pbk,
                   TH(gtile[tt_ % 2].ap[:, vt * 128:(vt + 1) * 128],
                      gtile[tt_ % 2].buf),
                   TH(wor.ap[:, vt * D + dc * 512:vt * D + (dc + 1) * 512],
                      wor.buf),
                   start=(vt == 0), stop=(vt == 15))
            xd = st.xt[tt_][:, dc * 512:(dc + 1) * 512]
            tt(st, xd, pbk, xd, ALU.add)
    a.pop(); f.pop()



def build_nc():
    nc = bass.Bass("TRN2", target_bir_lowering=False,
                   detect_race_conditions=False)
    st = declare_io(nc)
    S = Sched()
    st.S = S
    ctx = contextlib.ExitStack()
    with ctx:
        _build_body(nc, st, ctx)
        S.emit(nc, ctx)
    return nc


def _build_body(nc, st, ctx):
    di = st.di
    af32 = Arena(ctx.enter_context(
        nc.sbuf_tensor("af32", [128, 25856], F32)), 25856, "af32")
    abf = Arena(ctx.enter_context(
        nc.sbuf_tensor("abf", [128, 52480], BF16)), 52480, "abf")
    asm = Arena(ctx.enter_context(
        nc.sbuf_tensor("asm", [128, 512], F32)), 512, "asm")
    ai = Arena(ctx.enter_context(
        nc.sbuf_tensor("ai32", [1, 512], I32)), 512, "ai")
    st.af32, st.abf, st.asm, st.ai = af32, abf, asm, ai

    X = af32.alloc(NTILE * D, "X")
    st.X = X
    st.xt = [X[:, tt_ * D:(tt_ + 1) * D] for tt_ in range(NTILE)]

    pb = [ctx.enter_context(nc.psum_tensor(f"pb{i}", [128, 512], F32))
          for i in range(6)]
    pbt = [ctx.enter_context(nc.psum_tensor(f"pbt{i}", [128, 1024], BF16))
           for i in range(2)]
    st.PB = [TH(p[:, :], Buf(f"pb{i}")) for i, p in enumerate(pb)]
    st.PBT = [TH(p[:, :], Buf(f"pbt{i}")) for i, p in enumerate(pbt)]

    sb = {}
    sb["ident_bf"] = abf.alloc(128, "ident_bf")
    sb["utri"] = af32.alloc(128, "utri")
    sb["neg_utri"] = af32.alloc(128, "neg_utri")
    sb["stril"] = af32.alloc(128, "stril")
    sb["sel"] = af32.alloc(8, "sel")
    sb["ones_row"] = af32.row(512, "ones_row")
    for nm, wdt in [("l0a_bq", C), ("l0a_bk", C), ("l0a_bv", C),
                    ("l0a_bg", C), ("l0a_bab", 2), ("l0a_bo", D)]:
        sb[nm] = af32.row(wdt, nm)
    sb["l1a_xlu"] = af32.alloc(1, "xlu")
    sb["l1a_xlv"] = af32.alloc(1, "xlv")
    for nm, nn in [("offq", 16), ("offwv", 16), ("offxlr", 16),
                   ("offpad", 16)]:
        sb[nm] = TH(ai.t[0:1, ai.pos:ai.pos + nn], Buf(nm))
        ai.pos += nn
    sb["gi"] = af32.alloc(16, "gi")
    sb["iota512"] = af32.alloc(512, "iota512")
    st.sb = sb
    st.junk = abf.alloc(1024, "junk")
    st.rowstage = af32.row(512, "rowstage")
    st.wsc = af32.alloc(12, "wsc")
    st.magic_col = af32.alloc(1, "magic_col")
    memset(st, st.magic_col, MAGIC)

    S = st.S
    for name in ["ident_bf", "utri", "neg_utri", "stril", "sel",
                 "l1a_xlu", "l1a_xlv", "gi", "iota512"]:
        src = di[name]
        dst = sb[name]
        S.op("sync", lambda e, d=dst.ap, s=src[:]: e.dma_start(out=d, in_=s),
             writes=[dst.buf], dma=True)
    for name in ["ones_row", "l0a_bq", "l0a_bk", "l0a_bv", "l0a_bg",
                 "l0a_bab", "l0a_bo"]:
        src = di[name]
        dst = sb[name]
        S.op("sync", lambda e, d=dst.ap, s=src[0:1, :]:
             e.dma_start(out=d, in_=s), writes=[dst.buf], dma=True)
    for name in ["offq", "offwv", "offxlr", "offpad"]:
        src = di[name]
        dst = sb[name]
        S.op("sync", lambda e, d=dst.ap, s=src[0:1, :]:
             e.dma_start(out=d, in_=s), writes=[dst.buf], dma=True)

    xv = di["x_shard"].rearrange("(t p) d -> p t d", p=128)
    S.op("sync", lambda e: e.dma_start(
        out=af32.t[:, 0:NTILE * D].rearrange("p (t d) -> p t d", d=D),
        in_=xv[:, :, :]), writes=[X.buf], dma=True)

    S.barrier()
    if "ffn0a" in PHASES:
        ffn0_phase(st, first=True)
    if "att0" in PHASES:
        att0_phase(st)
    if "ffn0b" in PHASES:
        ffn0_phase(st, first=False)
    if "ffn1a" in PHASES:
        ffn1_phase(st, first=True)
    if "att1" in PHASES:
        att1_phase(st)
    if "ffn1b" in PHASES:
        ffn1_phase(st, first=False)

    yv = st.y.rearrange("(t p) d -> p t d", p=128)
    S.op("gpsimd", lambda e: e.dma_start(
        out=yv[:, :, :],
        in_=af32.t[:, 0:NTILE * D].rearrange("p (t d) -> p t d", d=D)),
        reads=[X.buf], writes=[st.db["y"]], dma=True)
    S.barrier()


def kernel(**inputs):
    in_maps = _shard_inputs(inputs)
    if "prog" not in _NC_CACHE:
        _NC_CACHE["prog"] = build_nc()
    nc = _NC_CACHE["prog"]
    res = run_bass_kernel_spmd(nc, in_maps, core_ids=list(range(NCORE)))
    return _unshard_output(res.results)


if __name__ == "__main__":
    nc = build_nc()
    print("build ok")
